# version 03
# baseline (speedup 1.0000x reference)
import math

import numpy as np
import ml_dtypes

import concourse.bass as bass
import concourse.mybir as mybir
from concourse.bass_utils import run_bass_kernel_spmd

F32 = mybir.dt.float32
BF16 = mybir.dt.bfloat16
AF = mybir.ActivationFunctionType
ALU = mybir.AluOpType
AX = mybir.AxisListType
BF = ml_dtypes.bfloat16

B, L, DM = 2, 16384, 1024
NCORES = 8
SL = 4096
CH = 256
SCH = 128
NFC = 8
NH, DH = 16, 64
OLEAD = 3584
XCOLS = OLEAD + SL
PREF = 8704
LEADC = OLEAD // CH
NXC = XCOLS // CH
W_LEAD = [12, 10, 8, 6, 4, 2, 0]
DEPTH = 6
DFF2 = 2730
DFF = 5460
NOG1 = 22
NKC2 = 22
QK_SCALE = 8.0

ENGINES = ("sync", "tensor", "vector", "scalar", "gpsimd")
QUEUE_OF = {"sync": "dsync", "gpsimd": "dpool"}


class Buf:
    __slots__ = ("name", "writer", "readers", "aliases")

    def __init__(self, name):
        self.name = name
        self.writer = None
        self.readers = []
        self.aliases = []

    def read_deps(self):
        return {self.writer} if self.writer is not None else set()

    def write_deps(self):
        d = set(self.readers)
        if self.writer is not None:
            d.add(self.writer)
        for a in self.aliases:
            if a.writer is not None:
                d.add(a.writer)
            d.update(a.readers)
        return d


NSLOT = 12


class Prog:
    def __init__(self):
        self.items = []
        self.seq = {e: 0 for e in ENGINES}
        self.dcount = {"dsync": 0, "dpool": 0}

    def op(self, eng, emit, reads=(), writes=(), tag=""):
        deps = set()
        for b in reads:
            deps |= b.read_deps()
        for b in writes:
            deps |= b.write_deps()
        self.seq[eng] += 1
        tok = (eng, self.seq[eng])
        self.items.append(
            {"eng": eng, "emit": emit, "deps": deps, "tok": tok, "dma": None, "tag": tag}
        )
        for b in reads:
            b.readers.append(tok)
        for b in writes:
            b.writer = tok
            b.readers = []
            for a in b.aliases:
                a.writer = tok
                a.readers = []
        return tok

    def dma(self, eng, emit, reads=(), writes=(), tag=""):
        q = QUEUE_OF[eng]
        deps = set()
        for b in reads:
            deps |= b.read_deps()
        for b in writes:
            deps |= b.write_deps()
        self.seq[eng] += 1
        self.dcount[q] += 1
        j = self.dcount[q] - 1
        slot = j % NSLOT
        gen = j // NSLOT
        tok = (f"{q}#{slot}", gen + 1)
        self.items.append(
            {"eng": eng, "emit": emit, "deps": deps, "tok": tok, "dma": tok[0],
             "gate": (tok[0], gen), "tag": tag}
        )
        for b in reads:
            b.readers.append(tok)
        for b in writes:
            b.writer = tok
            b.readers = []
            for a in b.aliases:
                a.writer = tok
                a.readers = []
        return tok

    def emit(self, nc):
        marked = {e: set() for e in ENGINES}
        for it in self.items:
            for (src, s) in it["deps"]:
                if src in self.seq and src != it["eng"]:
                    marked[src].add(s)
        msorted = {e: sorted(marked[e]) for e in ENGINES}
        import bisect

        def rank(e, s):
            return bisect.bisect_right(msorted[e], s)

        esem = {e: nc.alloc_semaphore(f"se_{e}") for e in ENGINES}
        qsem = {}
        qfinal = {}
        for q, n in self.dcount.items():
            for slot in range(min(n, NSLOT)):
                key = f"{q}#{slot}"
                qsem[key] = nc.alloc_semaphore(f"sq_{q}_{slot}")
                qfinal[key] = (n - slot + NSLOT - 1) // NSLOT

        per_eng = {e: [] for e in ENGINES}
        for it in self.items:
            per_eng[it["eng"]].append(it)

        def body(engobj, ename):
            waited = {}
            for it in per_eng[ename]:
                wants = {}
                deps = set(it["deps"])
                if it["dma"] is not None and it["gate"][1] > 0:
                    deps.add(it["gate"])
                for (src, s) in deps:
                    if src in self.seq:
                        if src == ename:
                            continue
                        sem, val = esem[src], rank(src, s)
                    else:
                        sem, val = qsem[src], 16 * s
                    k = sem.num if hasattr(sem, "num") else id(sem)
                    if val > wants.get(k, (None, -1))[1]:
                        wants[k] = (sem, val)
                for sem, val in wants.values():
                    k = sem.num if hasattr(sem, "num") else id(sem)
                    if val > waited.get(k, -1):
                        engobj.wait_ge(sem, val)
                        waited[k] = val
                inst = it["emit"](engobj)
                tok = it["tok"]
                if it["dma"] is not None:
                    inst.then_inc(qsem[it["dma"]], 16)
                elif tok[1] in marked[ename]:
                    inst.then_inc(esem[ename], 1)

        with nc.Block() as block:
            @block.sync
            def _(eng):
                body(eng, "sync")
                for key, gens in qfinal.items():
                    if gens:
                        eng.wait_ge(qsem[key], 16 * gens)

            @block.tensor
            def _(eng):
                body(eng, "tensor")

            @block.vector
            def _(eng):
                body(eng, "vector")

            @block.scalar
            def _(eng):
                body(eng, "scalar")

            @block.gpsimd
            def _(eng):
                body(eng, "gpsimd")


def region(bufs, c0, c1):
    return bufs[c0:c1]


class Ctx:
    pass


def build_nc(debug_taps=False):
    nc = bass.Bass("TRN2", target_bir_lowering=False, detect_race_conditions=False)
    C = Ctx()
    C.nc = nc
    P = Prog()
    C.P = P

    def din(name, shape, dt=F32):
        return nc.dram_tensor(name, list(shape), dt, kind="ExternalInput")

    C.xpref = din("xpref", [DM, PREF])
    C.xout = din("xout", [DM, XCOLS])
    C.wq = din("wq", [DM, DM]); C.wk = din("wk", [DM, DM])
    C.wv = din("wv", [DM, DM]); C.wo = din("wo", [DM, DM])
    C.Wqkv = din("Wqkv", [DEPTH * DM, 3 * DM])
    C.Wo6 = din("Wo6", [DEPTH * DM, DM])
    C.W1 = din("W1", [DEPTH * DM, DFF])
    C.W2 = din("W2", [DEPTH * DFF2, DM])
    C.qs6 = din("qs6", [DEPTH, DH]); C.ks6 = din("ks6", [DEPTH, DH])
    C.lng = din("lng", [DEPTH, DM]); C.lnb = din("lnb", [DEPTH, DM])
    C.b1 = din("b1", [DEPTH, DFF]); C.b2 = din("b2", [DEPTH, DM])
    C.TP = din("TP", [128, 2 * PREF])
    C.TO = din("TO", [128, 2 * XCOLS])
    C.TI = din("TI", [128, 4 * XCOLS], BF16)
    C.MSK = din("MSK", [128, 3 * 384], BF16)
    C.UT = din("UT", [128, 128], BF16)
    C.R2M = din("R2M", [128, 128], BF16)
    C.ROM = din("ROM", [128, 128], BF16)
    C.IDEN = din("IDEN", [128, 128], BF16)
    C.E2 = din("E2", [128, 2], BF16)
    C.CONSTS = din("CONSTS", [16])
    C.ONED = din("ONED", [64, 1])
    C.BASE2D = din("BASE2D", [128, 1])

    C.y = nc.dram_tensor("y", [DM, SL], F32, kind="ExternalOutput")

    C.xA = nc.dram_tensor("xA", [DM, XCOLS], F32)
    C.xB = nc.dram_tensor("xB", [DM, XCOLS], F32)
    C.out1 = nc.dram_tensor("out1d", [DM, SL], F32)
    C.wqkv_bf = nc.dram_tensor("wqkv_bf", [DEPTH * DM, 3 * DM], BF16)
    C.wo6_bf = nc.dram_tensor("wo6_bf", [DEPTH * DM, DM], BF16)
    C.w1_bf = nc.dram_tensor("w1_bf", [DEPTH * DM, DFF], BF16)
    C.w2_bf = nc.dram_tensor("w2_bf", [DEPTH * DFF2, DM], BF16)
    C.wqo_bf = nc.dram_tensor("wqo_bf", [4 * DM, DM], BF16)

    C.xA_b = [Buf(f"xA{c}") for c in range(NXC)]
    C.xB_b = [Buf(f"xB{c}") for c in range(NXC)]
    C.out1_b = [Buf(f"o1{c}") for c in range(SL // CH)]
    C.wbf_b = Buf("wbf")
    C.y_b = Buf("y")

    placed = []

    def SBAT(name, shape, dtype, off):
        nbytes = int(np.prod(shape[1:])) * (2 if dtype == BF16 else 4) if len(shape) > 1 else 4
        t = nc.alloc_sbuf_tensor_at(name, list(shape), dtype, offset=C.sb_base + off)
        b = Buf(name)
        for (o2, s2, b2) in placed:
            if off < o2 + s2 and o2 < off + nbytes:
                b.aliases.append(b2)
                b2.aliases.append(b)
        placed.append((off, nbytes, b))
        C.layout_max = max(getattr(C, "layout_max", 0), off + nbytes)
        return t, b

    C.sb_base = (nc.sbuf_base + 31) // 32 * 32
    cur = {"w": 0}

    def place(name, shape, dtype, region, pad=0):
        t, b = SBAT(name, shape, dtype, cur[region])
        nbytes = int(np.prod(shape[1:])) * (2 if dtype == BF16 else 4)
        cur[region] += ((nbytes + pad + 31) // 32) * 32
        return t, b

    C.Wqkv_sb, C.Wqkv_B = place("Wqkv_sb", [128, NFC * 3 * DM], BF16, "w")
    C.W1_sb, C.W1_B = place("W1_sb", [128, NFC * DFF], BF16, "w")
    C.W2_sb, C.W2_B = [], []
    for i in range(2):
        t, b = place(f"W2_sb{i}", [128, DM], BF16, "w")
        C.W2_sb.append(t); C.W2_B.append(b)
    C.WoB_sb, C.WoB_B = [], []
    for i in range(2):
        t, b = place(f"WoB{i}", [128, DM], BF16, "w")
        C.WoB_sb.append(t); C.WoB_B.append(b)
    W_END = cur["w"]
    assert W_END <= 144704 + 64, W_END

    cur["ow"] = 49152
    C.OWo_sb, C.OWo_B = place("OWo_sb", [128, NFC * DM], BF16, "ow")
    C.oxt, C.oxt_B = place("oxt", [128, NFC * 512], BF16, "ow")
    C.oyb, C.oyb_B = place("oyb", [128, NFC * 512], BF16, "ow")
    _ovb_off = cur["ow"]
    C.ovb, C.ovb_B = place("ovb", [128, NFC * 512], BF16, "ow")
    cur["ow_att"] = _ovb_off
    C.oatt, C.oatt_B = place("oatt", [128, NFC * 512], BF16, "ow_att")
    C.oQ, C.oQ_B = place("oQ", [128, NFC * 512], BF16, "ow")
    C.oK, C.oK_B = place("oK", [128, NFC * 512], BF16, "ow")
    C.oV, C.oV_B = place("oV", [128, 4 * NH * 65], BF16, "ow")
    C.oTt, C.oTt_B = [], []
    for i in range(2):
        t, b = place(f"oTt{i}", [128, 2 * 512], F32, "ow")
        C.oTt.append(t); C.oTt_B.append(b)
    C.odrow, C.odrow_B = place("odrow", [1, 128], F32, "ow")
    C.orc, C.orc_B = place("orc", [1, 128], BF16, "ow")
    C.orb, C.orb_B = place("orb", [64, 128], F32, "ow")
    C.oKmS, C.oKmS_B = place("oKmS", [128, NFC * 128], BF16, "ow")
    C.qc0, C.qc0_B = place("qc0", [128, SCH], BF16, "ow")
    C.kc0, C.kc0_B = place("kc0", [64, SCH], BF16, "ow")
    C.oSt, C.oSt_B = place("oSt", [64, NH * 65], F32, "ow")
    C.oStb, C.oStb_B = place("oStb", [128, NH * 65], BF16, "ow")
    C.tbig, C.tbig_B = [], []
    for i in range(2):
        t, b = place(f"tbig{i}", [128, 512], BF16, "ow")
        C.tbig.append(t); C.tbig_B.append(b)
    C.oSm, C.oSm_B = place("oSm", [128, 128], BF16, "ow")
    C.ofr, C.ofr_B = [], []
    for i in range(2):
        t, b = place(f"ofr{i}", [128, 512], F32, "ow")
        C.ofr.append(t); C.ofr_B.append(b)
    assert cur["ow"] <= 144704, cur["ow"]

    ACT = 144704
    cur["act"] = ACT
    C.sqb, C.sqb_B = place("sqb", [128, NFC * 512], BF16, "act")
    OV = cur["act"]
    cur["attn"] = OV
    C.xbf, C.xbf_B = place("xbf", [128, NFC * (CH + 1)], BF16, "attn")
    cur["attn_att"] = OV
    C.att, C.att_B = place("att", [128, NFC * CH], BF16, "attn_att")
    C.Qb, C.Qb_B = place("Qb", [128, NFC * CH], BF16, "attn")
    C.Kc, C.Kc_B = place("Kc", [128, NFC * CH], BF16, "attn")
    C.Kp, C.Kp_B = [], []
    for i in range(2):
        t, b = place(f"Kp{i}", [128, NFC * CH], BF16, "attn")
        C.Kp.append(t); C.Kp_B.append(b)
    C.Vkm, C.Vkm_B = [], []
    for i in range(2):
        t, b = place(f"Vkm{i}", [128, 2 * NH * DH], BF16, "attn")
        C.Vkm.append(t); C.Vkm_B.append(b)
    _sm_off = cur["attn"]
    C.Sm, C.Sm_B = place("Sm", [128, 384], F32, "attn")
    C.Pexp, C.Pexp_B = place("Pexp", [128, 384], BF16, "attn")
    C.Pt, C.Pt_B = place("Pt", [128, 384], BF16, "attn")
    C.TIt, C.TIt_B = place("TIt", [128, 4 * CH], BF16, "attn")
    _tq_off = cur["attn"]
    C.tq, C.tq_B = [], []
    for i in range(1):
        t, b = place(f"tq{i}", [128, CH], F32, "attn")
        C.tq.append(t); C.tq_B.append(b)
    C.tb, C.tb_B = [], []
    for i in range(3):
        t, b = place(f"tb{i}", [128, CH], BF16, "attn")
        C.tb.append(t); C.tb_B.append(b)
    C.sums, C.sums_B = place("sums", [128, 2 * NH], F32, "attn")
    C.recipc, C.recipc_B = place("recipc", [128, 2 * NH], F32, "attn")
    C.nmax, C.nmax_B = place("nmax", [128, 2], F32, "attn")
    C.xmo, C.xmo_B = [], []
    for i in range(2):
        t, b = place(f"xmo{i}", [128, 2 * CH], F32, "attn")
        C.xmo.append(t); C.xmo_B.append(b)
    cur["attn_afr0"] = _tq_off
    cur["attn_afr1"] = _sm_off
    C.afr, C.afr_B = [], []
    t, b = place("afr0", [128, 512], F32, "attn_afr0")
    C.afr.append(t); C.afr_B.append(b)
    t, b = place("afr1", [128, 512], F32, "attn_afr1")
    C.afr.append(t); C.afr_B.append(b)
    C.MSK_sb, C.MSK_B = place("MSK_sb", [128, 3 * 384], BF16, "attn")
    assert cur["attn"] <= ACT + 56064, cur["attn"]
    cur["ffn"] = OV
    C.xsb, C.xsb_B = place("xsb", [128, NFC * (CH + 1)], BF16, "ffn")
    C.xn, C.xn_B = place("xn", [128, NFC * CH], BF16, "ffn")
    C.h2i, C.h2i_B = [], []
    for i in range(2):
        t, b = place(f"h2i{i}", [128, CH], BF16, "ffn")
        C.h2i.append(t); C.h2i_B.append(b)
    C.gact, C.gact_B = [], []
    for i in range(2):
        t, b = place(f"gact{i}", [128, CH], BF16, "ffn")
        C.gact.append(t); C.gact_B.append(b)
    C.fres, C.fres_B = [], []
    for i in range(2):
        t, b = place(f"fres{i}", [128, 512], F32, "ffn")
        C.fres.append(t); C.fres_B.append(b)
    C.xnext, C.xnext_B = place("xnext", [128, NFC * CH], F32, "ffn")
    C.h2, C.h2_B = place("h2", [128, NOG1 * CH], BF16, "ffn")
    assert cur["ffn"] <= ACT + 56064, cur["ffn"]
    cur["oep"] = OV
    C.ozb, C.ozb_B = place("ozb", [128, NFC * 512], BF16, "oep")
    C.oo1, C.oo1_B = place("oo1", [128, NFC * 512], F32, "oep")
    C.oo1b, C.oo1b_B = place("oo1b", [128, NFC * 512], BF16, "oep")
    assert cur["oep"] <= ACT + 56064, cur["oep"]

    C.cursors = dict(cur)
    cur["fix"] = ((max(cur["attn"], cur["ffn"], cur["oep"]) + 31) // 32) * 32
    C.UT_sb, _b = place("UT_sb", [128, 128], BF16, "fix")
    C.R2_sb, _b = place("R2_sb", [128, 128], BF16, "fix")
    C.RO_sb, _b = place("RO_sb", [128, 128], BF16, "fix")
    C.ID_sb, _b = place("ID_sb", [128, 128], BF16, "fix")
    C.E2_sb, _b = place("E2_sb", [128, 2], BF16, "fix")
    C.ONES_sb, _b = place("ONES_sb", [128, 2], BF16, "fix")
    C.ONErow_sb, _b = place("ONErow_sb", [1, 128], BF16, "fix")
    C.CONST_sb, C.const_B = place("CONST_sb", [128, 16], F32, "fix")
    C.BASE2_sb, C.BASE2_B = place("BASE2_sb", [128, 1], F32, "fix")
    C.LQS_sb, C.LQS_B = place("LQS_sb", [2, 128], BF16, "fix")
    C.LKS_sb, C.LKS_B = place("LKS_sb", [2, 128], BF16, "fix")
    C.Lgb_sb, C.Lgb_B = place("Lgb_sb", [2, NFC * 128], BF16, "fix")
    C.gcol_sb, C.gcol_B = place("gcol_sb", [128, NFC], F32, "fix")
    C.b1a_sb, C.b1a_B = place("b1a_sb", [128, NOG1], F32, "fix")
    C.b1g_sb, C.b1g_B = place("b1g_sb", [128, NOG1], F32, "fix")
    C.b2c_sb, C.b2c_B = place("b2c_sb", [128, NFC], F32, "fix")
    C.rmu, C.rmu_B = place("rmu", [1, 512], F32, "fix")
    C.rvar, C.rvar_B = place("rvar", [1, 512], F32, "fix")
    C.rmsq, C.rmsq_B = place("rmsq", [1, 512], F32, "fix")
    C.rtmp, C.rtmp_B = C.rmsq, C.rmsq_B
    C.rowp, C.rowp_B = place("rowp", [2, 512], BF16, "fix")
    C.rstdb, C.rstdb_B = place("rstdb", [1, 512], BF16, "fix")
    C.rr, C.rr_B = place("rr", [2, CH], BF16, "fix")
    C.zcol, C.zcol_B = place("zcol", [128, 8], F32, "fix")
    C.lnt, C.lnt_B = place("lnt", [2, CH], F32, "fix")
    total = cur["fix"]
    avail = nc.sbuf_bytes_remaining if C.sb_base == nc.sbuf_base else 212863
    assert total <= 212863, f"SBUF layout over budget: {total}"

    cb = Buf("cbufs")
    C.cb = cb

    C.PS = [nc.alloc_psum_tensor(f"ps{i}", [128, 512], F32) for i in range(8)]
    C.PB = [Buf(f"ps{i}") for i in range(8)]

    def tap(name):
        if not debug_taps:
            return
        d = nc.dram_tensor(name, [DM, XCOLS], F32, kind="ExternalOutput")
        P.dma("gpsimd", (lambda d=d: lambda e: e.dma_start(out=d.ap(), in_=C.xA.ap()))(),
              reads=list(C.xA_b), writes=[], tag="tap")

    build_setup(C)
    build_prefix(C)
    build_outer(C)
    tap("dbg_x0")
    for ell in range(DEPTH):
        build_attn_layer(C, ell)
        if debug_taps and ell == 0:
            dxb = nc.dram_tensor("dbg_xb0", [DM, XCOLS], F32, kind="ExternalOutput")
            P.dma("gpsimd", lambda e: e.dma_start(out=dxb.ap(), in_=C.xB.ap()),
                  reads=list(C.xB_b), writes=[], tag="tapb")
        build_ffn_layer(C, ell)
        tap(f"dbg_x{ell + 1}")
    build_final(C)

    if debug_taps:
        C.dbg_out1 = nc.dram_tensor("dbg_out1", [DM, SL], F32, kind="ExternalOutput")
        P.dma(
            "gpsimd",
            lambda e: e.dma_start(out=C.dbg_out1.ap(), in_=C.out1.ap()),
            reads=list(C.out1_b),
            writes=[],
        )

    P.emit(nc)
    return nc, C


C_EPS24, C_EPS5, C_EPS6, C_I1024, C_MHALF, C_M1 = 0, 1, 2, 3, 4, 5


def ccol(C, i):
    return C.CONST_sb[:, i:i + 1]


def build_setup(C):
    P, nc = C.P, C.nc
    loads = [
        (C.UT_sb, C.UT), (C.R2_sb, C.R2M), (C.RO_sb, C.ROM),
        (C.ID_sb, C.IDEN), (C.E2_sb, C.E2),
    ]
    P.dma("sync", lambda e: e.dma_start(out=C.BASE2_sb[:, :], in_=C.BASE2D.ap()),
          writes=[C.BASE2_B], tag="constload")
    for sb, dr in loads:
        P.dma("sync", (lambda sb=sb, dr=dr: lambda e: e.dma_start(out=sb[:, :], in_=dr.ap()))(),
              writes=[C.cb], tag="constload")
    P.dma("gpsimd",
          lambda e: e.dma_start(out=C.CONST_sb[:, :],
                                in_=C.CONSTS.ap().unsqueeze(0).broadcast_to([128, 16])),
          writes=[C.const_B], tag="constload")
    P.op("vector", lambda e: e.memset(C.ONES_sb[:, :], 1.0), writes=[C.cb])
    P.op("vector", lambda e: e.memset(C.ONErow_sb[:, :], 1.0), writes=[C.cb])
    P.op("vector", lambda e: e.memset(C.rowp[0:2, :], -1.0), writes=[C.rowp_B])
    P.dma("gpsimd",
          lambda e: e.dma_start(
              out=C.oV[:, :].rearrange("p (en d) -> p en d", d=65)[:, :, 64:65],
              in_=C.ONED.ap().unsqueeze(0).broadcast_to([128, 64, 1]),
              allow_slow_non_contiguous=True),
          writes=[C.oV_B], tag="onedl")
    P.op("vector", lambda e: e.memset(C.oSt[:, :], 0.0), writes=[C.oSt_B])
    P.op("vector", lambda e: e.memset(C.zcol[:, :], 0.0), writes=[C.zcol_B])
    for ell in range(1, DEPTH):
        col = (14 - W_LEAD[ell]) * CH - 1
        P.dma("sync",
              (lambda col=col: lambda e: e.dma_start(
                  out=C.xA.ap().rearrange("(c p) w -> p c w", p=128)[:, :, col:col + 1],
                  in_=C.zcol[:, :].rearrange("p (c w) -> p c w", w=1),
                  allow_slow_non_contiguous=True))(),
              reads=[C.zcol_B],
              writes=region(C.xA_b, col // CH, col // CH + 1), tag="zc")
    P.op("vector", lambda e: e.memset(C.oStb[:, :], 0.0), writes=[C.oStb_B])
    P.op("vector", lambda e: e.memset(C.qc0[:, :], 0.0), writes=[C.qc0_B])

    pre = [
        (C.wqkv_bf, C.Wqkv), (C.wo6_bf, C.Wo6), (C.w1_bf, C.W1), (C.w2_bf, C.W2),
    ]
    for dst, src in pre:
        P.dma("gpsimd", (lambda dst=dst, src=src: lambda e: e.dma_start(out=dst.ap(), in_=src.ap()))(),
              writes=[C.wbf_b], tag="precast")
    for i, src in enumerate([C.wq, C.wk, C.wv, C.wo]):
        P.dma("gpsimd",
              (lambda i=i, src=src: lambda e: e.dma_start(
                  out=C.wqo_bf.ap()[i * DM:(i + 1) * DM, :], in_=src.ap()))(),
              writes=[C.wbf_b], tag="precast")

    for i in range(3):
        P.dma("sync",
              (lambda i=i: lambda e: e.dma_start(
                  out=C.Wqkv_sb[:, :].rearrange("p (w k c) -> p w k c", w=3, k=NFC)[:, i],
                  in_=C.wqo_bf.ap()[i * DM:(i + 1) * DM, :].rearrange("(k p) c -> p k c", p=128)))(),
              reads=[C.wbf_b], writes=[C.Wqkv_B], tag="owload")
    P.dma("sync",
          lambda e: e.dma_start(
              out=C.OWo_sb[:, :].rearrange("p (k c) -> p k c", k=NFC),
              in_=C.wqo_bf.ap()[3 * DM:4 * DM, :].rearrange("(k p) c -> p k c", p=128)),
          reads=[C.wbf_b], writes=[C.OWo_B], tag="owload")


def ln_rows(C, xsrc_bf, xsrc_buf, ncols, pb_sums, pb_sq, sq_tile, sq_buf, tag):
    P = C.P
    P.op("scalar",
         lambda e: e.activation(sq_tile[:, :NFC * ncols], xsrc_bf[:, :NFC * ncols], AF.Square),
         reads=[xsrc_buf], writes=[sq_buf], tag=tag + "_sq")
    for kc in range(NFC):
        P.op("tensor",
             (lambda kc=kc: lambda e: e.matmul(
                 C.PS[pb_sums][0:1, :ncols], C.ONES_sb[:, 0:1],
                 xsrc_bf[:, kc * ncols:(kc + 1) * ncols],
                 start=(kc == 0), stop=(kc == NFC - 1)))(),
             reads=[xsrc_buf, C.cb], writes=[C.PB[pb_sums]], tag=tag + "_s")
        P.op("tensor",
             (lambda kc=kc: lambda e: e.matmul(
                 C.PS[pb_sq][0:1, :ncols], C.ONES_sb[:, 0:1],
                 sq_tile[:, kc * ncols:(kc + 1) * ncols],
                 start=(kc == 0), stop=(kc == NFC - 1)))(),
             reads=[sq_buf, C.cb], writes=[C.PB[pb_sq]], tag=tag + "_sq2")


def ln_rows_finish(C, ncols, pb_sums, pb_sq, rows, rows_B, rstdb, rstdb_B, murow_dst, murow_buf, tag):
    P = C.P
    P.op("scalar",
         lambda e: e.activation(C.rmu[0:1, :ncols], C.PS[pb_sums][0:1, :ncols], AF.Copy,
                                scale=ccol(C, C_I1024)[0:1, :]),
         reads=[C.PB[pb_sums], C.const_B], writes=[C.rmu_B], tag=tag + "_mu")
    P.op("vector",
         lambda e: e.tensor_mul(C.rmsq[0:1, :ncols], C.rmu[0:1, :ncols], C.rmu[0:1, :ncols]),
         reads=[C.rmu_B], writes=[C.rmsq_B], tag=tag + "_musq")
    P.op("vector",
         lambda e: e.scalar_tensor_tensor(C.rvar[0:1, :ncols], C.PS[pb_sq][0:1, :ncols],
                                          ccol(C, C_I1024)[0:1, :], C.rmsq[0:1, :ncols],
                                          ALU.mult, ALU.subtract),
         reads=[C.PB[pb_sq], C.rmsq_B, C.const_B], writes=[C.rvar_B], tag=tag + "_var")
    P.op("scalar",
         lambda e: e.activation(C.rtmp[0:1, :ncols], C.rvar[0:1, :ncols], AF.Ln,
                                bias=ccol(C, C_EPS5)[0:1, :]),
         reads=[C.rvar_B, C.const_B], writes=[C.rtmp_B], tag=tag + "_ln")
    P.op("scalar",
         lambda e: e.activation(rstdb[0:1, :ncols], C.rtmp[0:1, :ncols], AF.Exp,
                                scale=ccol(C, C_MHALF)[0:1, :]),
         reads=[C.rtmp_B, C.const_B], writes=[rstdb_B], tag=tag + "_rstd")
    P.op("vector",
         lambda e: e.tensor_mul(murow_dst[0:1, :ncols], C.rmu[0:1, :ncols], rstdb[0:1, :ncols]),
         reads=[C.rmu_B, rstdb_B], writes=[murow_buf], tag=tag + "_mur")


def plain_ln(C, xb, xb_B, dst, dst_B, TT, tag):
    P = C.P
    ln_rows(C, xb, xb_B, TT, 0, 1, C.sqb, C.sqb_B, tag)
    ln_rows_finish(C, TT, 0, 1, None, None, C.rstdb, C.rstdb_B,
                   C.rowp, C.rowp_B, tag)
    P.op("tensor",
         lambda e: e.matmul(C.PS[2][:, :TT], C.ONErow_sb[0:1, :], C.rstdb[0:1, :TT],
                            start=True, stop=True),
         reads=[C.rstdb_B, C.cb], writes=[C.PB[2]], tag=tag + "_A")
    P.op("tensor",
         lambda e: e.matmul(C.PS[3][:, :TT], C.ONErow_sb[0:1, :], C.rowp[0:1, :TT],
                            start=True, stop=True),
         reads=[C.rowp_B, C.cb], writes=[C.PB[3]], tag=tag + "_C")
    for fc in range(NFC):
        P.op("vector",
             (lambda fc=fc: lambda e: e.tensor_mul(
                 dst[:, fc * TT:(fc + 1) * TT], xb[:, fc * TT:(fc + 1) * TT],
                 C.PS[2][:, :TT]))(),
             reads=[xb_B, C.PB[2]], writes=[dst_B], tag=tag + "_p1")
        P.op("vector",
             (lambda fc=fc: lambda e: e.tensor_sub(
                 dst[:, fc * TT:(fc + 1) * TT], dst[:, fc * TT:(fc + 1) * TT],
                 C.PS[3][:, :TT]))(),
             reads=[dst_B, C.PB[3]], writes=[dst_B], tag=tag + "_p2")


def build_prefix(C):
    P = C.P
    TT = 512
    ntiles = PREF // TT
    for t in range(ntiles):
        xb, xb_B = C.oxt, C.oxt_B
        P.dma("gpsimd",
              (lambda t=t, xb=xb: lambda e: e.dma_start(
                  out=xb[:, :].rearrange("p (c w) -> p c w", c=NFC),
                  in_=C.xpref.ap().rearrange("(c p) w -> p c w", p=128)[:, :, t * TT:(t + 1) * TT]))(),
              writes=[xb_B], tag="pf_x")
        tt, tt_B = C.oTt[t % 2], C.oTt_B[t % 2]
        P.dma("sync",
              (lambda t=t, tt=tt: lambda e: e.dma_start(
                  out=tt[:, :], in_=C.TP.ap()[:, t * 2 * TT:(t + 1) * 2 * TT]))(),
              writes=[tt_B], tag="pf_tt")
        plain_ln(C, xb, xb_B, C.oyb, C.oyb_B, TT, "pf_ln")
        kv_proj_scan(C, t, TT, C.oyb, C.oyb_B, tt, tt_B, with_q=False)


def kv_proj_scan(C, t, TT, ysrc, ysrc_B, tt, tt_B, with_q):
    P = C.P
    nsc = TT // SCH

    def wslot(i):
        return lambda kc, og: C.Wqkv_sb[:, (i * NFC + kc) * DM + og * 128:
                                        (i * NFC + kc) * DM + og * 128 + 128]

    proj_rope_outer(C, wslot(1), ysrc, ysrc_B, TT, tt, tt_B, sin_neg=False,
                    dst=C.oK, dst_B=C.oK_B, tag=f"kp{t}")
    for og in range(NFC):
        pb = 4 + og % 2
        for kc in range(NFC):
            P.op("tensor",
                 (lambda og=og, kc=kc, pb=pb: lambda e: e.matmul(
                     C.PS[pb][:, :TT], wslot(2)(kc, og),
                     ysrc[:, kc * TT:(kc + 1) * TT],
                     start=(kc == 0), stop=(kc == NFC - 1)))(),
                 reads=[ysrc_B, C.Wqkv_B], writes=[C.PB[pb]], tag=f"vproj{t}")
        P.op("scalar",
             (lambda og=og, pb=pb: lambda e: e.activation(
                 C.ovb[:, og * TT:(og + 1) * TT], C.PS[pb][:, :TT], AF.Copy))(),
             reads=[C.PB[pb]], writes=[C.ovb_B], tag=f"vevac{t}")
    for sc in range(nsc):
        for fc in range(NFC):
            pb = 6 + fc % 2
            P.op("tensor",
                 (lambda sc=sc, fc=fc, pb=pb: lambda e: e.transpose(
                     C.PS[pb].ap().bitcast(BF16)[:, 0:128],
                     C.ovb[:, fc * TT + sc * SCH: fc * TT + (sc + 1) * SCH],
                     C.ID_sb[:, :]))(),
                 reads=[C.ovb_B, C.cb], writes=[C.PB[pb]], tag=f"vT{t}")
            P.op("vector",
                 (lambda sc=sc, fc=fc, pb=pb: lambda e: e.tensor_copy(
                     C.oV[:, :].rearrange("p (en d) -> p en d", d=65)
                     [:, sc * NH + 2 * fc: sc * NH + 2 * fc + 2, 0:64],
                     C.PS[pb].ap().bitcast(BF16)[:, 0:128].rearrange(
                         "p (en d) -> p en d", d=DH)))(),
                 reads=[C.PB[pb]], writes=[C.oV_B], tag=f"vTe{t}")
    if with_q:
        proj_rope_outer(C, wslot(0), ysrc, ysrc_B, TT, tt, tt_B, sin_neg=False,
                        dst=C.oQ, dst_B=C.oQ_B, tag=f"qp{t}")
    for sc in range(nsc):
        for fc in range(NFC):
            pb = 6 + fc % 2
            P.op("tensor",
                 (lambda sc=sc, fc=fc, pb=pb: lambda e: e.transpose(
                     C.PS[pb].ap().bitcast(BF16)[:, 0:128],
                     C.oK[:, fc * TT + sc * SCH: fc * TT + (sc + 1) * SCH],
                     C.ID_sb[:, :]))(),
                 reads=[C.oK_B, C.cb], writes=[C.PB[pb]], tag=f"kT{t}")
            P.op("vector",
                 (lambda fc=fc, pb=pb: lambda e: e.tensor_copy(
                     C.oKmS[:, fc * 128:(fc + 1) * 128],
                     C.PS[pb].ap().bitcast(BF16)[:, 0:128]))(),
                 reads=[C.PB[pb]], writes=[C.oKmS_B], tag=f"kTe{t}")
        for h in range(NH):
            scan_chunk(C, t, sc, h, TT, with_q)


def proj_rope_outer(C, wsl, ysrc, ysrc_B, TT, tt, tt_B, sin_neg, dst, dst_B, tag):
    P = C.P
    for og in range(NFC):
        pb = 4 + og % 2
        for kc in range(NFC):
            P.op("tensor",
                 (lambda og=og, kc=kc, pb=pb: lambda e: e.matmul(
                     C.PS[pb][:, :TT], wsl(kc, og), ysrc[:, kc * TT:(kc + 1) * TT],
                     start=(kc == 0), stop=(kc == NFC - 1)))(),
                 reads=[ysrc_B, C.Wqkv_B], writes=[C.PB[pb]], tag=tag + "_mm")
        P.op("scalar",
             (lambda og=og, pb=pb: lambda e: e.activation(
                 dst[:, og * TT:(og + 1) * TT], C.PS[pb][:, :TT], AF.Copy))(),
             reads=[C.PB[pb]], writes=[dst_B], tag=tag + "_ev")
        P.op("tensor",
             (lambda og=og: lambda e: e.matmul(
                 C.PS[6][:, :TT], C.RO_sb[:, :], dst[:, og * TT:(og + 1) * TT],
                 start=True, stop=True))(),
             reads=[dst_B, C.cb], writes=[C.PB[6]], tag=tag + "_ro")
        P.op("vector",
             (lambda og=og: lambda e: e.tensor_mul(
                 C.tbig[0][:, :TT], C.PS[6][:, :TT], tt[:, TT:2 * TT]))(),
             reads=[C.PB[6], tt_B], writes=[C.tbig_B[0]], tag=tag + "_t2")
        P.op("gpsimd",
             (lambda og=og: lambda e: e.tensor_mul(
                 C.tbig[1][:, :TT], dst[:, og * TT:(og + 1) * TT], tt[:, 0:TT]))(),
             reads=[dst_B, tt_B], writes=[C.tbig_B[1]], tag=tag + "_t1")
        P.op("gpsimd",
             (lambda og=og: lambda e: e.tensor_add(
                 C.tbig[1][:, :TT], C.tbig[1][:, :TT], C.tbig[0][:, :TT]))(),
             reads=[C.tbig_B[1], C.tbig_B[0]], writes=[C.tbig_B[1]], tag=tag + "_ad")
        P.op("scalar",
             (lambda og=og: lambda e: e.activation(
                 dst[:, og * TT:(og + 1) * TT], C.tbig[1][:, :TT], AF.Relu))(),
             reads=[C.tbig_B[1]], writes=[dst_B], tag=tag + "_rl")


def scan_chunk(C, t, sc, h, TT, with_q):
    P = C.P
    fc, hp = h // 2, h % 2
    r0 = hp * 64
    kcols = lambda buf: buf[r0:r0 + 64, fc * TT + sc * SCH: fc * TT + (sc + 1) * SCH]
    ent = sc * NH + h
    vkm = C.oV[:, :].rearrange("p (en d) -> p en d", d=65)[:, ent]
    stf = C.oSt[:, h * 65:(h + 1) * 65]
    stb = C.oStb[:, h * 65:(h + 1) * 65]
    stb_lo = C.oStb[0:64, h * 65:(h + 1) * 65]

    P.op("gpsimd",
         lambda e: e.tensor_copy(C.kc0[:, :], kcols(C.oK)),
         reads=[C.oK_B], writes=[C.kc0_B], tag=f"kcp{t}")
    if with_q:
        P.op("gpsimd",
             lambda e: e.tensor_copy(C.qc0[0:64, :], kcols(C.oQ)),
             reads=[C.oQ_B], writes=[C.qc0_B], tag=f"qcp{t}")
        P.op("tensor",
             lambda e: e.matmul(C.PS[5][:, 0:128], C.kc0[:, :], C.qc0[0:64, :],
                                start=True, stop=True),
             reads=[C.kc0_B, C.qc0_B], writes=[C.PB[5]], tag=f"sT{t}")
        P.op("vector",
             lambda e: e.tensor_mul(C.oSm[:, :], C.PS[5][:, 0:128], C.UT_sb[:, :]),
             reads=[C.PB[5], C.cb], writes=[C.oSm_B], tag=f"sTm{t}")
        P.op("tensor",
             lambda e: e.matmul(C.PS[6][0:65, 0:128], stb, C.qc0[:, :],
                                start=True, stop=False),
             reads=[C.oStb_B, C.qc0_B], writes=[C.PB[6]], tag=f"nd1{t}")
        P.op("tensor",
             lambda e: e.matmul(C.PS[6][0:65, 0:128], vkm, C.oSm[:, :],
                                start=False, stop=True),
             reads=[C.oV_B, C.oSm_B], writes=[C.PB[6]], tag=f"nd2{t}")
        P.op("scalar",
             lambda e: e.copy(C.odrow[0:1, 0:128], C.PS[6][64:65, 0:128]),
             reads=[C.PB[6]], writes=[C.odrow_B], tag=f"dcp{t}")
        P.op("scalar",
             lambda e: e.activation(C.odrow[0:1, 0:128],
                                    C.odrow[0:1, 0:128], AF.Ln,
                                    bias=ccol(C, C_EPS6)[0:1, :]),
             reads=[C.odrow_B, C.const_B], writes=[C.odrow_B], tag=f"dln{t}")
        P.op("scalar",
             lambda e: e.activation(C.orc[0:1, 0:128],
                                    C.odrow[0:1, 0:128], AF.Exp,
                                    scale=ccol(C, C_M1)[0:1, :]),
             reads=[C.odrow_B, C.const_B], writes=[C.orc_B], tag=f"dex{t}")
        P.op("tensor",
             lambda e: e.matmul(C.PS[7][0:64, 0:128], C.ONErow_sb[0:1, 0:64],
                                C.orc[0:1, 0:128], start=True, stop=True),
             reads=[C.orc_B, C.cb], writes=[C.PB[7]], tag=f"dbc{t}")
        P.op("scalar",
             lambda e: e.copy(C.orb[:, :], C.PS[7][0:64, 0:128]),
             reads=[C.PB[7]], writes=[C.orb_B], tag=f"dbe{t}")
        P.op("vector",
             lambda e: e.tensor_mul(
                 C.oatt[r0:r0 + 64, fc * TT + sc * SCH: fc * TT + (sc + 1) * SCH],
                 C.PS[6][0:64, 0:128], C.orb[:, :]),
             reads=[C.PB[6], C.orb_B], writes=[C.oatt_B], tag=f"dmu{t}")

    P.op("tensor",
         lambda e: e.matmul(C.PS[4][0:64, 0:65],
                            C.oKmS[:, fc * 128 + r0: fc * 128 + r0 + 64], vkm,
                            start=True, stop=True),
         reads=[C.oKmS_B, C.oV_B], writes=[C.PB[4]], tag=f"cM{t}")
    P.op("vector",
         lambda e: e.tensor_add(stf, stf, C.PS[4][0:64, 0:65]),
         reads=[C.oSt_B, C.PB[4]], writes=[C.oSt_B], tag=f"stu{t}")
    P.op("gpsimd",
         lambda e: e.tensor_copy(stb_lo, stf),
         reads=[C.oSt_B], writes=[C.oStb_B], tag=f"stc{t}")


def build_outer(C):
    P = C.P
    TT = 512
    ntiles = XCOLS // TT
    for t in range(ntiles):
        xb, xb_B = C.oxt, C.oxt_B
        P.dma("gpsimd",
              (lambda t=t, xb=xb: lambda e: e.dma_start(
                  out=xb[:, :].rearrange("p (c w) -> p c w", c=NFC),
                  in_=C.xout.ap().rearrange("(c p) w -> p c w", p=128)[:, :, t * TT:(t + 1) * TT]))(),
              writes=[xb_B], tag="ot_x")
        tt, tt_B = C.oTt[t % 2], C.oTt_B[t % 2]
        P.dma("sync",
              (lambda t=t, tt=tt: lambda e: e.dma_start(
                  out=tt[:, :], in_=C.TO.ap()[:, t * 2 * TT:(t + 1) * 2 * TT]))(),
              writes=[tt_B], tag="ot_tt")
        plain_ln(C, xb, xb_B, C.oyb, C.oyb_B, TT, "ot_ln")
        kv_proj_scan(C, 100 + t, TT, C.oyb, C.oyb_B, tt, tt_B, with_q=True)
        for og in range(NFC):
            pb = 4 + og % 2
            for kc in range(NFC):
                P.op("tensor",
                     (lambda og=og, kc=kc, pb=pb: lambda e: e.matmul(
                         C.PS[pb][:, :TT],
                         C.OWo_sb[:, kc * DM + og * 128: kc * DM + og * 128 + 128],
                         C.oatt[:, kc * TT:(kc + 1) * TT],
                         start=(kc == 0), stop=(kc == NFC - 1)))(),
                     reads=[C.oatt_B, C.OWo_B], writes=[C.PB[pb]], tag="ot_wo")
            P.op("scalar",
                 (lambda og=og, pb=pb: lambda e: e.activation(
                     C.ozb[:, og * TT:(og + 1) * TT], C.PS[pb][:, :TT], AF.Copy))(),
                 reads=[C.PB[pb]], writes=[C.ozb_B], tag="ot_ze")
        plain_ln(C, C.ozb, C.ozb_B, C.oo1, C.oo1_B, TT, "ot_l2")
        for fc in range(NFC):
            fr, fr_B = C.ofr[fc % 2], C.ofr_B[fc % 2]
            P.dma("sync",
                  (lambda t=t, fc=fc, fr=fr: lambda e: e.dma_start(
                      out=fr[:, :],
                      in_=C.xout.ap()[fc * 128:(fc + 1) * 128, t * TT:(t + 1) * TT]))(),
                  writes=[fr_B], tag="ot_fr")
            P.op("gpsimd",
                 (lambda fc=fc, fr=fr: lambda e: e.tensor_add(
                     C.oo1[:, fc * TT:(fc + 1) * TT], C.oo1[:, fc * TT:(fc + 1) * TT],
                     fr[:, :]))(),
                 reads=[C.oo1_B, fr_B], writes=[C.oo1_B], tag="ot_o1c")
        if (t + 1) * TT > OLEAD:
            lo = max(0, OLEAD - t * TT)
            P.dma("sync",
                  (lambda t=t, lo=lo: lambda e: e.dma_start(
                      out=C.out1.ap().rearrange("(c p) w -> p c w", p=128)
                      [:, :, t * TT - OLEAD + lo: (t + 1) * TT - OLEAD],
                      in_=C.oo1[:, :].rearrange("p (c w) -> p c w", c=NFC)[:, :, lo:TT]))(),
                  reads=[C.oo1_B],
                  writes=region(C.out1_b, (t * TT - OLEAD + lo) // CH,
                                ((t + 1) * TT - OLEAD) // CH),
                  tag="ot_o1s")
        P.op("scalar", lambda e: e.activation(C.oo1b[:, :], C.oo1[:, :], AF.Copy),
             reads=[C.oo1_B], writes=[C.oo1b_B], tag="ot_x0c")
        plain_ln(C, C.oo1b, C.oo1b_B, C.oo1, C.oo1_B, TT, "ot_l3")
        P.dma("sync",
              (lambda t=t: lambda e: e.dma_start(
                  out=C.xA.ap().rearrange("(c p) w -> p c w", p=128)[:, :, t * TT:(t + 1) * TT],
                  in_=C.oo1[:, :].rearrange("p (c w) -> p c w", c=NFC)))(),
              reads=[C.oo1_B],
              writes=region(C.xA_b, t * TT // CH, (t + 1) * TT // CH),
              tag="ot_x0s")


def _wqkv(C, third, sub, kc):
    o = kc * 3072 + third * 1024 + sub * 128
    return C.Wqkv_sb[:, o:o + 128]


def _xs_slice(C, tile, fc, ncol=CH):
    off = fc * (CH + 1) + (1 if fc < 4 else 0)
    return tile[:, off:off + ncol]


def _ld_layer_params(C, ell):
    P = C.P
    P.op("vector", lambda e: e.memset(C.LQS_sb[:, :], 0.0), writes=[C.LQS_B], tag=f"L{ell}_pz")
    P.op("vector", lambda e: e.memset(C.LKS_sb[:, :], 0.0), writes=[C.LKS_B], tag=f"L{ell}_pz")
    P.dma("gpsimd", lambda e: e.dma_start(out=C.LQS_sb[0:1, 0:64],
                                          in_=C.qs6.ap()[ell:ell + 1, :]),
          writes=[C.LQS_B], tag=f"L{ell}_pq")
    P.dma("gpsimd", lambda e: e.dma_start(out=C.LQS_sb[1:2, 64:128],
                                          in_=C.qs6.ap()[ell:ell + 1, :]),
          writes=[C.LQS_B], tag=f"L{ell}_pq")
    P.dma("gpsimd", lambda e: e.dma_start(out=C.LKS_sb[0:1, 0:64],
                                          in_=C.ks6.ap()[ell:ell + 1, :]),
          writes=[C.LKS_B], tag=f"L{ell}_pk")
    P.dma("gpsimd", lambda e: e.dma_start(out=C.LKS_sb[1:2, 64:128],
                                          in_=C.ks6.ap()[ell:ell + 1, :]),
          writes=[C.LKS_B], tag=f"L{ell}_pk")
    P.dma("gpsimd", lambda e: e.dma_start(out=C.Lgb_sb[0:1, :],
                                          in_=C.lng.ap()[ell:ell + 1, :]),
          writes=[C.Lgb_B], tag=f"L{ell}_pg")
    P.dma("gpsimd", lambda e: e.dma_start(out=C.Lgb_sb[1:2, :],
                                          in_=C.lnb.ap()[ell:ell + 1, :]),
          writes=[C.Lgb_B], tag=f"L{ell}_pg")
    P.dma("sync", lambda e: e.dma_start(
        out=C.gcol_sb[:, :],
        in_=C.lng.ap()[ell].rearrange("(f p) -> p f", p=128),
        allow_slow_non_contiguous=True),
        writes=[C.gcol_B], tag=f"L{ell}_pg2")
    P.dma("sync", lambda e: e.dma_start(
        out=C.b1a_sb[:, 0:21],
        in_=C.b1.ap()[ell, 0:2688].rearrange("(og p) -> p og", p=128),
        allow_slow_non_contiguous=True),
        writes=[C.b1a_B], tag=f"L{ell}_pb")
    P.dma("sync", lambda e: e.dma_start(
        out=C.b1a_sb[0:42, 21:22],
        in_=C.b1.ap()[ell, 2688:2730].rearrange("(p o) -> p o", o=1)),
        writes=[C.b1a_B], tag=f"L{ell}_pb")
    P.dma("sync", lambda e: e.dma_start(
        out=C.b1g_sb[:, 0:21],
        in_=C.b1.ap()[ell, 2730:5418].rearrange("(og p) -> p og", p=128),
        allow_slow_non_contiguous=True),
        writes=[C.b1g_B], tag=f"L{ell}_pb")
    P.dma("sync", lambda e: e.dma_start(
        out=C.b1g_sb[0:42, 21:22],
        in_=C.b1.ap()[ell, 5418:5460].rearrange("(p o) -> p o", o=1)),
        writes=[C.b1g_B], tag=f"L{ell}_pb")
    P.dma("sync", lambda e: e.dma_start(
        out=C.b2c_sb[:, :],
        in_=C.b2.ap()[ell].rearrange("(f p) -> p f", p=128),
        allow_slow_non_contiguous=True),
        writes=[C.b2c_B], tag=f"L{ell}_pb2")


def _rope_inner(C, ell, bc, which, fc, pb_src, tag):
    P = C.P
    par = bc % 2
    LSB, LSB_B = (C.LQS_sb, C.LQS_B) if which == "q" else (C.LKS_sb, C.LKS_B)
    dst, dst_B = (C.Qb, C.Qb_B) if which == "q" else (C.Kc, C.Kc_B)
    toff = 0 if which == "q" else 2 * CH
    P.op("scalar",
         lambda e: e.activation(C.tq[0][:, :], C.PS[pb_src][:, 0:CH], AF.Copy),
         reads=[C.PB[pb_src]], writes=[C.tq_B[0]], tag=tag + "ev")
    P.op("scalar",
         lambda e: e.activation(C.Pt[:, 0:CH], C.tq[0][:, :], AF.Square),
         reads=[C.tq_B[0]], writes=[C.Pt_B], tag=tag + "sq")
    P.op("tensor",
         lambda e: e.matmul(C.PS[2][0:2, 0:CH], C.E2_sb[:, :], C.Pt[:, 0:CH],
                            start=True, stop=True),
         reads=[C.Pt_B, C.cb], writes=[C.PB[2]], tag=tag + "ss")
    P.op("scalar",
         lambda e: e.activation(C.lnt[0:2, 0:CH], C.PS[2][0:2, 0:CH], AF.Ln,
                                bias=ccol(C, C_EPS24)[0:2, :]),
         reads=[C.PB[2], C.const_B], writes=[C.lnt_B], tag=tag + "ln")
    P.op("scalar",
         lambda e: e.activation(C.rr[0:2, 0:CH], C.lnt[0:2, 0:CH], AF.Exp,
                                scale=ccol(C, C_MHALF)[0:2, :]),
         reads=[C.lnt_B, C.const_B], writes=[C.rr_B], tag=tag + "rs")
    P.op("tensor",
         lambda e: e.matmul(C.PS[3][:, 0:CH], LSB[:, :], C.rr[0:2, 0:CH],
                            start=True, stop=True),
         reads=[C.rr_B, LSB_B], writes=[C.PB[3]], tag=tag + "bc")
    P.op("vector",
         lambda e: e.tensor_mul(C.tb[0][:, :], C.tq[0][:, :], C.PS[3][:, 0:CH]),
         reads=[C.tq_B[0], C.PB[3]], writes=[C.tb_B[0]], tag=tag + "qn")
    P.op("tensor",
         lambda e: e.matmul(C.PS[2][:, 0:CH], C.R2_sb[:, :], C.tb[0][:, :],
                            start=True, stop=True),
         reads=[C.tb_B[0], C.cb], writes=[C.PB[2]], tag=tag + "rq")
    P.op("gpsimd",
         lambda e: e.tensor_mul(C.tb[1][:, :], C.tb[0][:, :],
                                C.TIt[:, toff:toff + CH]),
         reads=[C.tb_B[0], C.TIt_B], writes=[C.tb_B[1]], tag=tag + "t1")
    P.op("vector",
         lambda e: e.tensor_mul(C.tb[2][:, :], C.PS[2][:, 0:CH],
                                C.TIt[:, toff + CH:toff + 2 * CH]),
         reads=[C.PB[2], C.TIt_B], writes=[C.tb_B[2]], tag=tag + "t2")
    P.op("gpsimd",
         (lambda fc=fc: lambda e: e.tensor_add(
             dst[:, fc * CH:(fc + 1) * CH], C.tb[1][:, :], C.tb[2][:, :]))(),
         reads=[C.tb_B[1], C.tb_B[2]], writes=[dst_B], tag=tag + "o")
    if which == "k":
        P.op("gpsimd",
             (lambda fc=fc, par=par: lambda e: e.tensor_scalar_mul(
                 C.Kp[par][:, fc * CH:(fc + 1) * CH],
                 C.Kc[:, fc * CH:(fc + 1) * CH], C.BASE2_sb[:, 0:1]))(),
             reads=[dst_B, C.BASE2_B], writes=[C.Kp_B[par]], tag=tag + "kp")


def build_attn_layer(C, ell):
    P = C.P
    xsrc, xsrc_b = C.xA, C.xA_b
    xdst, xdst_b = C.xB, C.xB_b
    AQ0 = 15 - W_LEAD[ell]
    _ld_layer_params(C, ell)
    P.dma("sync",
          lambda e: e.dma_start(
              out=C.Wqkv_sb[:, :].rearrange("p (k c) -> p k c", k=NFC),
              in_=C.wqkv_bf.ap()[ell * DM:(ell + 1) * DM, :].rearrange(
                  "(k p) c -> p k c", p=128)),
          reads=[C.wbf_b], writes=[C.Wqkv_B], tag=f"aL{ell}_w")
    P.dma("sync", lambda e: e.dma_start(out=C.MSK_sb[:, :], in_=C.MSK.ap()),
          writes=[C.MSK_B], tag=f"aL{ell}_m")
    for bc in range(AQ0 - 1, NXC):
        attn_chunk(C, ell, bc, window=(bc >= AQ0))


def attn_chunk(C, ell, bc, window):
    P = C.P
    par = bc % 2
    xsrc, xsrc_b = C.xA, C.xA_b
    xdst, xdst_b = C.xB, C.xB_b
    tg = f"aL{ell}c{bc}_"
    P.dma("gpsimd",
          (lambda bc=bc: lambda e: e.dma_start(
              out=C.xbf[:, :].rearrange("p (c w) -> p c w", c=NFC),
              in_=xsrc.ap().rearrange("(c p) w -> p c w", p=128)
              [:, :, bc * CH - 1: (bc + 1) * CH]))(),
          reads=region(xsrc_b, bc - 1, bc + 1), writes=[C.xbf_B], tag=tg + "x")
    P.dma("sync",
          (lambda bc=bc: lambda e: e.dma_start(
              out=C.TIt[:, :], in_=C.TI.ap()[:, bc * 4 * CH:(bc + 1) * 4 * CH]))(),
          writes=[C.TIt_B], tag=tg + "tt")
    for og in range(24):
        third, sub = og // 8, og % 8
        pb = og % 2
        for kc in range(NFC):
            P.op("tensor",
                 (lambda third=third, sub=sub, kc=kc, pb=pb: lambda e: e.matmul(
                     C.PS[pb][:, 0:CH], _wqkv(C, third, sub, kc),
                     _xs_slice(C, C.xbf, kc),
                     start=(kc == 0), stop=(kc == NFC - 1)))(),
                 reads=[C.xbf_B, C.Wqkv_B], writes=[C.PB[pb]], tag=tg + "mm")
        if third == 0:
            _rope_inner(C, ell, bc, "q", sub, pb, tg + f"rq{sub}")
        elif third == 1:
            _rope_inner(C, ell, bc, "k", sub, pb, tg + f"rk{sub}")
        else:
            fc = sub
            P.op("scalar",
                 (lambda pb=pb: lambda e: e.activation(
                     C.Pexp[:, 0:CH], C.PS[pb][:, 0:CH], AF.Copy))(),
                 reads=[C.PB[pb]], writes=[C.Pexp_B], tag=tg + "vev")
            for kb in range(2):
                pbt = 6 + kb
                P.op("tensor",
                     (lambda kb=kb, pbt=pbt: lambda e: e.transpose(
                         C.PS[pbt].ap().bitcast(BF16)[:, 0:128],
                         C.Pexp[:, kb * 128:(kb + 1) * 128], C.ID_sb[:, :]))(),
                     reads=[C.Pexp_B, C.cb], writes=[C.PB[pbt]], tag=tg + "vT")
                P.op("vector",
                     (lambda fc=fc, kb=kb, par=par, pbt=pbt: lambda e: e.tensor_copy(
                         C.Vkm[par][:, :].rearrange("p (en d) -> p en d", d=DH)
                         [:, kb * NH + 2 * fc: kb * NH + 2 * fc + 2, :],
                         C.PS[pbt].ap().bitcast(BF16)[:, 0:128].rearrange(
                             "p (en d) -> p en d", d=DH)))(),
                     reads=[C.PB[pbt]], writes=[C.Vkm_B[par]], tag=tg + "vTe")
    if window:
        first = (bc == LEADC)
        for h in range(NH):
            attn_window_head(C, ell, bc, h, first, tg)
    if window:
        attn_wo(C, ell, bc, tg)


def attn_window_head(C, ell, bc, h, first, tg):
    P = C.P
    par = bc % 2
    fc, hp = h // 2, h % 2
    r0 = hp * 64
    for qh in range(2):
        j = h * 2 + qh
        pb_s = 4 + (j % 2)
        qsl = C.Qb[r0:r0 + 64, fc * CH + qh * 128: fc * CH + qh * 128 + 128]
        kp = C.Kp[1 - par]
        if qh == 0:
            P.op("tensor",
                 (lambda pb_s=pb_s, qsl=qsl, kp=kp, r0=r0, fc=fc: lambda e: e.matmul(
                     C.PS[pb_s][:, 0:256], qsl,
                     kp[r0:r0 + 64, fc * CH: fc * CH + 256], start=True, stop=True))(),
                 reads=[C.Qb_B, C.Kp_B[1 - par]], writes=[C.PB[pb_s]], tag=tg + "s1")
            P.op("tensor",
                 (lambda pb_s=pb_s, qsl=qsl, r0=r0, fc=fc: lambda e: e.matmul(
                     C.PS[pb_s][:, 256:384], qsl,
                     C.Kc[r0:r0 + 64, fc * CH: fc * CH + 128], start=True, stop=True))(),
                 reads=[C.Qb_B, C.Kc_B], writes=[C.PB[pb_s]], tag=tg + "s2")
        else:
            P.op("tensor",
                 (lambda pb_s=pb_s, qsl=qsl, kp=kp, r0=r0, fc=fc: lambda e: e.matmul(
                     C.PS[pb_s][:, 0:128], qsl,
                     kp[r0:r0 + 64, fc * CH + 128: fc * CH + 256], start=True, stop=True))(),
                 reads=[C.Qb_B, C.Kp_B[1 - par]], writes=[C.PB[pb_s]], tag=tg + "s1")
            P.op("tensor",
                 (lambda pb_s=pb_s, qsl=qsl, r0=r0, fc=fc: lambda e: e.matmul(
                     C.PS[pb_s][:, 128:384], qsl,
                     C.Kc[r0:r0 + 64, fc * CH: fc * CH + 256], start=True, stop=True))(),
                 reads=[C.Qb_B, C.Kc_B], writes=[C.PB[pb_s]], tag=tg + "s2")
        mblk = (1 + qh) if first else 0
        P.op("vector",
             (lambda pb_s=pb_s, mblk=mblk: lambda e: e.tensor_add(
                 C.Sm[:, :], C.PS[pb_s][:, 0:384], C.MSK_sb[:, mblk * 384:(mblk + 1) * 384]))(),
             reads=[C.PB[pb_s], C.MSK_B], writes=[C.Sm_B], tag=tg + "msk")
        nmc = j % 2
        P.op("vector",
             (lambda nmc=nmc: lambda e: e.tensor_reduce(
                 C.nmax[:, nmc:nmc + 1], C.Sm[:, :], AX.X, ALU.max, negate=True))(),
             reads=[C.Sm_B], writes=[C.nmax_B], tag=tg + "mx")
        P.op("scalar",
             (lambda nmc=nmc, j=j: lambda e: e.activation(
                 C.Pexp[:, :], C.Sm[:, :], AF.Exp, bias=C.nmax[:, nmc:nmc + 1],
                 accum_out=C.sums[:, j:j + 1]))(),
             reads=[C.Sm_B, C.nmax_B], writes=[C.Pexp_B, C.sums_B], tag=tg + "ex")
        P.op("vector",
             (lambda j=j: lambda e: e.reciprocal(C.recipc[:, j:j + 1], C.sums[:, j:j + 1]))(),
             reads=[C.sums_B], writes=[C.recipc_B], tag=tg + "rc")
        P.op("gpsimd",
             (lambda j=j: lambda e: e.tensor_scalar_mul(
                 C.Pexp[:, :], C.Pexp[:, :], C.recipc[:, j:j + 1]))(),
             reads=[C.Pexp_B, C.recipc_B], writes=[C.Pexp_B], tag=tg + "dv")
        for kb in range(3):
            P.op("tensor",
                 (lambda kb=kb: lambda e: e.transpose(
                     C.PS[6].ap().bitcast(BF16)[:, kb * 128:(kb + 1) * 128],
                     C.Pexp[:, kb * 128:(kb + 1) * 128], C.ID_sb[:, :]))(),
                 reads=[C.Pexp_B, C.cb], writes=[C.PB[6]], tag=tg + "pT")
        P.op("vector",
             lambda e: e.tensor_copy(C.Pt[:, :], C.PS[6].ap().bitcast(BF16)[:, 0:384]),
             reads=[C.PB[6]], writes=[C.Pt_B], tag=tg + "pTe")
        blocks = [(1 - par, 0), (1 - par, 1), (par, 0)] if qh == 0 else \
                 [(1 - par, 1), (par, 0), (par, 1)]
        for i3, (vp, kb) in enumerate(blocks):
            P.op("tensor",
                 (lambda i3=i3, vp=vp, kb=kb, r0=r0, h=h: lambda e: e.matmul(
                     C.PS[7][r0:r0 + 64, 0:128],
                     C.Vkm[vp][:, (kb * NH + h) * DH:(kb * NH + h) * DH + DH],
                     C.Pt[:, i3 * 128:(i3 + 1) * 128],
                     start=(i3 == 0), stop=(i3 == 2), tile_position=(0, r0)))(),
                 reads=[C.Vkm_B[vp], C.Pt_B], writes=[C.PB[7]], tag=tg + "pv")
        P.op("scalar",
             (lambda r0=r0, fc=fc, qh=qh: lambda e: e.activation(
                 C.att[r0:r0 + 64, fc * CH + qh * 128: fc * CH + qh * 128 + 128],
                 C.PS[7][r0:r0 + 64, 0:128], AF.Copy))(),
             reads=[C.PB[7]], writes=[C.att_B], tag=tg + "ave")


def attn_wo(C, ell, bc, tg):
    P = C.P
    par = bc % 2
    xsrc, xsrc_b = C.xA, C.xA_b
    xdst, xdst_b = C.xB, C.xB_b
    for i in range(4):
        fr, fr_B = C.afr[i % 2], C.afr_B[i % 2]
        P.dma("sync",
              (lambda i=i, bc=bc, fr=fr: lambda e: e.dma_start(
                  out=fr[:, :].rearrange("p (c w) -> p c w", c=2),
                  in_=xsrc.ap().rearrange("(c p) w -> p c w", p=128)
                  [:, 2 * i:2 * i + 2, bc * CH:(bc + 1) * CH]))(),
              reads=region(xsrc_b, bc, bc + 1), writes=[fr_B], tag=tg + "fr")
        for k in range(2):
            og = 2 * i + k
            pb = og % 2
            wb, wb_B = C.WoB_sb[og % 2], C.WoB_B[og % 2]
            P.dma("sync",
                  (lambda og=og, wb=wb: lambda e: e.dma_start(
                      out=wb[:, :].rearrange("p (k c) -> p k c", k=NFC),
                      in_=C.wo6_bf.ap()[ell * DM:(ell + 1) * DM,
                                        og * 128:(og + 1) * 128].rearrange(
                          "(k p) c -> p k c", p=128)))(),
                  reads=[C.wbf_b], writes=[wb_B], tag=tg + "wol")
            for kc in range(NFC):
                P.op("tensor",
                     (lambda og=og, kc=kc, pb=pb, wb=wb: lambda e: e.matmul(
                         C.PS[pb][:, 0:CH], wb[:, kc * 128:(kc + 1) * 128],
                         C.att[:, kc * CH:(kc + 1) * CH],
                         start=(kc == 0), stop=(kc == NFC - 1)))(),
                     reads=[C.att_B, wb_B], writes=[C.PB[pb]], tag=tg + "wo")
            P.op("vector",
                 (lambda og=og, pb=pb, fr=fr, k=k, i=i: lambda e: e.tensor_add(
                     C.xmo[i % 2][:, k * CH:(k + 1) * CH], C.PS[pb][:, 0:CH],
                     fr[:, k * CH:(k + 1) * CH]))(),
                 reads=[C.PB[pb], fr_B], writes=[C.xmo_B[i % 2]], tag=tg + "xm")
        P.dma("sync",
              (lambda bc=bc, i=i: lambda e: e.dma_start(
                  out=xdst.ap().rearrange("(c p) w -> p c w", p=128)
                  [:, 2 * i:2 * i + 2, bc * CH:(bc + 1) * CH],
                  in_=C.xmo[i % 2][:, :].rearrange("p (c w) -> p c w", c=2)))(),
              reads=[C.xmo_B[i % 2]], writes=region(xdst_b, bc, bc + 1), tag=tg + "xs")


def build_ffn_layer(C, ell):
    P = C.P
    AQ0 = 15 - W_LEAD[ell]
    FQ0 = AQ0 + 1
    P.dma("sync",
          lambda e: e.dma_start(
              out=C.W1_sb[:, :].rearrange("p (k c) -> p k c", k=NFC),
              in_=C.w1_bf.ap()[ell * DM:(ell + 1) * DM, :].rearrange(
                  "(k p) c -> p k c", p=128)),
          reads=[C.wbf_b], writes=[C.W1_B], tag=f"fL{ell}_w")
    for bc in range(FQ0, NXC):
        ffn_chunk(C, ell, bc)


def ffn_chunk(C, ell, bc):
    P = C.P
    xsrc, xsrc_b = C.xB, C.xB_b
    xdst, xdst_b = C.xA, C.xA_b
    tg = f"fL{ell}c{bc}_"
    P.dma("gpsimd",
          (lambda bc=bc: lambda e: e.dma_start(
              out=C.xsb[:, :].rearrange("p (c w) -> p c w", c=NFC),
              in_=xsrc.ap().rearrange("(c p) w -> p c w", p=128)
              [:, :, bc * CH - 1:(bc + 1) * CH]))(),
          reads=region(xsrc_b, bc - 1, bc + 1), writes=[C.xsb_B], tag=tg + "x")
    for fc in range(NFC):
        P.op("scalar",
             (lambda fc=fc: lambda e: e.activation(
                 C.sqb[:, fc * CH:(fc + 1) * CH], _xs_slice(C, C.xsb, fc), AF.Square))(),
             reads=[C.xsb_B], writes=[C.sqb_B], tag=tg + "sq")
        P.op("tensor",
             (lambda fc=fc: lambda e: e.matmul(
                 C.PS[0][0:1, 0:CH], C.ONES_sb[:, 0:1], _xs_slice(C, C.xsb, fc),
                 start=(fc == 0), stop=(fc == NFC - 1)))(),
             reads=[C.xsb_B, C.cb], writes=[C.PB[0]], tag=tg + "st1")
        P.op("tensor",
             (lambda fc=fc: lambda e: e.matmul(
                 C.PS[1][0:1, 0:CH], C.ONES_sb[:, 0:1], C.sqb[:, fc * CH:(fc + 1) * CH],
                 start=(fc == 0), stop=(fc == NFC - 1)))(),
             reads=[C.sqb_B, C.cb], writes=[C.PB[1]], tag=tg + "st2")
    ln_rows_finish(C, CH, 0, 1, None, None, C.rstdb, C.rstdb_B, C.rowp, C.rowp_B, tg + "lr")
    P.op("tensor",
         lambda e: e.matmul(C.PS[2][:, 0:CH], C.ONErow_sb[0:1, :], C.rstdb[0:1, 0:CH],
                            start=True, stop=True),
         reads=[C.rstdb_B, C.cb], writes=[C.PB[2]], tag=tg + "A")
    for fc in range(NFC):
        pbc = 3 + fc % 2
        P.op("tensor",
             (lambda fc=fc, pbc=pbc: lambda e: e.matmul(
                 C.PS[pbc][:, 0:CH], C.Lgb_sb[:, fc * 128:(fc + 1) * 128],
                 C.rowp[0:2, 0:CH], start=True, stop=True))(),
             reads=[C.rowp_B, C.Lgb_B], writes=[C.PB[pbc]], tag=tg + "C")
        P.op("vector",
             (lambda fc=fc: lambda e: e.scalar_tensor_tensor(
                 C.h2i[0][:, :], _xs_slice(C, C.xsb, fc), C.gcol_sb[:, fc:fc + 1],
                 C.PS[2][:, 0:CH], ALU.mult, ALU.mult))(),
             reads=[C.xsb_B, C.gcol_B, C.PB[2]], writes=[C.h2i_B[0]], tag=tg + "n1")
        P.op("vector",
             (lambda fc=fc, pbc=pbc: lambda e: e.tensor_sub(
                 C.xn[:, fc * CH:(fc + 1) * CH], C.h2i[0][:, :], C.PS[pbc][:, 0:CH]))(),
             reads=[C.h2i_B[0], C.PB[pbc]], writes=[C.xn_B], tag=tg + "n2")
    for og in range(NOG1):
        m = 128 if og < NOG1 - 1 else 42
        pa, pg = 5, 6
        for kc in range(NFC):
            P.op("tensor",
                 (lambda og=og, kc=kc, m=m: lambda e: e.matmul(
                     C.PS[5][0:m, 0:CH],
                     C.W1_sb[:, kc * DFF + og * 128: kc * DFF + og * 128 + m],
                     C.xn[:, kc * CH:(kc + 1) * CH],
                     start=(kc == 0), stop=(kc == NFC - 1)))(),
                 reads=[C.xn_B, C.W1_B], writes=[C.PB[5]], tag=tg + "w1a")
            P.op("tensor",
                 (lambda og=og, kc=kc, m=m: lambda e: e.matmul(
                     C.PS[6][0:m, 0:CH],
                     C.W1_sb[:, kc * DFF + DFF2 + og * 128: kc * DFF + DFF2 + og * 128 + m],
                     C.xn[:, kc * CH:(kc + 1) * CH],
                     start=(kc == 0), stop=(kc == NFC - 1)))(),
                 reads=[C.xn_B, C.W1_B], writes=[C.PB[6]], tag=tg + "w1g")
        gpar = og % 2
        P.op("scalar",
             (lambda og=og, m=m, gpar=gpar: lambda e: e.activation(
                 C.gact[gpar][0:m, :], C.PS[6][0:m, 0:CH], AF.Gelu,
                 bias=C.b1g_sb[0:m, og:og + 1]))(),
             reads=[C.PB[6], C.b1g_B], writes=[C.gact_B[gpar]], tag=tg + "gl")
        P.op("vector",
             (lambda og=og, m=m, gpar=gpar: lambda e: e.scalar_tensor_tensor(
                 C.h2[0:m, og * CH:(og + 1) * CH], C.PS[5][0:m, 0:CH],
                 C.b1a_sb[0:m, og:og + 1],
                 C.gact[gpar][0:m, :], ALU.add, ALU.mult))(),
             reads=[C.PB[5], C.b1a_B, C.gact_B[gpar]], writes=[C.h2_B], tag=tg + "h2")
    for og in range(NOG1):
        m = 128 if og < NOG1 - 1 else 42
        wb, wb_B = C.W2_sb[og % 2], C.W2_B[og % 2]
        P.dma("sync",
              (lambda og=og, m=m, wb=wb: lambda e: e.dma_start(
                  out=wb[0:m, :],
                  in_=C.w2_bf.ap()[ell * DFF2 + og * 128: ell * DFF2 + og * 128 + m, :]))(),
              reads=[C.wbf_b], writes=[wb_B], tag=tg + "w2l")
        for og2 in range(NFC):
            P.op("tensor",
                 (lambda og=og, og2=og2, m=m, wb=wb: lambda e: e.matmul(
                      C.PS[og2][:, 0:CH],
                      wb[0:m, og2 * 128:(og2 + 1) * 128], C.h2[0:m, og * CH:(og + 1) * CH],
                      start=(og == 0), stop=(og == NOG1 - 1)))(),
                 reads=[C.h2_B, wb_B], writes=[C.PB[og2]], tag=tg + "w2")
    for i in range(4):
        fr, fr_B = C.fres[i % 2], C.fres_B[i % 2]
        P.dma("sync",
              (lambda i=i, bc=bc, fr=fr: lambda e: e.dma_start(
                  out=fr[:, :].rearrange("p (c w) -> p c w", c=2),
                  in_=xsrc.ap().rearrange("(c p) w -> p c w", p=128)
                  [:, 2 * i:2 * i + 2, bc * CH:(bc + 1) * CH]))(),
              reads=region(xsrc_b, bc, bc + 1), writes=[fr_B], tag=tg + "fr")
        for k in range(2):
            og2 = 2 * i + k
            P.op("vector",
                 (lambda og2=og2, fr=fr, k=k: lambda e:
                  e.scalar_tensor_tensor(
                      C.xnext[:, og2 * CH:(og2 + 1) * CH],
                      C.PS[og2][:, 0:CH],
                      C.b2c_sb[:, og2:og2 + 1], fr[:, k * CH:(k + 1) * CH],
                      ALU.add, ALU.add))(),
                 reads=[C.PB[og2], C.b2c_B, fr_B], writes=[C.xnext_B], tag=tg + "xo")
    P.dma("sync",
          (lambda bc=bc: lambda e: e.dma_start(
              out=xdst.ap().rearrange("(c p) w -> p c w", p=128)[:, :, bc * CH:(bc + 1) * CH],
              in_=C.xnext[:, :].rearrange("p (c w) -> p c w", c=NFC)))(),
          reads=[C.xnext_B], writes=region(xdst_b, bc, bc + 1), tag=tg + "xs")


def build_final(C):
    P = C.P
    TT = 512
    for t in range(SL // TT):
        col0 = OLEAD + t * TT
        xb, xb_B = C.oxt, C.oxt_B
        P.dma("gpsimd",
              (lambda col0=col0, xb=xb: lambda e: e.dma_start(
                  out=xb[:, :].rearrange("p (c w) -> p c w", c=NFC),
                  in_=C.xA.ap().rearrange("(c p) w -> p c w", p=128)[:, :, col0:col0 + TT]))(),
              reads=region(C.xA_b, col0 // CH, (col0 + TT) // CH),
              writes=[xb_B], tag="fi_x")
        plain_ln(C, xb, xb_B, C.oo1, C.oo1_B, TT, "fi_ln")
        for fc in range(NFC):
            fr, fr_B = C.ofr[fc % 2], C.ofr_B[fc % 2]
            P.dma("sync",
                  (lambda t=t, fc=fc, fr=fr: lambda e: e.dma_start(
                      out=fr[:, :],
                      in_=C.out1.ap()[fc * 128:(fc + 1) * 128, t * TT:(t + 1) * TT]))(),
                  reads=region(C.out1_b, t * TT // CH, (t + 1) * TT // CH),
                  writes=[fr_B], tag="fi_fr")
            P.op("gpsimd",
                 (lambda fc=fc, fr=fr: lambda e: e.tensor_add(
                     C.oo1[:, fc * TT:(fc + 1) * TT], C.oo1[:, fc * TT:(fc + 1) * TT],
                     fr[:, :]))(),
                 reads=[C.oo1_B, fr_B], writes=[C.oo1_B], tag="fi_c")
        P.dma("sync",
              (lambda t=t: lambda e: e.dma_start(
                  out=C.y.ap().rearrange("(c p) w -> p c w", p=128)[:, :, t * TT:(t + 1) * TT],
                  in_=C.oo1[:, :].rearrange("p (c w) -> p c w", c=NFC)))(),
              reads=[C.oo1_B], writes=[C.y_b], tag="fi_ys")


def _host_shared():
    sh = {}
    i = np.arange(128)[:, None]
    j = np.arange(384)[None, :]
    band = (i <= j) & (j <= i + 256)
    m_norm = np.where(band, 0.0, -1e30).astype(np.float32)
    m_f0 = np.where(band & (j >= 256), 0.0, -1e30).astype(np.float32)
    m_f1 = np.where(band & (j >= 128), 0.0, -1e30).astype(np.float32)
    sh["m_norm"], sh["m_f0"], sh["m_f1"] = m_norm, m_f0, m_f1
    jj = np.arange(128)[:, None]
    ii = np.arange(128)[None, :]
    sh["UT"] = (jj <= ii).astype(np.float32).astype(BF)
    R = np.zeros((128, 128), np.float32)
    for h0 in (0, 64):
        for r in range(32):
            R[h0 + r + 32, h0 + r] = -1.0
            R[h0 + r, h0 + r + 32] = 1.0
    sh["R2M"] = R.astype(BF)
    RO = np.zeros((128, 128), np.float32)
    for h0 in (0, 64):
        for p in range(32):
            RO[h0 + 2 * p + 1, h0 + 2 * p] = -1.0
            RO[h0 + 2 * p, h0 + 2 * p + 1] = 1.0
    sh["ROM"] = RO.astype(BF)
    sh["IDEN"] = np.eye(128, dtype=BF)
    E2 = np.zeros((128, 2), np.float32)
    E2[0:64, 0] = 1.0
    E2[64:128, 1] = 1.0
    sh["E2"] = E2.astype(BF)
    consts = np.zeros(16, np.float32)
    consts[C_EPS24] = 1e-24
    consts[C_EPS5] = 1e-5
    consts[C_EPS6] = 1e-6
    consts[C_I1024] = 1.0 / 1024.0
    consts[C_MHALF] = -0.5
    consts[C_M1] = -1.0
    sh["CONSTS"] = consts
    return sh


def _rope_outer_tables(pos):
    j = np.arange(32)
    invf = (10000.0 ** (-2 * j / 64.0)).astype(np.float64)
    ang = pos[None, :].astype(np.float64) * invf[:, None]
    cos = np.cos(ang)
    sin = np.sin(ang)
    cos64 = np.repeat(cos, 2, axis=0)
    sin64 = np.repeat(sin, 2, axis=0)
    C128 = np.concatenate([cos64, cos64], axis=0).astype(np.float32)
    S128 = np.concatenate([sin64, sin64], axis=0).astype(np.float32)
    return C128, S128


def _rope_inner_tables(pos):
    p = np.maximum(pos, 0).astype(np.float64)
    i = np.arange(32)
    invf = 10000.0 ** (-2 * i / 64.0)
    base = (2 * i + 0.4 * 64) / (1.4 * 64)
    ang = p[None, :] * invf[:, None]
    cos = np.cos(ang)
    sin = np.sin(ang)
    tm = np.mod(p, 256.0)
    fq = base[:, None] ** ((tm[None, :] - 128.0) / 128.0)
    fk = base[:, None] ** ((128.0 - tm[None, :]) / 128.0)
    def dup(a):
        a64 = np.concatenate([a, a], axis=0)
        return np.concatenate([a64, a64], axis=0)
    Tqc = (dup(cos * fq) * QK_SCALE).astype(np.float32)
    Tqs = (dup(sin * fq) * QK_SCALE).astype(np.float32)
    Tkc = dup(cos * fk).astype(np.float32)
    Tks = dup(sin * fk).astype(np.float32)
    return Tqc, Tqs, Tkc, Tks


def _fm(x2d):
    return np.ascontiguousarray(x2d.T, dtype=np.float32)


def _core_inputs(core, inputs, sh):
    b, k = core // 4, core % 4
    s = k * SL
    x = np.asarray(inputs["L0_attn_x"])[b]
    lo = s - OLEAD - PREF
    hi = s - OLEAD
    xp = np.zeros((PREF, DM), np.float32)
    vlo = max(lo, 0)
    if hi > 0:
        xp[vlo - lo: hi - lo] = x[vlo:hi]
    xo = np.zeros((XCOLS, DM), np.float32)
    olo = s - OLEAD
    vlo = max(olo, 0)
    xo[vlo - olo:] = x[vlo: s + SL]
    d = {
        "xpref": _fm(xp),
        "xout": _fm(xo),
        "wq": np.asarray(inputs["L0_attn_wq"], np.float32),
        "wk": np.asarray(inputs["L0_attn_wk"], np.float32),
        "wv": np.asarray(inputs["L0_attn_wv"], np.float32),
        "wo": np.asarray(inputs["L0_attn_wo"], np.float32),
        "Wqkv": np.asarray(inputs["L0_ffn_Wqkv"], np.float32).reshape(DEPTH * DM, 3 * DM),
        "Wo6": np.asarray(inputs["L0_ffn_Wo"], np.float32).reshape(DEPTH * DM, DM),
        "W1": np.asarray(inputs["L0_ffn_W1"], np.float32).reshape(DEPTH * DM, DFF),
        "W2": np.asarray(inputs["L0_ffn_W2"], np.float32).reshape(DEPTH * DFF2, DM),
        "qs6": np.asarray(inputs["L0_ffn_q_scale"], np.float32),
        "ks6": np.asarray(inputs["L0_ffn_k_scale"], np.float32),
        "lng": np.asarray(inputs["L0_ffn_ln_g"], np.float32),
        "lnb": np.asarray(inputs["L0_ffn_ln_b"], np.float32),
        "b1": np.asarray(inputs["L0_ffn_b1"], np.float32),
        "b2": np.asarray(inputs["L0_ffn_b2"], np.float32),
    }
    ppos = np.arange(lo, hi)
    opos = np.arange(olo, s + SL)
    TPc, TPs = _rope_outer_tables(np.maximum(ppos, 0))
    TP = np.empty((128, 2 * PREF), np.float32)
    for t in range(PREF // 512):
        TP[:, t * 1024: t * 1024 + 512] = TPc[:, t * 512:(t + 1) * 512]
        TP[:, t * 1024 + 512: (t + 1) * 1024] = TPs[:, t * 512:(t + 1) * 512]
    TOc, TOs = _rope_outer_tables(np.maximum(opos, 0))
    TO = np.empty((128, 2 * XCOLS), np.float32)
    for t in range(XCOLS // 512):
        TO[:, t * 1024: t * 1024 + 512] = TOc[:, t * 512:(t + 1) * 512]
        TO[:, t * 1024 + 512:(t + 1) * 1024] = TOs[:, t * 512:(t + 1) * 512]
    d["TP"], d["TO"] = TP, TO
    Tqc, Tqs, Tkc, Tks = _rope_inner_tables(opos)
    TI = np.empty((128, 4 * XCOLS), BF)
    for c in range(NXC):
        base = c * 4 * CH
        TI[:, base:base + CH] = Tqc[:, c * CH:(c + 1) * CH].astype(BF)
        TI[:, base + CH:base + 2 * CH] = Tqs[:, c * CH:(c + 1) * CH].astype(BF)
        TI[:, base + 2 * CH:base + 3 * CH] = Tkc[:, c * CH:(c + 1) * CH].astype(BF)
        TI[:, base + 3 * CH:base + 4 * CH] = Tks[:, c * CH:(c + 1) * CH].astype(BF)
    d["TI"] = TI
    mf0 = sh["m_f0"] if k == 0 else sh["m_norm"]
    mf1 = sh["m_f1"] if k == 0 else sh["m_norm"]
    d["MSK"] = np.concatenate(
        [sh["m_norm"].astype(BF), mf0.astype(BF), mf1.astype(BF)], axis=1)
    d["UT"] = sh["UT"]; d["R2M"] = sh["R2M"]; d["ROM"] = sh["ROM"]
    d["IDEN"] = sh["IDEN"]; d["E2"] = sh["E2"]; d["CONSTS"] = sh["CONSTS"]
    d["ONED"] = np.ones((64, 1), np.float32)
    i6 = np.arange(128) % 64
    base = (2 * (i6 % 32) + 0.4 * 64) / (1.4 * 64)
    d["BASE2D"] = (base ** 2).astype(np.float32).reshape(128, 1)
    return d


_NC_CACHE = {}


def kernel(**inputs):
    key = "main"
    if key not in _NC_CACHE:
        _NC_CACHE[key] = build_nc()
    nc, C = _NC_CACHE[key]
    sh = _host_shared()
    in_maps = [_core_inputs(c, inputs, sh) for c in range(NCORES)]
    res = run_bass_kernel_spmd(nc, in_maps, core_ids=list(range(NCORES)))
    out = np.empty((B, L, DM), np.float32)
    for c in range(NCORES):
        b, k = c // 4, c % 4
        out[b, k * SL:(k + 1) * SL] = np.asarray(res.results[c]["y"]).T
    return out


# version 06 (final)
# speedup vs baseline: 1.2810x; 1.2810x over previous
import math

import numpy as np
import ml_dtypes

import concourse.bass as bass
import concourse.mybir as mybir
from concourse.bass_utils import run_bass_kernel_spmd

F32 = mybir.dt.float32
BF16 = mybir.dt.bfloat16
AF = mybir.ActivationFunctionType
ALU = mybir.AluOpType
AX = mybir.AxisListType
BF = ml_dtypes.bfloat16

B, L, DM = 2, 16384, 1024
NCORES = 8
SL = 4096
CH = 256
SCH = 128
NFC = 8
NH, DH = 16, 64
OLEAD = 3584
XCOLS = OLEAD + SL
PREF = 8704
LEADC = OLEAD // CH
NXC = XCOLS // CH
W_LEAD = [12, 10, 8, 6, 4, 2, 0]
DEPTH = 6
DFF2 = 2730
DFF = 5460
NOG1 = 22
NKC2 = 22
QK_SCALE = 8.0

ENGINES = ("sync", "tensor", "vector", "scalar", "gpsimd")
QUEUE_OF = {"sync": "dsync", "gpsimd": "dpool"}


class Buf:
    __slots__ = ("name", "writer", "readers", "aliases")

    def __init__(self, name):
        self.name = name
        self.writer = None
        self.readers = []
        self.aliases = []

    def read_deps(self):
        return {self.writer} if self.writer is not None else set()

    def write_deps(self):
        d = set(self.readers)
        if self.writer is not None:
            d.add(self.writer)
        for a in self.aliases:
            if a.writer is not None:
                d.add(a.writer)
            d.update(a.readers)
        return d


NSLOT = 12


class Prog:
    def __init__(self):
        self.items = []
        self.seq = {e: 0 for e in ENGINES}
        self.dcount = {"dsync": 0, "dpool": 0}

    def op(self, eng, emit, reads=(), writes=(), tag=""):
        deps = set()
        for b in reads:
            deps |= b.read_deps()
        for b in writes:
            deps |= b.write_deps()
        self.seq[eng] += 1
        tok = (eng, self.seq[eng])
        self.items.append(
            {"eng": eng, "emit": emit, "deps": deps, "tok": tok, "dma": None, "tag": tag}
        )
        for b in reads:
            b.readers.append(tok)
        for b in writes:
            b.writer = tok
            b.readers = []
            for a in b.aliases:
                a.writer = tok
                a.readers = []
        return tok

    def dma(self, eng, emit, reads=(), writes=(), tag=""):
        q = QUEUE_OF[eng]
        deps = set()
        for b in reads:
            deps |= b.read_deps()
        for b in writes:
            deps |= b.write_deps()
        self.seq[eng] += 1
        self.dcount[q] += 1
        j = self.dcount[q] - 1
        slot = j % NSLOT
        gen = j // NSLOT
        tok = (f"{q}#{slot}", gen + 1)
        self.items.append(
            {"eng": eng, "emit": emit, "deps": deps, "tok": tok, "dma": tok[0],
             "gate": (tok[0], gen), "tag": tag}
        )
        for b in reads:
            b.readers.append(tok)
        for b in writes:
            b.writer = tok
            b.readers = []
            for a in b.aliases:
                a.writer = tok
                a.readers = []
        return tok

    def emit(self, nc):
        marked = {e: set() for e in ENGINES}
        for it in self.items:
            for (src, s) in it["deps"]:
                if src in self.seq and src != it["eng"]:
                    marked[src].add(s)
        msorted = {e: sorted(marked[e]) for e in ENGINES}
        import bisect

        def rank(e, s):
            return bisect.bisect_right(msorted[e], s)

        esem = {e: nc.alloc_semaphore(f"se_{e}") for e in ENGINES}
        qsem = {}
        qfinal = {}
        for q, n in self.dcount.items():
            for slot in range(min(n, NSLOT)):
                key = f"{q}#{slot}"
                qsem[key] = nc.alloc_semaphore(f"sq_{q}_{slot}")
                qfinal[key] = (n - slot + NSLOT - 1) // NSLOT

        per_eng = {e: [] for e in ENGINES}
        for it in self.items:
            per_eng[it["eng"]].append(it)

        def body(engobj, ename):
            waited = {}
            for it in per_eng[ename]:
                wants = {}
                deps = set(it["deps"])
                if it["dma"] is not None and it["gate"][1] > 0:
                    deps.add(it["gate"])
                for (src, s) in deps:
                    if src in self.seq:
                        if src == ename:
                            continue
                        sem, val = esem[src], rank(src, s)
                    else:
                        sem, val = qsem[src], 16 * s
                    k = sem.num if hasattr(sem, "num") else id(sem)
                    if val > wants.get(k, (None, -1))[1]:
                        wants[k] = (sem, val)
                for sem, val in wants.values():
                    k = sem.num if hasattr(sem, "num") else id(sem)
                    if val > waited.get(k, -1):
                        engobj.wait_ge(sem, val)
                        waited[k] = val
                inst = it["emit"](engobj)
                tok = it["tok"]
                if it["dma"] is not None:
                    inst.then_inc(qsem[it["dma"]], 16)
                elif tok[1] in marked[ename]:
                    inst.then_inc(esem[ename], 1)

        with nc.Block() as block:
            @block.sync
            def _(eng):
                body(eng, "sync")
                for key, gens in qfinal.items():
                    if gens:
                        eng.wait_ge(qsem[key], 16 * gens)

            @block.tensor
            def _(eng):
                body(eng, "tensor")

            @block.vector
            def _(eng):
                body(eng, "vector")

            @block.scalar
            def _(eng):
                body(eng, "scalar")

            @block.gpsimd
            def _(eng):
                body(eng, "gpsimd")


def region(bufs, c0, c1):
    return bufs[c0:c1]


class Ctx:
    pass


def build_nc(debug_taps=False):
    nc = bass.Bass("TRN2", target_bir_lowering=False, detect_race_conditions=False)
    C = Ctx()
    C.nc = nc
    P = Prog()
    C.P = P

    def din(name, shape, dt=F32):
        return nc.dram_tensor(name, list(shape), dt, kind="ExternalInput")

    C.xpref = din("xpref", [DM, PREF])
    C.xout = din("xout", [DM, XCOLS])
    C.wq = din("wq", [DM, DM]); C.wk = din("wk", [DM, DM])
    C.wv = din("wv", [DM, DM]); C.wo = din("wo", [DM, DM])
    C.Wqkv = din("Wqkv", [DEPTH * DM, 3 * DM])
    C.Wo6 = din("Wo6", [DEPTH * DM, DM])
    C.W1 = din("W1", [DEPTH * DM, DFF])
    C.W2 = din("W2", [DEPTH * DFF2, DM])
    C.qs6 = din("qs6", [DEPTH, DH]); C.ks6 = din("ks6", [DEPTH, DH])
    C.lng = din("lng", [DEPTH, DM]); C.lnb = din("lnb", [DEPTH, DM])
    C.b1 = din("b1", [DEPTH, DFF]); C.b2 = din("b2", [DEPTH, DM])
    C.TP = din("TP", [128, 2 * PREF])
    C.TO = din("TO", [128, 2 * XCOLS])
    C.TI = din("TI", [128, 4 * XCOLS], BF16)
    C.MSK = din("MSK", [128, 3 * 384], BF16)
    C.UT = din("UT", [128, 128], BF16)
    C.R2M = din("R2M", [128, 128], BF16)
    C.ROM = din("ROM", [128, 128], BF16)
    C.IDEN = din("IDEN", [128, 128], BF16)
    C.E2 = din("E2", [128, 2], BF16)
    C.CONSTS = din("CONSTS", [16])
    C.ONED = din("ONED", [64, 1])
    C.BASE2D = din("BASE2D", [128, 1])

    C.y = nc.dram_tensor("y", [DM, SL], F32, kind="ExternalOutput")

    C.xA = nc.dram_tensor("xA", [DM, XCOLS], F32)
    C.xB = nc.dram_tensor("xB", [DM, XCOLS], F32)
    C.out1 = nc.dram_tensor("out1d", [DM, SL], F32)
    C.wqkv_bf = nc.dram_tensor("wqkv_bf", [DEPTH * DM, 3 * DM], BF16)
    C.wo6_bf = nc.dram_tensor("wo6_bf", [DEPTH * DM, DM], BF16)
    C.w1_bf = nc.dram_tensor("w1_bf", [DEPTH * DM, DFF], BF16)
    C.w2_bf = nc.dram_tensor("w2_bf", [DEPTH * DFF2, DM], BF16)
    C.wqo_bf = nc.dram_tensor("wqo_bf", [4 * DM, DM], BF16)

    C.xA_b = [Buf(f"xA{c}") for c in range(NXC)]
    C.xB_b = [Buf(f"xB{c}") for c in range(NXC)]
    C.out1_b = [Buf(f"o1{c}") for c in range(SL // CH)]
    C.wbf_b = Buf("wbf")
    C.y_b = Buf("y")

    placed = []

    def SBAT(name, shape, dtype, off):
        nbytes = int(np.prod(shape[1:])) * (2 if dtype == BF16 else 4) if len(shape) > 1 else 4
        t = nc.alloc_sbuf_tensor_at(name, list(shape), dtype, offset=C.sb_base + off)
        b = Buf(name)
        for (o2, s2, b2) in placed:
            if off < o2 + s2 and o2 < off + nbytes:
                b.aliases.append(b2)
                b2.aliases.append(b)
        placed.append((off, nbytes, b))
        C.layout_max = max(getattr(C, "layout_max", 0), off + nbytes)
        return t, b

    C.sb_base = (nc.sbuf_base + 31) // 32 * 32
    cur = {"w": 0}

    def place(name, shape, dtype, region, pad=0):
        t, b = SBAT(name, shape, dtype, cur[region])
        nbytes = int(np.prod(shape[1:])) * (2 if dtype == BF16 else 4)
        cur[region] += ((nbytes + pad + 31) // 32) * 32
        return t, b

    C.Wqkv_sb, C.Wqkv_B = place("Wqkv_sb", [128, NFC * 3 * DM], BF16, "w")
    C.W1_sb, C.W1_B = place("W1_sb", [128, NFC * DFF], BF16, "w")
    C.W2_sb, C.W2_B = [], []
    for i in range(2):
        t, b = place(f"W2_sb{i}", [128, DM], BF16, "w")
        C.W2_sb.append(t); C.W2_B.append(b)
    C.WoB_sb, C.WoB_B = [], []
    for i in range(2):
        t, b = place(f"WoB{i}", [128, DM], BF16, "w")
        C.WoB_sb.append(t); C.WoB_B.append(b)
    W_END = cur["w"]
    assert W_END <= 144704 + 64, W_END

    cur["ow"] = 49152
    C.OWo_sb, C.OWo_B = place("OWo_sb", [128, NFC * DM], BF16, "ow")
    C.oxt, C.oxt_B = place("oxt", [128, NFC * 512], BF16, "ow")
    C.oyb, C.oyb_B = place("oyb", [128, NFC * 512], BF16, "ow")
    _ovb_off = cur["ow"]
    C.ovb, C.ovb_B = place("ovb", [128, NFC * 512], BF16, "ow")
    cur["ow_att"] = _ovb_off
    C.oatt, C.oatt_B = place("oatt", [128, NFC * 512], BF16, "ow_att")
    C.oQ, C.oQ_B = place("oQ", [128, NFC * 512], BF16, "ow")
    C.oK, C.oK_B = place("oK", [128, NFC * 512], BF16, "ow")
    C.oV, C.oV_B = place("oV", [128, 4 * NH * 65], BF16, "ow")
    C.oTt, C.oTt_B = [], []
    for i in range(2):
        t, b = place(f"oTt{i}", [128, 2 * 512], F32, "ow")
        C.oTt.append(t); C.oTt_B.append(b)
    C.odrow, C.odrow_B = place("odrow", [1, 128], F32, "ow")
    C.orc, C.orc_B = place("orc", [1, 128], BF16, "ow")
    C.orb, C.orb_B = place("orb", [64, 128], F32, "ow")
    C.oKmS, C.oKmS_B = place("oKmS", [128, NFC * 128], BF16, "ow")
    C.qc0, C.qc0_B = place("qc0", [128, SCH], BF16, "ow")
    C.kc0, C.kc0_B = place("kc0", [64, SCH], BF16, "ow")
    C.oSt, C.oSt_B = place("oSt", [64, NH * 65], F32, "ow")
    C.oStb, C.oStb_B = place("oStb", [128, NH * 65], BF16, "ow")
    C.tbig, C.tbig_B = [], []
    for i in range(2):
        t, b = place(f"tbig{i}", [128, 512], BF16, "ow")
        C.tbig.append(t); C.tbig_B.append(b)
    C.oSm, C.oSm_B = place("oSm", [128, 128], BF16, "ow")
    C.ofr, C.ofr_B = [], []
    for i in range(2):
        t, b = place(f"ofr{i}", [128, 512], F32, "ow")
        C.ofr.append(t); C.ofr_B.append(b)
    assert cur["ow"] <= 144704, cur["ow"]

    ACT = 144704
    cur["act"] = ACT
    C.sqb, C.sqb_B = place("sqb", [128, NFC * 512], BF16, "act")
    OV = cur["act"]
    cur["attn"] = OV
    C.xbf, C.xbf_B = place("xbf", [128, NFC * (CH + 1)], BF16, "attn")
    cur["attn_att"] = OV
    C.att, C.att_B = place("att", [128, NFC * CH], BF16, "attn_att")
    C.Qb, C.Qb_B = place("Qb", [128, NFC * CH], BF16, "attn")
    C.Kc, C.Kc_B = place("Kc", [128, NFC * CH], BF16, "attn")
    C.Kp, C.Kp_B = [], []
    for i in range(2):
        t, b = place(f"Kp{i}", [128, NFC * CH], BF16, "attn")
        C.Kp.append(t); C.Kp_B.append(b)
    C.Vkm, C.Vkm_B = [], []
    for i in range(2):
        t, b = place(f"Vkm{i}", [128, 2 * NH * DH], BF16, "attn")
        C.Vkm.append(t); C.Vkm_B.append(b)
    _sm_off = cur["attn"]
    C.Sm, C.Sm_B = place("Sm", [128, 384], F32, "attn")
    C.Pexp, C.Pexp_B = place("Pexp", [128, 384], BF16, "attn")
    C.Pt, C.Pt_B = place("Pt", [128, 384], BF16, "attn")
    C.TIt, C.TIt_B = place("TIt", [128, 4 * CH], BF16, "attn")
    _tq_off = cur["attn"]
    C.tq, C.tq_B = [], []
    for i in range(1):
        t, b = place(f"tq{i}", [128, CH], F32, "attn")
        C.tq.append(t); C.tq_B.append(b)
    C.tb, C.tb_B = [], []
    for i in range(3):
        t, b = place(f"tb{i}", [128, CH], BF16, "attn")
        C.tb.append(t); C.tb_B.append(b)
    C.sums, C.sums_B = place("sums", [128, 2 * NH], F32, "attn")
    C.recipc, C.recipc_B = place("recipc", [128, 2 * NH], F32, "attn")
    C.nmax, C.nmax_B = place("nmax", [128, 2], F32, "attn")
    C.xmo, C.xmo_B = [], []
    for i in range(2):
        t, b = place(f"xmo{i}", [128, 2 * CH], F32, "attn")
        C.xmo.append(t); C.xmo_B.append(b)
    cur["attn_afr0"] = _tq_off
    cur["attn_afr1"] = _sm_off
    C.afr, C.afr_B = [], []
    t, b = place("afr0", [128, 512], F32, "attn_afr0")
    C.afr.append(t); C.afr_B.append(b)
    t, b = place("afr1", [128, 512], F32, "attn_afr1")
    C.afr.append(t); C.afr_B.append(b)
    C.MSK_sb, C.MSK_B = place("MSK_sb", [128, 3 * 384], BF16, "attn")
    assert cur["attn"] <= ACT + 56064, cur["attn"]
    cur["ffn"] = OV
    C.xsb, C.xsb_B = place("xsb", [128, NFC * (CH + 1)], BF16, "ffn")
    C.xn, C.xn_B = place("xn", [128, NFC * CH], BF16, "ffn")
    C.h2i, C.h2i_B = [], []
    for i in range(2):
        t, b = place(f"h2i{i}", [128, CH], BF16, "ffn")
        C.h2i.append(t); C.h2i_B.append(b)
    C.gact, C.gact_B = [], []
    for i in range(2):
        t, b = place(f"gact{i}", [128, CH], BF16, "ffn")
        C.gact.append(t); C.gact_B.append(b)
    C.fres, C.fres_B = [], []
    for i in range(2):
        t, b = place(f"fres{i}", [128, 512], F32, "ffn")
        C.fres.append(t); C.fres_B.append(b)
    C.xnext, C.xnext_B = place("xnext", [128, NFC * CH], F32, "ffn")
    C.h2, C.h2_B = place("h2", [128, NOG1 * CH], BF16, "ffn")
    assert cur["ffn"] <= ACT + 56064, cur["ffn"]
    cur["oep"] = OV
    C.ozb, C.ozb_B = place("ozb", [128, NFC * 512], BF16, "oep")
    C.oo1, C.oo1_B = place("oo1", [128, NFC * 512], F32, "oep")
    C.oo1b, C.oo1b_B = place("oo1b", [128, NFC * 512], BF16, "oep")
    assert cur["oep"] <= ACT + 56064, cur["oep"]

    C.cursors = dict(cur)
    cur["fix"] = ((max(cur["attn"], cur["ffn"], cur["oep"]) + 31) // 32) * 32
    C.UT_sb, _b = place("UT_sb", [128, 128], BF16, "fix")
    C.R2_sb, _b = place("R2_sb", [128, 128], BF16, "fix")
    C.RO_sb, _b = place("RO_sb", [128, 128], BF16, "fix")
    C.ID_sb, _b = place("ID_sb", [128, 128], BF16, "fix")
    C.E2_sb, _b = place("E2_sb", [128, 2], BF16, "fix")
    C.ONES_sb, _b = place("ONES_sb", [128, 2], BF16, "fix")
    C.ONErow_sb, _b = place("ONErow_sb", [1, 128], BF16, "fix")
    C.CONST_sb, C.const_B = place("CONST_sb", [128, 16], F32, "fix")
    C.BASE2_sb, C.BASE2_B = place("BASE2_sb", [128, 1], F32, "fix")
    C.LQS_sb, C.LQS_B = place("LQS_sb", [2, 128], BF16, "fix")
    C.LKS_sb, C.LKS_B = place("LKS_sb", [2, 128], BF16, "fix")
    C.Lgb_sb, C.Lgb_B = place("Lgb_sb", [2, NFC * 128], BF16, "fix")
    C.gcol_sb, C.gcol_B = place("gcol_sb", [128, NFC], F32, "fix")
    C.b1a_sb, C.b1a_B = place("b1a_sb", [128, NOG1], F32, "fix")
    C.b1g_sb, C.b1g_B = place("b1g_sb", [128, NOG1], F32, "fix")
    C.b2c_sb, C.b2c_B = place("b2c_sb", [128, NFC], F32, "fix")
    C.rmu, C.rmu_B = place("rmu", [1, 512], F32, "fix")
    C.rvar, C.rvar_B = place("rvar", [1, 512], F32, "fix")
    C.rmsq, C.rmsq_B = place("rmsq", [1, 512], F32, "fix")
    C.rtmp, C.rtmp_B = C.rmsq, C.rmsq_B
    C.rowp, C.rowp_B = place("rowp", [2, 512], BF16, "fix")
    C.rstdb, C.rstdb_B = place("rstdb", [1, 512], BF16, "fix")
    C.rr, C.rr_B = place("rr", [2, CH], BF16, "fix")
    C.zcol, C.zcol_B = place("zcol", [128, 8], F32, "fix")
    C.lnt, C.lnt_B = place("lnt", [2, CH], F32, "fix")
    total = cur["fix"]
    avail = nc.sbuf_bytes_remaining if C.sb_base == nc.sbuf_base else 212863
    assert total <= 212863, f"SBUF layout over budget: {total}"

    cb = Buf("cbufs")
    C.cb = cb

    C.PS = [nc.alloc_psum_tensor(f"ps{i}", [128, 512], F32) for i in range(8)]
    C.PB = [Buf(f"ps{i}") for i in range(8)]

    def tap(name):
        if not debug_taps:
            return
        d = nc.dram_tensor(name, [DM, XCOLS], F32, kind="ExternalOutput")
        P.dma("gpsimd", (lambda d=d: lambda e: e.dma_start(out=d.ap(), in_=C.xA.ap()))(),
              reads=list(C.xA_b), writes=[], tag="tap")

    build_setup(C)
    build_prefix(C)
    build_outer(C)
    tap("dbg_x0")
    for ell in range(DEPTH):
        build_attn_layer(C, ell)
        if debug_taps and ell == 0:
            dxb = nc.dram_tensor("dbg_xb0", [DM, XCOLS], F32, kind="ExternalOutput")
            P.dma("gpsimd", lambda e: e.dma_start(out=dxb.ap(), in_=C.xB.ap()),
                  reads=list(C.xB_b), writes=[], tag="tapb")
        build_ffn_layer(C, ell)
        tap(f"dbg_x{ell + 1}")
    build_final(C)

    if debug_taps:
        C.dbg_out1 = nc.dram_tensor("dbg_out1", [DM, SL], F32, kind="ExternalOutput")
        P.dma(
            "gpsimd",
            lambda e: e.dma_start(out=C.dbg_out1.ap(), in_=C.out1.ap()),
            reads=list(C.out1_b),
            writes=[],
        )

    P.emit(nc)
    return nc, C


C_EPS24, C_EPS5, C_EPS6, C_I1024, C_MHALF, C_M1 = 0, 1, 2, 3, 4, 5


def ccol(C, i):
    return C.CONST_sb[:, i:i + 1]


def build_setup(C):
    P, nc = C.P, C.nc
    loads = [
        (C.UT_sb, C.UT), (C.R2_sb, C.R2M), (C.RO_sb, C.ROM),
        (C.ID_sb, C.IDEN), (C.E2_sb, C.E2),
    ]
    P.dma("sync", lambda e: e.dma_start(out=C.BASE2_sb[:, :], in_=C.BASE2D.ap()),
          writes=[C.BASE2_B], tag="constload")
    for sb, dr in loads:
        P.dma("sync", (lambda sb=sb, dr=dr: lambda e: e.dma_start(out=sb[:, :], in_=dr.ap()))(),
              writes=[C.cb], tag="constload")
    P.dma("gpsimd",
          lambda e: e.dma_start(out=C.CONST_sb[:, :],
                                in_=C.CONSTS.ap().unsqueeze(0).broadcast_to([128, 16])),
          writes=[C.const_B], tag="constload")
    P.op("vector", lambda e: e.memset(C.ONES_sb[:, :], 1.0), writes=[C.cb])
    P.op("vector", lambda e: e.memset(C.ONErow_sb[:, :], 1.0), writes=[C.cb])
    P.op("vector", lambda e: e.memset(C.rowp[0:2, :], -1.0), writes=[C.rowp_B])
    P.dma("gpsimd",
          lambda e: e.dma_start(
              out=C.oV[:, :].rearrange("p (en d) -> p en d", d=65)[:, :, 64:65],
              in_=C.ONED.ap().unsqueeze(0).broadcast_to([128, 64, 1]),
              allow_slow_non_contiguous=True),
          writes=[C.oV_B], tag="onedl")
    P.op("vector", lambda e: e.memset(C.oSt[:, :], 0.0), writes=[C.oSt_B])
    P.op("vector", lambda e: e.memset(C.zcol[:, :], 0.0), writes=[C.zcol_B])
    for ell in range(1, DEPTH):
        col = (14 - W_LEAD[ell]) * CH - 1
        P.dma("sync",
              (lambda col=col: lambda e: e.dma_start(
                  out=C.xA.ap().rearrange("(c p) w -> p c w", p=128)[:, :, col:col + 1],
                  in_=C.zcol[:, :].rearrange("p (c w) -> p c w", w=1),
                  allow_slow_non_contiguous=True))(),
              reads=[C.zcol_B],
              writes=region(C.xA_b, col // CH, col // CH + 1), tag="zc")
    P.op("vector", lambda e: e.memset(C.oStb[:, :], 0.0), writes=[C.oStb_B])
    P.op("vector", lambda e: e.memset(C.qc0[:, :], 0.0), writes=[C.qc0_B])

    pre = [
        (C.wqkv_bf, C.Wqkv), (C.wo6_bf, C.Wo6), (C.w1_bf, C.W1), (C.w2_bf, C.W2),
    ]
    for dst, src in pre:
        P.dma("gpsimd", (lambda dst=dst, src=src: lambda e: e.dma_start(out=dst.ap(), in_=src.ap()))(),
              writes=[C.wbf_b], tag="precast")
    for i, src in enumerate([C.wq, C.wk, C.wv, C.wo]):
        P.dma("gpsimd",
              (lambda i=i, src=src: lambda e: e.dma_start(
                  out=C.wqo_bf.ap()[i * DM:(i + 1) * DM, :], in_=src.ap()))(),
              writes=[C.wbf_b], tag="precast")

    for i in range(3):
        P.dma("sync",
              (lambda i=i: lambda e: e.dma_start(
                  out=C.Wqkv_sb[:, :].rearrange("p (w k c) -> p w k c", w=3, k=NFC)[:, i],
                  in_=C.wqo_bf.ap()[i * DM:(i + 1) * DM, :].rearrange("(k p) c -> p k c", p=128)))(),
              reads=[C.wbf_b], writes=[C.Wqkv_B], tag="owload")
    P.dma("sync",
          lambda e: e.dma_start(
              out=C.OWo_sb[:, :].rearrange("p (k c) -> p k c", k=NFC),
              in_=C.wqo_bf.ap()[3 * DM:4 * DM, :].rearrange("(k p) c -> p k c", p=128)),
          reads=[C.wbf_b], writes=[C.OWo_B], tag="owload")


def ln_rows(C, xsrc_bf, xsrc_buf, ncols, pb_sums, pb_sq, sq_tile, sq_buf, tag):
    P = C.P
    P.op("scalar",
         lambda e: e.activation(sq_tile[:, :NFC * ncols], xsrc_bf[:, :NFC * ncols], AF.Square),
         reads=[xsrc_buf], writes=[sq_buf], tag=tag + "_sq")
    for kc in range(NFC):
        P.op("tensor",
             (lambda kc=kc: lambda e: e.matmul(
                 C.PS[pb_sums][0:1, :ncols], C.ONES_sb[:, 0:1],
                 xsrc_bf[:, kc * ncols:(kc + 1) * ncols],
                 start=(kc == 0), stop=(kc == NFC - 1)))(),
             reads=[xsrc_buf, C.cb], writes=[C.PB[pb_sums]], tag=tag + "_s")
        P.op("tensor",
             (lambda kc=kc: lambda e: e.matmul(
                 C.PS[pb_sq][0:1, :ncols], C.ONES_sb[:, 0:1],
                 sq_tile[:, kc * ncols:(kc + 1) * ncols],
                 start=(kc == 0), stop=(kc == NFC - 1)))(),
             reads=[sq_buf, C.cb], writes=[C.PB[pb_sq]], tag=tag + "_sq2")


def ln_rows_finish(C, ncols, pb_sums, pb_sq, rows, rows_B, rstdb, rstdb_B, murow_dst, murow_buf, tag):
    P = C.P
    P.op("scalar",
         lambda e: e.activation(C.rmu[0:1, :ncols], C.PS[pb_sums][0:1, :ncols], AF.Copy,
                                scale=ccol(C, C_I1024)[0:1, :]),
         reads=[C.PB[pb_sums], C.const_B], writes=[C.rmu_B], tag=tag + "_mu")
    P.op("vector",
         lambda e: e.tensor_mul(C.rmsq[0:1, :ncols], C.rmu[0:1, :ncols], C.rmu[0:1, :ncols]),
         reads=[C.rmu_B], writes=[C.rmsq_B], tag=tag + "_musq")
    P.op("vector",
         lambda e: e.scalar_tensor_tensor(C.rvar[0:1, :ncols], C.PS[pb_sq][0:1, :ncols],
                                          ccol(C, C_I1024)[0:1, :], C.rmsq[0:1, :ncols],
                                          ALU.mult, ALU.subtract),
         reads=[C.PB[pb_sq], C.rmsq_B, C.const_B], writes=[C.rvar_B], tag=tag + "_var")
    P.op("scalar",
         lambda e: e.activation(C.rtmp[0:1, :ncols], C.rvar[0:1, :ncols], AF.Ln,
                                bias=ccol(C, C_EPS5)[0:1, :]),
         reads=[C.rvar_B, C.const_B], writes=[C.rtmp_B], tag=tag + "_ln")
    P.op("scalar",
         lambda e: e.activation(rstdb[0:1, :ncols], C.rtmp[0:1, :ncols], AF.Exp,
                                scale=ccol(C, C_MHALF)[0:1, :]),
         reads=[C.rtmp_B, C.const_B], writes=[rstdb_B], tag=tag + "_rstd")
    P.op("vector",
         lambda e: e.tensor_mul(murow_dst[0:1, :ncols], C.rmu[0:1, :ncols], rstdb[0:1, :ncols]),
         reads=[C.rmu_B, rstdb_B], writes=[murow_buf], tag=tag + "_mur")


def plain_ln(C, xb, xb_B, dst, dst_B, TT, tag):
    P = C.P
    ln_rows(C, xb, xb_B, TT, 0, 1, C.sqb, C.sqb_B, tag)
    ln_rows_finish(C, TT, 0, 1, None, None, C.rstdb, C.rstdb_B,
                   C.rowp, C.rowp_B, tag)
    P.op("tensor",
         lambda e: e.matmul(C.PS[2][:, :TT], C.ONErow_sb[0:1, :], C.rstdb[0:1, :TT],
                            start=True, stop=True),
         reads=[C.rstdb_B, C.cb], writes=[C.PB[2]], tag=tag + "_A")
    P.op("tensor",
         lambda e: e.matmul(C.PS[3][:, :TT], C.ONErow_sb[0:1, :], C.rowp[0:1, :TT],
                            start=True, stop=True),
         reads=[C.rowp_B, C.cb], writes=[C.PB[3]], tag=tag + "_C")
    for fc in range(NFC):
        P.op("vector",
             (lambda fc=fc: lambda e: e.tensor_mul(
                 dst[:, fc * TT:(fc + 1) * TT], xb[:, fc * TT:(fc + 1) * TT],
                 C.PS[2][:, :TT]))(),
             reads=[xb_B, C.PB[2]], writes=[dst_B], tag=tag + "_p1")
        P.op("vector",
             (lambda fc=fc: lambda e: e.tensor_sub(
                 dst[:, fc * TT:(fc + 1) * TT], dst[:, fc * TT:(fc + 1) * TT],
                 C.PS[3][:, :TT]))(),
             reads=[dst_B, C.PB[3]], writes=[dst_B], tag=tag + "_p2")


def build_prefix(C):
    P = C.P
    TT = 512
    ntiles = PREF // TT
    for t in range(ntiles):
        xb, xb_B = C.oxt, C.oxt_B
        P.dma("gpsimd",
              (lambda t=t, xb=xb: lambda e: e.dma_start(
                  out=xb[:, :].rearrange("p (c w) -> p c w", c=NFC),
                  in_=C.xpref.ap().rearrange("(c p) w -> p c w", p=128)[:, :, t * TT:(t + 1) * TT]))(),
              writes=[xb_B], tag="pf_x")
        tt, tt_B = C.oTt[t % 2], C.oTt_B[t % 2]
        P.dma("sync",
              (lambda t=t, tt=tt: lambda e: e.dma_start(
                  out=tt[:, :], in_=C.TP.ap()[:, t * 2 * TT:(t + 1) * 2 * TT]))(),
              writes=[tt_B], tag="pf_tt")
        plain_ln(C, xb, xb_B, C.oyb, C.oyb_B, TT, "pf_ln")
        kv_proj_scan(C, t, TT, C.oyb, C.oyb_B, tt, tt_B, with_q=False)


def kv_proj_scan(C, t, TT, ysrc, ysrc_B, tt, tt_B, with_q):
    P = C.P
    nsc = TT // SCH

    def wslot(i):
        return lambda kc, og: C.Wqkv_sb[:, (i * NFC + kc) * DM + og * 128:
                                        (i * NFC + kc) * DM + og * 128 + 128]

    proj_rope_outer(C, wslot(1), ysrc, ysrc_B, TT, tt, tt_B, sin_neg=False,
                    dst=C.oK, dst_B=C.oK_B, tag=f"kp{t}")
    for og in range(NFC):
        pb = 4 + og % 2
        for kc in range(NFC):
            P.op("tensor",
                 (lambda og=og, kc=kc, pb=pb: lambda e: e.matmul(
                     C.PS[pb][:, :TT], wslot(2)(kc, og),
                     ysrc[:, kc * TT:(kc + 1) * TT],
                     start=(kc == 0), stop=(kc == NFC - 1)))(),
                 reads=[ysrc_B, C.Wqkv_B], writes=[C.PB[pb]], tag=f"vproj{t}")
        P.op("scalar",
             (lambda og=og, pb=pb: lambda e: e.activation(
                 C.ovb[:, og * TT:(og + 1) * TT], C.PS[pb][:, :TT], AF.Copy))(),
             reads=[C.PB[pb]], writes=[C.ovb_B], tag=f"vevac{t}")
    for sc in range(nsc):
        for fc in range(NFC):
            pb = 6 + fc % 2
            P.op("tensor",
                 (lambda sc=sc, fc=fc, pb=pb: lambda e: e.transpose(
                     C.PS[pb].ap().bitcast(BF16)[:, 0:128],
                     C.ovb[:, fc * TT + sc * SCH: fc * TT + (sc + 1) * SCH],
                     C.ID_sb[:, :]))(),
                 reads=[C.ovb_B, C.cb], writes=[C.PB[pb]], tag=f"vT{t}")
            P.op("vector",
                 (lambda sc=sc, fc=fc, pb=pb: lambda e: e.tensor_copy(
                     C.oV[:, :].rearrange("p (en d) -> p en d", d=65)
                     [:, sc * NH + 2 * fc: sc * NH + 2 * fc + 2, 0:64],
                     C.PS[pb].ap().bitcast(BF16)[:, 0:128].rearrange(
                         "p (en d) -> p en d", d=DH)))(),
                 reads=[C.PB[pb]], writes=[C.oV_B], tag=f"vTe{t}")
    if with_q:
        proj_rope_outer(C, wslot(0), ysrc, ysrc_B, TT, tt, tt_B, sin_neg=False,
                        dst=C.oQ, dst_B=C.oQ_B, tag=f"qp{t}")
    for sc in range(nsc):
        for fc in range(NFC):
            pb = 6 + fc % 2
            P.op("tensor",
                 (lambda sc=sc, fc=fc, pb=pb: lambda e: e.transpose(
                     C.PS[pb].ap().bitcast(BF16)[:, 0:128],
                     C.oK[:, fc * TT + sc * SCH: fc * TT + (sc + 1) * SCH],
                     C.ID_sb[:, :]))(),
                 reads=[C.oK_B, C.cb], writes=[C.PB[pb]], tag=f"kT{t}")
            P.op("vector",
                 (lambda fc=fc, pb=pb: lambda e: e.tensor_copy(
                     C.oKmS[:, fc * 128:(fc + 1) * 128],
                     C.PS[pb].ap().bitcast(BF16)[:, 0:128]))(),
                 reads=[C.PB[pb]], writes=[C.oKmS_B], tag=f"kTe{t}")
        for h in range(NH):
            scan_chunk(C, t, sc, h, TT, with_q)


def proj_rope_outer(C, wsl, ysrc, ysrc_B, TT, tt, tt_B, sin_neg, dst, dst_B, tag):
    P = C.P
    for og in range(NFC):
        pb = 4 + og % 2
        for kc in range(NFC):
            P.op("tensor",
                 (lambda og=og, kc=kc, pb=pb: lambda e: e.matmul(
                     C.PS[pb][:, :TT], wsl(kc, og), ysrc[:, kc * TT:(kc + 1) * TT],
                     start=(kc == 0), stop=(kc == NFC - 1)))(),
                 reads=[ysrc_B, C.Wqkv_B], writes=[C.PB[pb]], tag=tag + "_mm")
        P.op("scalar",
             (lambda og=og, pb=pb: lambda e: e.activation(
                 dst[:, og * TT:(og + 1) * TT], C.PS[pb][:, :TT], AF.Copy))(),
             reads=[C.PB[pb]], writes=[dst_B], tag=tag + "_ev")
        P.op("tensor",
             (lambda og=og: lambda e: e.matmul(
                 C.PS[6][:, :TT], C.RO_sb[:, :], dst[:, og * TT:(og + 1) * TT],
                 start=True, stop=True))(),
             reads=[dst_B, C.cb], writes=[C.PB[6]], tag=tag + "_ro")
        P.op("vector",
             (lambda og=og: lambda e: e.tensor_mul(
                 C.tbig[0][:, :TT], C.PS[6][:, :TT], tt[:, TT:2 * TT]))(),
             reads=[C.PB[6], tt_B], writes=[C.tbig_B[0]], tag=tag + "_t2")
        P.op("vector",
             (lambda og=og: lambda e: e.tensor_mul(
                 C.tbig[1][:, :TT], dst[:, og * TT:(og + 1) * TT], tt[:, 0:TT]))(),
             reads=[dst_B, tt_B], writes=[C.tbig_B[1]], tag=tag + "_t1")
        P.op("vector",
             (lambda og=og: lambda e: e.tensor_add(
                 C.tbig[1][:, :TT], C.tbig[1][:, :TT], C.tbig[0][:, :TT]))(),
             reads=[C.tbig_B[1], C.tbig_B[0]], writes=[C.tbig_B[1]], tag=tag + "_ad")
        P.op("scalar",
             (lambda og=og: lambda e: e.activation(
                 dst[:, og * TT:(og + 1) * TT], C.tbig[1][:, :TT], AF.Relu))(),
             reads=[C.tbig_B[1]], writes=[dst_B], tag=tag + "_rl")


def scan_chunk(C, t, sc, h, TT, with_q):
    P = C.P
    fc, hp = h // 2, h % 2
    r0 = hp * 64
    kcols = lambda buf: buf[r0:r0 + 64, fc * TT + sc * SCH: fc * TT + (sc + 1) * SCH]
    ent = sc * NH + h
    vkm = C.oV[:, :].rearrange("p (en d) -> p en d", d=65)[:, ent]
    stf = C.oSt[:, h * 65:(h + 1) * 65]
    stb = C.oStb[:, h * 65:(h + 1) * 65]
    stb_lo = C.oStb[0:64, h * 65:(h + 1) * 65]

    P.op("gpsimd",
         lambda e: e.tensor_copy(C.kc0[:, :], kcols(C.oK)),
         reads=[C.oK_B], writes=[C.kc0_B], tag=f"kcp{t}")
    if with_q:
        P.op("gpsimd",
             lambda e: e.tensor_copy(C.qc0[0:64, :], kcols(C.oQ)),
             reads=[C.oQ_B], writes=[C.qc0_B], tag=f"qcp{t}")
        P.op("tensor",
             lambda e: e.matmul(C.PS[5][:, 0:128], C.kc0[:, :], C.qc0[0:64, :],
                                start=True, stop=True),
             reads=[C.kc0_B, C.qc0_B], writes=[C.PB[5]], tag=f"sT{t}")
        P.op("vector",
             lambda e: e.tensor_mul(C.oSm[:, :], C.PS[5][:, 0:128], C.UT_sb[:, :]),
             reads=[C.PB[5], C.cb], writes=[C.oSm_B], tag=f"sTm{t}")
        P.op("tensor",
             lambda e: e.matmul(C.PS[6][0:65, 0:128], stb, C.qc0[:, :],
                                start=True, stop=False),
             reads=[C.oStb_B, C.qc0_B], writes=[C.PB[6]], tag=f"nd1{t}")
        P.op("tensor",
             lambda e: e.matmul(C.PS[6][0:65, 0:128], vkm, C.oSm[:, :],
                                start=False, stop=True),
             reads=[C.oV_B, C.oSm_B], writes=[C.PB[6]], tag=f"nd2{t}")
        P.op("scalar",
             lambda e: e.copy(C.odrow[0:1, 0:128], C.PS[6][64:65, 0:128]),
             reads=[C.PB[6]], writes=[C.odrow_B], tag=f"dcp{t}")
        P.op("scalar",
             lambda e: e.activation(C.odrow[0:1, 0:128],
                                    C.odrow[0:1, 0:128], AF.Ln,
                                    bias=ccol(C, C_EPS6)[0:1, :]),
             reads=[C.odrow_B, C.const_B], writes=[C.odrow_B], tag=f"dln{t}")
        P.op("scalar",
             lambda e: e.activation(C.orc[0:1, 0:128],
                                    C.odrow[0:1, 0:128], AF.Exp,
                                    scale=ccol(C, C_M1)[0:1, :]),
             reads=[C.odrow_B, C.const_B], writes=[C.orc_B], tag=f"dex{t}")
        P.op("tensor",
             lambda e: e.matmul(C.PS[7][0:64, 0:128], C.ONErow_sb[0:1, 0:64],
                                C.orc[0:1, 0:128], start=True, stop=True),
             reads=[C.orc_B, C.cb], writes=[C.PB[7]], tag=f"dbc{t}")
        P.op("scalar",
             lambda e: e.copy(C.orb[:, :], C.PS[7][0:64, 0:128]),
             reads=[C.PB[7]], writes=[C.orb_B], tag=f"dbe{t}")
        P.op("vector",
             lambda e: e.tensor_mul(
                 C.oatt[r0:r0 + 64, fc * TT + sc * SCH: fc * TT + (sc + 1) * SCH],
                 C.PS[6][0:64, 0:128], C.orb[:, :]),
             reads=[C.PB[6], C.orb_B], writes=[C.oatt_B], tag=f"dmu{t}")

    P.op("tensor",
         lambda e: e.matmul(C.PS[4][0:64, 0:65],
                            C.oKmS[:, fc * 128 + r0: fc * 128 + r0 + 64], vkm,
                            start=True, stop=True),
         reads=[C.oKmS_B, C.oV_B], writes=[C.PB[4]], tag=f"cM{t}")
    P.op("vector",
         lambda e: e.tensor_add(stf, stf, C.PS[4][0:64, 0:65]),
         reads=[C.oSt_B, C.PB[4]], writes=[C.oSt_B], tag=f"stu{t}")
    P.op("gpsimd",
         lambda e: e.tensor_copy(stb_lo, stf),
         reads=[C.oSt_B], writes=[C.oStb_B], tag=f"stc{t}")


def build_outer(C):
    P = C.P
    TT = 512
    ntiles = XCOLS // TT
    for t in range(ntiles):
        xb, xb_B = C.oxt, C.oxt_B
        P.dma("gpsimd",
              (lambda t=t, xb=xb: lambda e: e.dma_start(
                  out=xb[:, :].rearrange("p (c w) -> p c w", c=NFC),
                  in_=C.xout.ap().rearrange("(c p) w -> p c w", p=128)[:, :, t * TT:(t + 1) * TT]))(),
              writes=[xb_B], tag="ot_x")
        tt, tt_B = C.oTt[t % 2], C.oTt_B[t % 2]
        P.dma("sync",
              (lambda t=t, tt=tt: lambda e: e.dma_start(
                  out=tt[:, :], in_=C.TO.ap()[:, t * 2 * TT:(t + 1) * 2 * TT]))(),
              writes=[tt_B], tag="ot_tt")
        plain_ln(C, xb, xb_B, C.oyb, C.oyb_B, TT, "ot_ln")
        kv_proj_scan(C, 100 + t, TT, C.oyb, C.oyb_B, tt, tt_B, with_q=True)
        for og in range(NFC):
            pb = 4 + og % 2
            for kc in range(NFC):
                P.op("tensor",
                     (lambda og=og, kc=kc, pb=pb: lambda e: e.matmul(
                         C.PS[pb][:, :TT],
                         C.OWo_sb[:, kc * DM + og * 128: kc * DM + og * 128 + 128],
                         C.oatt[:, kc * TT:(kc + 1) * TT],
                         start=(kc == 0), stop=(kc == NFC - 1)))(),
                     reads=[C.oatt_B, C.OWo_B], writes=[C.PB[pb]], tag="ot_wo")
            P.op("scalar",
                 (lambda og=og, pb=pb: lambda e: e.activation(
                     C.ozb[:, og * TT:(og + 1) * TT], C.PS[pb][:, :TT], AF.Copy))(),
                 reads=[C.PB[pb]], writes=[C.ozb_B], tag="ot_ze")
        plain_ln(C, C.ozb, C.ozb_B, C.oo1, C.oo1_B, TT, "ot_l2")
        for fc in range(NFC):
            fr, fr_B = C.ofr[fc % 2], C.ofr_B[fc % 2]
            P.dma("sync",
                  (lambda t=t, fc=fc, fr=fr: lambda e: e.dma_start(
                      out=fr[:, :],
                      in_=C.xout.ap()[fc * 128:(fc + 1) * 128, t * TT:(t + 1) * TT]))(),
                  writes=[fr_B], tag="ot_fr")
            P.op("vector",
                 (lambda fc=fc, fr=fr: lambda e: e.tensor_add(
                     C.oo1[:, fc * TT:(fc + 1) * TT], C.oo1[:, fc * TT:(fc + 1) * TT],
                     fr[:, :]))(),
                 reads=[C.oo1_B, fr_B], writes=[C.oo1_B], tag="ot_o1c")
        if (t + 1) * TT > OLEAD:
            lo = max(0, OLEAD - t * TT)
            P.dma("sync",
                  (lambda t=t, lo=lo: lambda e: e.dma_start(
                      out=C.out1.ap().rearrange("(c p) w -> p c w", p=128)
                      [:, :, t * TT - OLEAD + lo: (t + 1) * TT - OLEAD],
                      in_=C.oo1[:, :].rearrange("p (c w) -> p c w", c=NFC)[:, :, lo:TT]))(),
                  reads=[C.oo1_B],
                  writes=region(C.out1_b, (t * TT - OLEAD + lo) // CH,
                                ((t + 1) * TT - OLEAD) // CH),
                  tag="ot_o1s")
        P.op("scalar", lambda e: e.activation(C.oo1b[:, :], C.oo1[:, :], AF.Copy),
             reads=[C.oo1_B], writes=[C.oo1b_B], tag="ot_x0c")
        plain_ln(C, C.oo1b, C.oo1b_B, C.oo1, C.oo1_B, TT, "ot_l3")
        P.dma("sync",
              (lambda t=t: lambda e: e.dma_start(
                  out=C.xA.ap().rearrange("(c p) w -> p c w", p=128)[:, :, t * TT:(t + 1) * TT],
                  in_=C.oo1[:, :].rearrange("p (c w) -> p c w", c=NFC)))(),
              reads=[C.oo1_B],
              writes=region(C.xA_b, t * TT // CH, (t + 1) * TT // CH),
              tag="ot_x0s")


def _wqkv(C, third, sub, kc):
    o = kc * 3072 + third * 1024 + sub * 128
    return C.Wqkv_sb[:, o:o + 128]


def _xs_slice(C, tile, fc, ncol=CH):
    off = fc * (CH + 1) + (1 if fc < 4 else 0)
    return tile[:, off:off + ncol]


def _ld_layer_params(C, ell):
    P = C.P
    P.op("vector", lambda e: e.memset(C.LQS_sb[:, :], 0.0), writes=[C.LQS_B], tag=f"L{ell}_pz")
    P.op("vector", lambda e: e.memset(C.LKS_sb[:, :], 0.0), writes=[C.LKS_B], tag=f"L{ell}_pz")
    P.dma("gpsimd", lambda e: e.dma_start(out=C.LQS_sb[0:1, 0:64],
                                          in_=C.qs6.ap()[ell:ell + 1, :]),
          writes=[C.LQS_B], tag=f"L{ell}_pq")
    P.dma("gpsimd", lambda e: e.dma_start(out=C.LQS_sb[1:2, 64:128],
                                          in_=C.qs6.ap()[ell:ell + 1, :]),
          writes=[C.LQS_B], tag=f"L{ell}_pq")
    P.dma("gpsimd", lambda e: e.dma_start(out=C.LKS_sb[0:1, 0:64],
                                          in_=C.ks6.ap()[ell:ell + 1, :]),
          writes=[C.LKS_B], tag=f"L{ell}_pk")
    P.dma("gpsimd", lambda e: e.dma_start(out=C.LKS_sb[1:2, 64:128],
                                          in_=C.ks6.ap()[ell:ell + 1, :]),
          writes=[C.LKS_B], tag=f"L{ell}_pk")
    P.dma("gpsimd", lambda e: e.dma_start(out=C.Lgb_sb[0:1, :],
                                          in_=C.lng.ap()[ell:ell + 1, :]),
          writes=[C.Lgb_B], tag=f"L{ell}_pg")
    P.dma("gpsimd", lambda e: e.dma_start(out=C.Lgb_sb[1:2, :],
                                          in_=C.lnb.ap()[ell:ell + 1, :]),
          writes=[C.Lgb_B], tag=f"L{ell}_pg")
    P.dma("sync", lambda e: e.dma_start(
        out=C.gcol_sb[:, :],
        in_=C.lng.ap()[ell].rearrange("(f p) -> p f", p=128),
        allow_slow_non_contiguous=True),
        writes=[C.gcol_B], tag=f"L{ell}_pg2")
    P.dma("sync", lambda e: e.dma_start(
        out=C.b1a_sb[:, 0:21],
        in_=C.b1.ap()[ell, 0:2688].rearrange("(og p) -> p og", p=128),
        allow_slow_non_contiguous=True),
        writes=[C.b1a_B], tag=f"L{ell}_pb")
    P.dma("sync", lambda e: e.dma_start(
        out=C.b1a_sb[0:42, 21:22],
        in_=C.b1.ap()[ell, 2688:2730].rearrange("(p o) -> p o", o=1)),
        writes=[C.b1a_B], tag=f"L{ell}_pb")
    P.dma("sync", lambda e: e.dma_start(
        out=C.b1g_sb[:, 0:21],
        in_=C.b1.ap()[ell, 2730:5418].rearrange("(og p) -> p og", p=128),
        allow_slow_non_contiguous=True),
        writes=[C.b1g_B], tag=f"L{ell}_pb")
    P.dma("sync", lambda e: e.dma_start(
        out=C.b1g_sb[0:42, 21:22],
        in_=C.b1.ap()[ell, 5418:5460].rearrange("(p o) -> p o", o=1)),
        writes=[C.b1g_B], tag=f"L{ell}_pb")
    P.dma("sync", lambda e: e.dma_start(
        out=C.b2c_sb[:, :],
        in_=C.b2.ap()[ell].rearrange("(f p) -> p f", p=128),
        allow_slow_non_contiguous=True),
        writes=[C.b2c_B], tag=f"L{ell}_pb2")


def _rope_inner(C, ell, bc, which, fc, pb_src, tag):
    P = C.P
    par = bc % 2
    LSB, LSB_B = (C.LQS_sb, C.LQS_B) if which == "q" else (C.LKS_sb, C.LKS_B)
    dst, dst_B = (C.Qb, C.Qb_B) if which == "q" else (C.Kc, C.Kc_B)
    toff = 0 if which == "q" else 2 * CH
    P.op("scalar",
         lambda e: e.activation(C.tq[0][:, :], C.PS[pb_src][:, 0:CH], AF.Copy),
         reads=[C.PB[pb_src]], writes=[C.tq_B[0]], tag=tag + "ev")
    P.op("scalar",
         lambda e: e.activation(C.Pt[:, 0:CH], C.tq[0][:, :], AF.Square),
         reads=[C.tq_B[0]], writes=[C.Pt_B], tag=tag + "sq")
    P.op("tensor",
         lambda e: e.matmul(C.PS[2][0:2, 0:CH], C.E2_sb[:, :], C.Pt[:, 0:CH],
                            start=True, stop=True),
         reads=[C.Pt_B, C.cb], writes=[C.PB[2]], tag=tag + "ss")
    P.op("scalar",
         lambda e: e.activation(C.lnt[0:2, 0:CH], C.PS[2][0:2, 0:CH], AF.Ln,
                                bias=ccol(C, C_EPS24)[0:2, :]),
         reads=[C.PB[2], C.const_B], writes=[C.lnt_B], tag=tag + "ln")
    P.op("scalar",
         lambda e: e.activation(C.rr[0:2, 0:CH], C.lnt[0:2, 0:CH], AF.Exp,
                                scale=ccol(C, C_MHALF)[0:2, :]),
         reads=[C.lnt_B, C.const_B], writes=[C.rr_B], tag=tag + "rs")
    P.op("tensor",
         lambda e: e.matmul(C.PS[3][:, 0:CH], LSB[:, :], C.rr[0:2, 0:CH],
                            start=True, stop=True),
         reads=[C.rr_B, LSB_B], writes=[C.PB[3]], tag=tag + "bc")
    P.op("vector",
         lambda e: e.tensor_mul(C.tb[0][:, :], C.tq[0][:, :], C.PS[3][:, 0:CH]),
         reads=[C.tq_B[0], C.PB[3]], writes=[C.tb_B[0]], tag=tag + "qn")
    P.op("tensor",
         lambda e: e.matmul(C.PS[2][:, 0:CH], C.R2_sb[:, :], C.tb[0][:, :],
                            start=True, stop=True),
         reads=[C.tb_B[0], C.cb], writes=[C.PB[2]], tag=tag + "rq")
    P.op("vector",
         lambda e: e.tensor_mul(C.tb[1][:, :], C.tb[0][:, :],
                                C.TIt[:, toff:toff + CH]),
         reads=[C.tb_B[0], C.TIt_B], writes=[C.tb_B[1]], tag=tag + "t1")
    P.op("vector",
         lambda e: e.tensor_mul(C.tb[2][:, :], C.PS[2][:, 0:CH],
                                C.TIt[:, toff + CH:toff + 2 * CH]),
         reads=[C.PB[2], C.TIt_B], writes=[C.tb_B[2]], tag=tag + "t2")
    P.op("vector",
         (lambda fc=fc: lambda e: e.tensor_add(
             dst[:, fc * CH:(fc + 1) * CH], C.tb[1][:, :], C.tb[2][:, :]))(),
         reads=[C.tb_B[1], C.tb_B[2]], writes=[dst_B], tag=tag + "o")
    if which == "k":
        P.op("vector",
             (lambda fc=fc, par=par: lambda e: e.tensor_scalar_mul(
                 C.Kp[par][:, fc * CH:(fc + 1) * CH],
                 C.Kc[:, fc * CH:(fc + 1) * CH], C.BASE2_sb[:, 0:1]))(),
             reads=[dst_B, C.BASE2_B], writes=[C.Kp_B[par]], tag=tag + "kp")


def build_attn_layer(C, ell):
    P = C.P
    xsrc, xsrc_b = C.xA, C.xA_b
    xdst, xdst_b = C.xB, C.xB_b
    AQ0 = 15 - W_LEAD[ell]
    _ld_layer_params(C, ell)
    P.dma("sync",
          lambda e: e.dma_start(
              out=C.Wqkv_sb[:, :].rearrange("p (k c) -> p k c", k=NFC),
              in_=C.wqkv_bf.ap()[ell * DM:(ell + 1) * DM, :].rearrange(
                  "(k p) c -> p k c", p=128)),
          reads=[C.wbf_b], writes=[C.Wqkv_B], tag=f"aL{ell}_w")
    P.dma("sync", lambda e: e.dma_start(out=C.MSK_sb[:, :], in_=C.MSK.ap()),
          writes=[C.MSK_B], tag=f"aL{ell}_m")
    for bc in range(AQ0 - 1, NXC):
        attn_chunk(C, ell, bc, window=(bc >= AQ0))


def attn_chunk(C, ell, bc, window):
    P = C.P
    par = bc % 2
    xsrc, xsrc_b = C.xA, C.xA_b
    xdst, xdst_b = C.xB, C.xB_b
    tg = f"aL{ell}c{bc}_"
    P.dma("gpsimd",
          (lambda bc=bc: lambda e: e.dma_start(
              out=C.xbf[:, :].rearrange("p (c w) -> p c w", c=NFC),
              in_=xsrc.ap().rearrange("(c p) w -> p c w", p=128)
              [:, :, bc * CH - 1: (bc + 1) * CH]))(),
          reads=region(xsrc_b, bc - 1, bc + 1), writes=[C.xbf_B], tag=tg + "x")
    P.dma("sync",
          (lambda bc=bc: lambda e: e.dma_start(
              out=C.TIt[:, :], in_=C.TI.ap()[:, bc * 4 * CH:(bc + 1) * 4 * CH]))(),
          writes=[C.TIt_B], tag=tg + "tt")
    for og in range(24):
        third, sub = og // 8, og % 8
        pb = og % 2
        for kc in range(NFC):
            P.op("tensor",
                 (lambda third=third, sub=sub, kc=kc, pb=pb: lambda e: e.matmul(
                     C.PS[pb][:, 0:CH], _wqkv(C, third, sub, kc),
                     _xs_slice(C, C.xbf, kc),
                     start=(kc == 0), stop=(kc == NFC - 1)))(),
                 reads=[C.xbf_B, C.Wqkv_B], writes=[C.PB[pb]], tag=tg + "mm")
        if third == 0:
            _rope_inner(C, ell, bc, "q", sub, pb, tg + f"rq{sub}")
        elif third == 1:
            _rope_inner(C, ell, bc, "k", sub, pb, tg + f"rk{sub}")
        else:
            fc = sub
            P.op("scalar",
                 (lambda pb=pb: lambda e: e.activation(
                     C.Pexp[:, 0:CH], C.PS[pb][:, 0:CH], AF.Copy))(),
                 reads=[C.PB[pb]], writes=[C.Pexp_B], tag=tg + "vev")
            for kb in range(2):
                pbt = 6 + kb
                P.op("tensor",
                     (lambda kb=kb, pbt=pbt: lambda e: e.transpose(
                         C.PS[pbt].ap().bitcast(BF16)[:, 0:128],
                         C.Pexp[:, kb * 128:(kb + 1) * 128], C.ID_sb[:, :]))(),
                     reads=[C.Pexp_B, C.cb], writes=[C.PB[pbt]], tag=tg + "vT")
                P.op("vector",
                     (lambda fc=fc, kb=kb, par=par, pbt=pbt: lambda e: e.tensor_copy(
                         C.Vkm[par][:, :].rearrange("p (en d) -> p en d", d=DH)
                         [:, kb * NH + 2 * fc: kb * NH + 2 * fc + 2, :],
                         C.PS[pbt].ap().bitcast(BF16)[:, 0:128].rearrange(
                             "p (en d) -> p en d", d=DH)))(),
                     reads=[C.PB[pbt]], writes=[C.Vkm_B[par]], tag=tg + "vTe")
    if window:
        first = (bc == LEADC)
        for h in range(NH):
            attn_window_head(C, ell, bc, h, first, tg)
    if window:
        attn_wo(C, ell, bc, tg)


def attn_window_head(C, ell, bc, h, first, tg):
    P = C.P
    par = bc % 2
    fc, hp = h // 2, h % 2
    r0 = hp * 64
    for qh in range(2):
        j = h * 2 + qh
        pb_s = 4 + (j % 2)
        qsl = C.Qb[r0:r0 + 64, fc * CH + qh * 128: fc * CH + qh * 128 + 128]
        kp = C.Kp[1 - par]
        if qh == 0:
            P.op("tensor",
                 (lambda pb_s=pb_s, qsl=qsl, kp=kp, r0=r0, fc=fc: lambda e: e.matmul(
                     C.PS[pb_s][:, 0:256], qsl,
                     kp[r0:r0 + 64, fc * CH: fc * CH + 256], start=True, stop=True))(),
                 reads=[C.Qb_B, C.Kp_B[1 - par]], writes=[C.PB[pb_s]], tag=tg + "s1")
            P.op("tensor",
                 (lambda pb_s=pb_s, qsl=qsl, r0=r0, fc=fc: lambda e: e.matmul(
                     C.PS[pb_s][:, 256:384], qsl,
                     C.Kc[r0:r0 + 64, fc * CH: fc * CH + 128], start=True, stop=True))(),
                 reads=[C.Qb_B, C.Kc_B], writes=[C.PB[pb_s]], tag=tg + "s2")
        else:
            P.op("tensor",
                 (lambda pb_s=pb_s, qsl=qsl, kp=kp, r0=r0, fc=fc: lambda e: e.matmul(
                     C.PS[pb_s][:, 0:128], qsl,
                     kp[r0:r0 + 64, fc * CH + 128: fc * CH + 256], start=True, stop=True))(),
                 reads=[C.Qb_B, C.Kp_B[1 - par]], writes=[C.PB[pb_s]], tag=tg + "s1")
            P.op("tensor",
                 (lambda pb_s=pb_s, qsl=qsl, r0=r0, fc=fc: lambda e: e.matmul(
                     C.PS[pb_s][:, 128:384], qsl,
                     C.Kc[r0:r0 + 64, fc * CH: fc * CH + 256], start=True, stop=True))(),
                 reads=[C.Qb_B, C.Kc_B], writes=[C.PB[pb_s]], tag=tg + "s2")
        mblk = (1 + qh) if first else 0
        P.op("vector",
             (lambda pb_s=pb_s, mblk=mblk: lambda e: e.tensor_add(
                 C.Sm[:, :], C.PS[pb_s][:, 0:384], C.MSK_sb[:, mblk * 384:(mblk + 1) * 384]))(),
             reads=[C.PB[pb_s], C.MSK_B], writes=[C.Sm_B], tag=tg + "msk")
        nmc = j % 2
        P.op("vector",
             (lambda nmc=nmc: lambda e: e.tensor_reduce(
                 C.nmax[:, nmc:nmc + 1], C.Sm[:, :], AX.X, ALU.max, negate=True))(),
             reads=[C.Sm_B], writes=[C.nmax_B], tag=tg + "mx")
        P.op("scalar",
             (lambda nmc=nmc, j=j: lambda e: e.activation(
                 C.Pexp[:, :], C.Sm[:, :], AF.Exp, bias=C.nmax[:, nmc:nmc + 1],
                 accum_out=C.sums[:, j:j + 1]))(),
             reads=[C.Sm_B, C.nmax_B], writes=[C.Pexp_B, C.sums_B], tag=tg + "ex")
        P.op("vector",
             (lambda j=j: lambda e: e.reciprocal(C.recipc[:, j:j + 1], C.sums[:, j:j + 1]))(),
             reads=[C.sums_B], writes=[C.recipc_B], tag=tg + "rc")
        P.op("scalar",
             (lambda j=j: lambda e: e.activation(
                 C.Pexp[:, :], C.Pexp[:, :], AF.Copy,
                 scale=C.recipc[:, j:j + 1]))(),
             reads=[C.Pexp_B, C.recipc_B], writes=[C.Pexp_B], tag=tg + "dv")
        for kb in range(3):
            P.op("tensor",
                 (lambda kb=kb: lambda e: e.transpose(
                     C.PS[6].ap().bitcast(BF16)[:, kb * 128:(kb + 1) * 128],
                     C.Pexp[:, kb * 128:(kb + 1) * 128], C.ID_sb[:, :]))(),
                 reads=[C.Pexp_B, C.cb], writes=[C.PB[6]], tag=tg + "pT")
        P.op("vector",
             lambda e: e.tensor_copy(C.Pt[:, :], C.PS[6].ap().bitcast(BF16)[:, 0:384]),
             reads=[C.PB[6]], writes=[C.Pt_B], tag=tg + "pTe")
        blocks = [(1 - par, 0), (1 - par, 1), (par, 0)] if qh == 0 else \
                 [(1 - par, 1), (par, 0), (par, 1)]
        for i3, (vp, kb) in enumerate(blocks):
            P.op("tensor",
                 (lambda i3=i3, vp=vp, kb=kb, r0=r0, h=h: lambda e: e.matmul(
                     C.PS[7][r0:r0 + 64, 0:128],
                     C.Vkm[vp][:, (kb * NH + h) * DH:(kb * NH + h) * DH + DH],
                     C.Pt[:, i3 * 128:(i3 + 1) * 128],
                     start=(i3 == 0), stop=(i3 == 2), tile_position=(0, r0)))(),
                 reads=[C.Vkm_B[vp], C.Pt_B], writes=[C.PB[7]], tag=tg + "pv")
        P.op("scalar",
             (lambda r0=r0, fc=fc, qh=qh: lambda e: e.activation(
                 C.att[r0:r0 + 64, fc * CH + qh * 128: fc * CH + qh * 128 + 128],
                 C.PS[7][r0:r0 + 64, 0:128], AF.Copy))(),
             reads=[C.PB[7]], writes=[C.att_B], tag=tg + "ave")


def attn_wo(C, ell, bc, tg):
    P = C.P
    par = bc % 2
    xsrc, xsrc_b = C.xA, C.xA_b
    xdst, xdst_b = C.xB, C.xB_b
    for i in range(4):
        fr, fr_B = C.afr[i % 2], C.afr_B[i % 2]
        P.dma("sync",
              (lambda i=i, bc=bc, fr=fr: lambda e: e.dma_start(
                  out=fr[:, :].rearrange("p (c w) -> p c w", c=2),
                  in_=xsrc.ap().rearrange("(c p) w -> p c w", p=128)
                  [:, 2 * i:2 * i + 2, bc * CH:(bc + 1) * CH]))(),
              reads=region(xsrc_b, bc, bc + 1), writes=[fr_B], tag=tg + "fr")
        for k in range(2):
            og = 2 * i + k
            pb = og % 2
            wb, wb_B = C.WoB_sb[og % 2], C.WoB_B[og % 2]
            P.dma("sync",
                  (lambda og=og, wb=wb: lambda e: e.dma_start(
                      out=wb[:, :].rearrange("p (k c) -> p k c", k=NFC),
                      in_=C.wo6_bf.ap()[ell * DM:(ell + 1) * DM,
                                        og * 128:(og + 1) * 128].rearrange(
                          "(k p) c -> p k c", p=128)))(),
                  reads=[C.wbf_b], writes=[wb_B], tag=tg + "wol")
            for kc in range(NFC):
                P.op("tensor",
                     (lambda og=og, kc=kc, pb=pb, wb=wb: lambda e: e.matmul(
                         C.PS[pb][:, 0:CH], wb[:, kc * 128:(kc + 1) * 128],
                         C.att[:, kc * CH:(kc + 1) * CH],
                         start=(kc == 0), stop=(kc == NFC - 1)))(),
                     reads=[C.att_B, wb_B], writes=[C.PB[pb]], tag=tg + "wo")
            P.op("vector",
                 (lambda og=og, pb=pb, fr=fr, k=k, i=i: lambda e: e.tensor_add(
                     C.xmo[i % 2][:, k * CH:(k + 1) * CH], C.PS[pb][:, 0:CH],
                     fr[:, k * CH:(k + 1) * CH]))(),
                 reads=[C.PB[pb], fr_B], writes=[C.xmo_B[i % 2]], tag=tg + "xm")
        P.dma("sync",
              (lambda bc=bc, i=i: lambda e: e.dma_start(
                  out=xdst.ap().rearrange("(c p) w -> p c w", p=128)
                  [:, 2 * i:2 * i + 2, bc * CH:(bc + 1) * CH],
                  in_=C.xmo[i % 2][:, :].rearrange("p (c w) -> p c w", c=2)))(),
              reads=[C.xmo_B[i % 2]], writes=region(xdst_b, bc, bc + 1), tag=tg + "xs")


def build_ffn_layer(C, ell):
    P = C.P
    AQ0 = 15 - W_LEAD[ell]
    FQ0 = AQ0 + 1
    P.dma("sync",
          lambda e: e.dma_start(
              out=C.W1_sb[:, :].rearrange("p (k c) -> p k c", k=NFC),
              in_=C.w1_bf.ap()[ell * DM:(ell + 1) * DM, :].rearrange(
                  "(k p) c -> p k c", p=128)),
          reads=[C.wbf_b], writes=[C.W1_B], tag=f"fL{ell}_w")
    for bc in range(FQ0, NXC):
        ffn_chunk(C, ell, bc)


def ffn_chunk(C, ell, bc):
    P = C.P
    xsrc, xsrc_b = C.xB, C.xB_b
    xdst, xdst_b = C.xA, C.xA_b
    tg = f"fL{ell}c{bc}_"
    P.dma("gpsimd",
          (lambda bc=bc: lambda e: e.dma_start(
              out=C.xsb[:, :].rearrange("p (c w) -> p c w", c=NFC),
              in_=xsrc.ap().rearrange("(c p) w -> p c w", p=128)
              [:, :, bc * CH - 1:(bc + 1) * CH]))(),
          reads=region(xsrc_b, bc - 1, bc + 1), writes=[C.xsb_B], tag=tg + "x")
    for fc in range(NFC):
        P.op("scalar",
             (lambda fc=fc: lambda e: e.activation(
                 C.sqb[:, fc * CH:(fc + 1) * CH], _xs_slice(C, C.xsb, fc), AF.Square))(),
             reads=[C.xsb_B], writes=[C.sqb_B], tag=tg + "sq")
        P.op("tensor",
             (lambda fc=fc: lambda e: e.matmul(
                 C.PS[0][0:1, 0:CH], C.ONES_sb[:, 0:1], _xs_slice(C, C.xsb, fc),
                 start=(fc == 0), stop=(fc == NFC - 1)))(),
             reads=[C.xsb_B, C.cb], writes=[C.PB[0]], tag=tg + "st1")
        P.op("tensor",
             (lambda fc=fc: lambda e: e.matmul(
                 C.PS[1][0:1, 0:CH], C.ONES_sb[:, 0:1], C.sqb[:, fc * CH:(fc + 1) * CH],
                 start=(fc == 0), stop=(fc == NFC - 1)))(),
             reads=[C.sqb_B, C.cb], writes=[C.PB[1]], tag=tg + "st2")
    ln_rows_finish(C, CH, 0, 1, None, None, C.rstdb, C.rstdb_B, C.rowp, C.rowp_B, tg + "lr")
    P.op("tensor",
         lambda e: e.matmul(C.PS[2][:, 0:CH], C.ONErow_sb[0:1, :], C.rstdb[0:1, 0:CH],
                            start=True, stop=True),
         reads=[C.rstdb_B, C.cb], writes=[C.PB[2]], tag=tg + "A")
    for fc in range(NFC):
        pbc = 3 + fc % 2
        P.op("tensor",
             (lambda fc=fc, pbc=pbc: lambda e: e.matmul(
                 C.PS[pbc][:, 0:CH], C.Lgb_sb[:, fc * 128:(fc + 1) * 128],
                 C.rowp[0:2, 0:CH], start=True, stop=True))(),
             reads=[C.rowp_B, C.Lgb_B], writes=[C.PB[pbc]], tag=tg + "C")
        P.op("vector",
             (lambda fc=fc: lambda e: e.scalar_tensor_tensor(
                 C.h2i[0][:, :], _xs_slice(C, C.xsb, fc), C.gcol_sb[:, fc:fc + 1],
                 C.PS[2][:, 0:CH], ALU.mult, ALU.mult))(),
             reads=[C.xsb_B, C.gcol_B, C.PB[2]], writes=[C.h2i_B[0]], tag=tg + "n1")
        P.op("vector",
             (lambda fc=fc, pbc=pbc: lambda e: e.tensor_sub(
                 C.xn[:, fc * CH:(fc + 1) * CH], C.h2i[0][:, :], C.PS[pbc][:, 0:CH]))(),
             reads=[C.h2i_B[0], C.PB[pbc]], writes=[C.xn_B], tag=tg + "n2")
    for og in range(NOG1):
        m = 128 if og < NOG1 - 1 else 42
        pa, pg = 5, 6
        for kc in range(NFC):
            P.op("tensor",
                 (lambda og=og, kc=kc, m=m: lambda e: e.matmul(
                     C.PS[5][0:m, 0:CH],
                     C.W1_sb[:, kc * DFF + og * 128: kc * DFF + og * 128 + m],
                     C.xn[:, kc * CH:(kc + 1) * CH],
                     start=(kc == 0), stop=(kc == NFC - 1)))(),
                 reads=[C.xn_B, C.W1_B], writes=[C.PB[5]], tag=tg + "w1a")
            P.op("tensor",
                 (lambda og=og, kc=kc, m=m: lambda e: e.matmul(
                     C.PS[6][0:m, 0:CH],
                     C.W1_sb[:, kc * DFF + DFF2 + og * 128: kc * DFF + DFF2 + og * 128 + m],
                     C.xn[:, kc * CH:(kc + 1) * CH],
                     start=(kc == 0), stop=(kc == NFC - 1)))(),
                 reads=[C.xn_B, C.W1_B], writes=[C.PB[6]], tag=tg + "w1g")
        gpar = og % 2
        P.op("scalar",
             (lambda og=og, m=m, gpar=gpar: lambda e: e.activation(
                 C.gact[gpar][0:m, :], C.PS[6][0:m, 0:CH], AF.Gelu,
                 bias=C.b1g_sb[0:m, og:og + 1]))(),
             reads=[C.PB[6], C.b1g_B], writes=[C.gact_B[gpar]], tag=tg + "gl")
        P.op("vector",
             (lambda og=og, m=m, gpar=gpar: lambda e: e.scalar_tensor_tensor(
                 C.h2[0:m, og * CH:(og + 1) * CH], C.PS[5][0:m, 0:CH],
                 C.b1a_sb[0:m, og:og + 1],
                 C.gact[gpar][0:m, :], ALU.add, ALU.mult))(),
             reads=[C.PB[5], C.b1a_B, C.gact_B[gpar]], writes=[C.h2_B], tag=tg + "h2")
    for og in range(NOG1):
        m = 128 if og < NOG1 - 1 else 42
        wb, wb_B = C.W2_sb[og % 2], C.W2_B[og % 2]
        P.dma("sync",
              (lambda og=og, m=m, wb=wb: lambda e: e.dma_start(
                  out=wb[0:m, :],
                  in_=C.w2_bf.ap()[ell * DFF2 + og * 128: ell * DFF2 + og * 128 + m, :]))(),
              reads=[C.wbf_b], writes=[wb_B], tag=tg + "w2l")
        for og2 in range(NFC):
            P.op("tensor",
                 (lambda og=og, og2=og2, m=m, wb=wb: lambda e: e.matmul(
                      C.PS[og2][:, 0:CH],
                      wb[0:m, og2 * 128:(og2 + 1) * 128], C.h2[0:m, og * CH:(og + 1) * CH],
                      start=(og == 0), stop=(og == NOG1 - 1)))(),
                 reads=[C.h2_B, wb_B], writes=[C.PB[og2]], tag=tg + "w2")
    for i in range(4):
        fr, fr_B = C.fres[i % 2], C.fres_B[i % 2]
        P.dma("sync",
              (lambda i=i, bc=bc, fr=fr: lambda e: e.dma_start(
                  out=fr[:, :].rearrange("p (c w) -> p c w", c=2),
                  in_=xsrc.ap().rearrange("(c p) w -> p c w", p=128)
                  [:, 2 * i:2 * i + 2, bc * CH:(bc + 1) * CH]))(),
              reads=region(xsrc_b, bc, bc + 1), writes=[fr_B], tag=tg + "fr")
        for k in range(2):
            og2 = 2 * i + k
            P.op("vector",
                 (lambda og2=og2, fr=fr, k=k: lambda e:
                  e.scalar_tensor_tensor(
                      C.xnext[:, og2 * CH:(og2 + 1) * CH],
                      C.PS[og2][:, 0:CH],
                      C.b2c_sb[:, og2:og2 + 1], fr[:, k * CH:(k + 1) * CH],
                      ALU.add, ALU.add))(),
                 reads=[C.PB[og2], C.b2c_B, fr_B], writes=[C.xnext_B], tag=tg + "xo")
    P.dma("sync",
          (lambda bc=bc: lambda e: e.dma_start(
              out=xdst.ap().rearrange("(c p) w -> p c w", p=128)[:, :, bc * CH:(bc + 1) * CH],
              in_=C.xnext[:, :].rearrange("p (c w) -> p c w", c=NFC)))(),
          reads=[C.xnext_B], writes=region(xdst_b, bc, bc + 1), tag=tg + "xs")


def build_final(C):
    P = C.P
    TT = 512
    for t in range(SL // TT):
        col0 = OLEAD + t * TT
        xb, xb_B = C.oxt, C.oxt_B
        P.dma("gpsimd",
              (lambda col0=col0, xb=xb: lambda e: e.dma_start(
                  out=xb[:, :].rearrange("p (c w) -> p c w", c=NFC),
                  in_=C.xA.ap().rearrange("(c p) w -> p c w", p=128)[:, :, col0:col0 + TT]))(),
              reads=region(C.xA_b, col0 // CH, (col0 + TT) // CH),
              writes=[xb_B], tag="fi_x")
        plain_ln(C, xb, xb_B, C.oo1, C.oo1_B, TT, "fi_ln")
        for fc in range(NFC):
            fr, fr_B = C.ofr[fc % 2], C.ofr_B[fc % 2]
            P.dma("sync",
                  (lambda t=t, fc=fc, fr=fr: lambda e: e.dma_start(
                      out=fr[:, :],
                      in_=C.out1.ap()[fc * 128:(fc + 1) * 128, t * TT:(t + 1) * TT]))(),
                  reads=region(C.out1_b, t * TT // CH, (t + 1) * TT // CH),
                  writes=[fr_B], tag="fi_fr")
            P.op("vector",
                 (lambda fc=fc, fr=fr: lambda e: e.tensor_add(
                     C.oo1[:, fc * TT:(fc + 1) * TT], C.oo1[:, fc * TT:(fc + 1) * TT],
                     fr[:, :]))(),
                 reads=[C.oo1_B, fr_B], writes=[C.oo1_B], tag="fi_c")
        P.dma("sync",
              (lambda t=t: lambda e: e.dma_start(
                  out=C.y.ap().rearrange("(c p) w -> p c w", p=128)[:, :, t * TT:(t + 1) * TT],
                  in_=C.oo1[:, :].rearrange("p (c w) -> p c w", c=NFC)))(),
              reads=[C.oo1_B], writes=[C.y_b], tag="fi_ys")


def _host_shared():
    sh = {}
    i = np.arange(128)[:, None]
    j = np.arange(384)[None, :]
    band = (i <= j) & (j <= i + 256)
    m_norm = np.where(band, 0.0, -1e30).astype(np.float32)
    m_f0 = np.where(band & (j >= 256), 0.0, -1e30).astype(np.float32)
    m_f1 = np.where(band & (j >= 128), 0.0, -1e30).astype(np.float32)
    sh["m_norm"], sh["m_f0"], sh["m_f1"] = m_norm, m_f0, m_f1
    jj = np.arange(128)[:, None]
    ii = np.arange(128)[None, :]
    sh["UT"] = (jj <= ii).astype(np.float32).astype(BF)
    R = np.zeros((128, 128), np.float32)
    for h0 in (0, 64):
        for r in range(32):
            R[h0 + r + 32, h0 + r] = -1.0
            R[h0 + r, h0 + r + 32] = 1.0
    sh["R2M"] = R.astype(BF)
    RO = np.zeros((128, 128), np.float32)
    for h0 in (0, 64):
        for p in range(32):
            RO[h0 + 2 * p + 1, h0 + 2 * p] = -1.0
            RO[h0 + 2 * p, h0 + 2 * p + 1] = 1.0
    sh["ROM"] = RO.astype(BF)
    sh["IDEN"] = np.eye(128, dtype=BF)
    E2 = np.zeros((128, 2), np.float32)
    E2[0:64, 0] = 1.0
    E2[64:128, 1] = 1.0
    sh["E2"] = E2.astype(BF)
    consts = np.zeros(16, np.float32)
    consts[C_EPS24] = 1e-24
    consts[C_EPS5] = 1e-5
    consts[C_EPS6] = 1e-6
    consts[C_I1024] = 1.0 / 1024.0
    consts[C_MHALF] = -0.5
    consts[C_M1] = -1.0
    sh["CONSTS"] = consts
    return sh


def _rope_outer_tables(pos):
    j = np.arange(32)
    invf = (10000.0 ** (-2 * j / 64.0)).astype(np.float64)
    ang = pos[None, :].astype(np.float64) * invf[:, None]
    cos = np.cos(ang)
    sin = np.sin(ang)
    cos64 = np.repeat(cos, 2, axis=0)
    sin64 = np.repeat(sin, 2, axis=0)
    C128 = np.concatenate([cos64, cos64], axis=0).astype(np.float32)
    S128 = np.concatenate([sin64, sin64], axis=0).astype(np.float32)
    return C128, S128


def _rope_inner_tables(pos):
    p = np.maximum(pos, 0).astype(np.float64)
    i = np.arange(32)
    invf = 10000.0 ** (-2 * i / 64.0)
    base = (2 * i + 0.4 * 64) / (1.4 * 64)
    ang = p[None, :] * invf[:, None]
    cos = np.cos(ang)
    sin = np.sin(ang)
    tm = np.mod(p, 256.0)
    fq = base[:, None] ** ((tm[None, :] - 128.0) / 128.0)
    fk = base[:, None] ** ((128.0 - tm[None, :]) / 128.0)
    def dup(a):
        a64 = np.concatenate([a, a], axis=0)
        return np.concatenate([a64, a64], axis=0)
    Tqc = (dup(cos * fq) * QK_SCALE).astype(np.float32)
    Tqs = (dup(sin * fq) * QK_SCALE).astype(np.float32)
    Tkc = dup(cos * fk).astype(np.float32)
    Tks = dup(sin * fk).astype(np.float32)
    return Tqc, Tqs, Tkc, Tks


def _fm(x2d):
    return np.ascontiguousarray(x2d.T, dtype=np.float32)


def _core_inputs(core, inputs, sh):
    b, k = core // 4, core % 4
    s = k * SL
    x = np.asarray(inputs["L0_attn_x"])[b]
    lo = s - OLEAD - PREF
    hi = s - OLEAD
    xp = np.zeros((PREF, DM), np.float32)
    vlo = max(lo, 0)
    if hi > 0:
        xp[vlo - lo: hi - lo] = x[vlo:hi]
    xo = np.zeros((XCOLS, DM), np.float32)
    olo = s - OLEAD
    vlo = max(olo, 0)
    xo[vlo - olo:] = x[vlo: s + SL]
    d = {
        "xpref": _fm(xp),
        "xout": _fm(xo),
        "wq": np.asarray(inputs["L0_attn_wq"], np.float32),
        "wk": np.asarray(inputs["L0_attn_wk"], np.float32),
        "wv": np.asarray(inputs["L0_attn_wv"], np.float32),
        "wo": np.asarray(inputs["L0_attn_wo"], np.float32),
        "Wqkv": np.asarray(inputs["L0_ffn_Wqkv"], np.float32).reshape(DEPTH * DM, 3 * DM),
        "Wo6": np.asarray(inputs["L0_ffn_Wo"], np.float32).reshape(DEPTH * DM, DM),
        "W1": np.asarray(inputs["L0_ffn_W1"], np.float32).reshape(DEPTH * DM, DFF),
        "W2": np.asarray(inputs["L0_ffn_W2"], np.float32).reshape(DEPTH * DFF2, DM),
        "qs6": np.asarray(inputs["L0_ffn_q_scale"], np.float32),
        "ks6": np.asarray(inputs["L0_ffn_k_scale"], np.float32),
        "lng": np.asarray(inputs["L0_ffn_ln_g"], np.float32),
        "lnb": np.asarray(inputs["L0_ffn_ln_b"], np.float32),
        "b1": np.asarray(inputs["L0_ffn_b1"], np.float32),
        "b2": np.asarray(inputs["L0_ffn_b2"], np.float32),
    }
    ppos = np.arange(lo, hi)
    opos = np.arange(olo, s + SL)
    TPc, TPs = _rope_outer_tables(np.maximum(ppos, 0))
    TP = np.empty((128, 2 * PREF), np.float32)
    for t in range(PREF // 512):
        TP[:, t * 1024: t * 1024 + 512] = TPc[:, t * 512:(t + 1) * 512]
        TP[:, t * 1024 + 512: (t + 1) * 1024] = TPs[:, t * 512:(t + 1) * 512]
    TOc, TOs = _rope_outer_tables(np.maximum(opos, 0))
    TO = np.empty((128, 2 * XCOLS), np.float32)
    for t in range(XCOLS // 512):
        TO[:, t * 1024: t * 1024 + 512] = TOc[:, t * 512:(t + 1) * 512]
        TO[:, t * 1024 + 512:(t + 1) * 1024] = TOs[:, t * 512:(t + 1) * 512]
    d["TP"], d["TO"] = TP, TO
    Tqc, Tqs, Tkc, Tks = _rope_inner_tables(opos)
    TI = np.empty((128, 4 * XCOLS), BF)
    for c in range(NXC):
        base = c * 4 * CH
        TI[:, base:base + CH] = Tqc[:, c * CH:(c + 1) * CH].astype(BF)
        TI[:, base + CH:base + 2 * CH] = Tqs[:, c * CH:(c + 1) * CH].astype(BF)
        TI[:, base + 2 * CH:base + 3 * CH] = Tkc[:, c * CH:(c + 1) * CH].astype(BF)
        TI[:, base + 3 * CH:base + 4 * CH] = Tks[:, c * CH:(c + 1) * CH].astype(BF)
    d["TI"] = TI
    mf0 = sh["m_f0"] if k == 0 else sh["m_norm"]
    mf1 = sh["m_f1"] if k == 0 else sh["m_norm"]
    d["MSK"] = np.concatenate(
        [sh["m_norm"].astype(BF), mf0.astype(BF), mf1.astype(BF)], axis=1)
    d["UT"] = sh["UT"]; d["R2M"] = sh["R2M"]; d["ROM"] = sh["ROM"]
    d["IDEN"] = sh["IDEN"]; d["E2"] = sh["E2"]; d["CONSTS"] = sh["CONSTS"]
    d["ONED"] = np.ones((64, 1), np.float32)
    i6 = np.arange(128) % 64
    base = (2 * (i6 % 32) + 0.4 * 64) / (1.4 * 64)
    d["BASE2D"] = (base ** 2).astype(np.float32).reshape(128, 1)
    return d


_NC_CACHE = {}


def kernel(**inputs):
    key = "main"
    if key not in _NC_CACHE:
        _NC_CACHE[key] = build_nc()
    nc, C = _NC_CACHE[key]
    sh = _host_shared()
    in_maps = [_core_inputs(c, inputs, sh) for c in range(NCORES)]
    res = run_bass_kernel_spmd(nc, in_maps, core_ids=list(range(NCORES)))
    out = np.empty((B, L, DM), np.float32)
    for c in range(NCORES):
        b, k = c // 4, c % 4
        out[b, k * SL:(k + 1) * SL] = np.asarray(res.results[c]["y"]).T
    return out


# version 07
# speedup vs baseline: 1.2953x; 1.0111x over previous
import math

import numpy as np
import ml_dtypes

import concourse.bass as bass
import concourse.mybir as mybir
from concourse.bass_utils import run_bass_kernel_spmd

F32 = mybir.dt.float32
BF16 = mybir.dt.bfloat16
AF = mybir.ActivationFunctionType
ALU = mybir.AluOpType
AX = mybir.AxisListType
BF = ml_dtypes.bfloat16

B, L, DM = 2, 16384, 1024
NCORES = 8
SL = 4096
CH = 256
SCH = 128
NFC = 8
NH, DH = 16, 64
OLEAD = 3584
XCOLS = OLEAD + SL
PREF = 8704
LEADC = OLEAD // CH
NXC = XCOLS // CH
W_LEAD = [12, 10, 8, 6, 4, 2, 0]
DEPTH = 6
DFF2 = 2730
DFF = 5460
NOG1 = 22
NKC2 = 22
QK_SCALE = 8.0

ENGINES = ("sync", "tensor", "vector", "scalar", "gpsimd")
QUEUE_OF = {"sync": "dsync", "gpsimd": "dpool"}


class Buf:
    __slots__ = ("name", "writer", "readers", "aliases")

    def __init__(self, name):
        self.name = name
        self.writer = None
        self.readers = []
        self.aliases = []

    def read_deps(self):
        return {self.writer} if self.writer is not None else set()

    def write_deps(self):
        d = set(self.readers)
        if self.writer is not None:
            d.add(self.writer)
        for a in self.aliases:
            if a.writer is not None:
                d.add(a.writer)
            d.update(a.readers)
        return d


NSLOT = 12


class Prog:
    def __init__(self):
        self.items = []
        self.seq = {e: 0 for e in ENGINES}
        self.dcount = {"dsync": 0, "dpool": 0}

    def op(self, eng, emit, reads=(), writes=(), tag=""):
        deps = set()
        for b in reads:
            deps |= b.read_deps()
        for b in writes:
            deps |= b.write_deps()
        self.seq[eng] += 1
        tok = (eng, self.seq[eng])
        self.items.append(
            {"eng": eng, "emit": emit, "deps": deps, "tok": tok, "dma": None, "tag": tag}
        )
        for b in reads:
            b.readers.append(tok)
        for b in writes:
            b.writer = tok
            b.readers = []
            for a in b.aliases:
                a.writer = tok
                a.readers = []
        return tok

    def dma(self, eng, emit, reads=(), writes=(), tag=""):
        q = QUEUE_OF[eng]
        deps = set()
        for b in reads:
            deps |= b.read_deps()
        for b in writes:
            deps |= b.write_deps()
        self.seq[eng] += 1
        self.dcount[q] += 1
        j = self.dcount[q] - 1
        slot = j % NSLOT
        gen = j // NSLOT
        tok = (f"{q}#{slot}", gen + 1)
        self.items.append(
            {"eng": eng, "emit": emit, "deps": deps, "tok": tok, "dma": tok[0],
             "gate": (tok[0], gen), "tag": tag}
        )
        for b in reads:
            b.readers.append(tok)
        for b in writes:
            b.writer = tok
            b.readers = []
            for a in b.aliases:
                a.writer = tok
                a.readers = []
        return tok

    def emit(self, nc):
        marked = {e: set() for e in ENGINES}
        for it in self.items:
            for (src, s) in it["deps"]:
                if src in self.seq and src != it["eng"]:
                    marked[src].add(s)
        msorted = {e: sorted(marked[e]) for e in ENGINES}
        import bisect

        def rank(e, s):
            return bisect.bisect_right(msorted[e], s)

        esem = {e: nc.alloc_semaphore(f"se_{e}") for e in ENGINES}
        qsem = {}
        qfinal = {}
        for q, n in self.dcount.items():
            for slot in range(min(n, NSLOT)):
                key = f"{q}#{slot}"
                qsem[key] = nc.alloc_semaphore(f"sq_{q}_{slot}")
                qfinal[key] = (n - slot + NSLOT - 1) // NSLOT

        per_eng = {e: [] for e in ENGINES}
        for it in self.items:
            per_eng[it["eng"]].append(it)

        def body(engobj, ename):
            waited = {}
            for it in per_eng[ename]:
                wants = {}
                deps = set(it["deps"])
                if it["dma"] is not None and it["gate"][1] > 0:
                    deps.add(it["gate"])
                for (src, s) in deps:
                    if src in self.seq:
                        if src == ename:
                            continue
                        sem, val = esem[src], rank(src, s)
                    else:
                        sem, val = qsem[src], 16 * s
                    k = sem.num if hasattr(sem, "num") else id(sem)
                    if val > wants.get(k, (None, -1))[1]:
                        wants[k] = (sem, val)
                for sem, val in wants.values():
                    k = sem.num if hasattr(sem, "num") else id(sem)
                    if val > waited.get(k, -1):
                        engobj.wait_ge(sem, val)
                        waited[k] = val
                inst = it["emit"](engobj)
                tok = it["tok"]
                if it["dma"] is not None:
                    inst.then_inc(qsem[it["dma"]], 16)
                elif tok[1] in marked[ename]:
                    inst.then_inc(esem[ename], 1)

        with nc.Block() as block:
            @block.sync
            def _(eng):
                body(eng, "sync")
                for key, gens in qfinal.items():
                    if gens:
                        eng.wait_ge(qsem[key], 16 * gens)

            @block.tensor
            def _(eng):
                body(eng, "tensor")

            @block.vector
            def _(eng):
                body(eng, "vector")

            @block.scalar
            def _(eng):
                body(eng, "scalar")

            @block.gpsimd
            def _(eng):
                body(eng, "gpsimd")


def region(bufs, c0, c1):
    return bufs[c0:c1]


class Ctx:
    pass


def build_nc(debug_taps=False):
    nc = bass.Bass("TRN2", target_bir_lowering=False, detect_race_conditions=False)
    C = Ctx()
    C.nc = nc
    P = Prog()
    C.P = P

    def din(name, shape, dt=F32):
        return nc.dram_tensor(name, list(shape), dt, kind="ExternalInput")

    C.xpref = din("xpref", [DM, PREF])
    C.xout = din("xout", [DM, XCOLS])
    C.wq = din("wq", [DM, DM]); C.wk = din("wk", [DM, DM])
    C.wv = din("wv", [DM, DM]); C.wo = din("wo", [DM, DM])
    C.Wqkv = din("Wqkv", [DEPTH * DM, 3 * DM])
    C.Wo6 = din("Wo6", [DEPTH * DM, DM])
    C.W1 = din("W1", [DEPTH * DM, DFF])
    C.W2 = din("W2", [DEPTH * DFF2, DM])
    C.qs6 = din("qs6", [DEPTH, DH]); C.ks6 = din("ks6", [DEPTH, DH])
    C.lng = din("lng", [DEPTH, DM]); C.lnb = din("lnb", [DEPTH, DM])
    C.b1 = din("b1", [DEPTH, DFF]); C.b2 = din("b2", [DEPTH, DM])
    C.TP = din("TP", [128, 2 * PREF])
    C.TO = din("TO", [128, 2 * XCOLS])
    C.TI = din("TI", [128, 4 * XCOLS], BF16)
    C.MSK = din("MSK", [128, 3 * 384], BF16)
    C.UT = din("UT", [128, 128], BF16)
    C.R2M = din("R2M", [128, 128], BF16)
    C.ROM = din("ROM", [128, 128], BF16)
    C.IDEN = din("IDEN", [128, 128], BF16)
    C.E2 = din("E2", [128, 2], BF16)
    C.CONSTS = din("CONSTS", [16])
    C.ONED = din("ONED", [64, 1])
    C.BASE2D = din("BASE2D", [128, 1])

    C.y = nc.dram_tensor("y", [DM, SL], F32, kind="ExternalOutput")

    C.xA = nc.dram_tensor("xA", [DM, XCOLS], F32)
    C.xB = nc.dram_tensor("xB", [DM, XCOLS], F32)
    C.out1 = nc.dram_tensor("out1d", [DM, SL], F32)
    C.wqkv_bf = nc.dram_tensor("wqkv_bf", [DEPTH * DM, 3 * DM], BF16)
    C.wo6_bf = nc.dram_tensor("wo6_bf", [DEPTH * DM, DM], BF16)
    C.w1_bf = nc.dram_tensor("w1_bf", [DEPTH * DM, DFF], BF16)
    C.w2_bf = nc.dram_tensor("w2_bf", [DEPTH * DFF2, DM], BF16)
    C.wqo_bf = nc.dram_tensor("wqo_bf", [4 * DM, DM], BF16)

    C.xA_b = [Buf(f"xA{c}") for c in range(NXC)]
    C.xB_b = [Buf(f"xB{c}") for c in range(NXC)]
    C.out1_b = [Buf(f"o1{c}") for c in range(SL // CH)]
    C.wbf_b = Buf("wbf")
    C.y_b = Buf("y")

    placed = []

    def SBAT(name, shape, dtype, off):
        nbytes = int(np.prod(shape[1:])) * (2 if dtype == BF16 else 4) if len(shape) > 1 else 4
        t = nc.alloc_sbuf_tensor_at(name, list(shape), dtype, offset=C.sb_base + off)
        b = Buf(name)
        for (o2, s2, b2) in placed:
            if off < o2 + s2 and o2 < off + nbytes:
                b.aliases.append(b2)
                b2.aliases.append(b)
        placed.append((off, nbytes, b))
        C.layout_max = max(getattr(C, "layout_max", 0), off + nbytes)
        return t, b

    C.sb_base = (nc.sbuf_base + 31) // 32 * 32
    cur = {"w": 0}

    def place(name, shape, dtype, region, pad=0):
        t, b = SBAT(name, shape, dtype, cur[region])
        nbytes = int(np.prod(shape[1:])) * (2 if dtype == BF16 else 4)
        cur[region] += ((nbytes + pad + 31) // 32) * 32
        return t, b

    C.Wqkv_sb, C.Wqkv_B = place("Wqkv_sb", [128, NFC * 3 * DM], BF16, "w")
    C.W1_sb, C.W1_B = place("W1_sb", [128, NFC * DFF], BF16, "w")
    C.W2_sb, C.W2_B = [], []
    for i in range(2):
        t, b = place(f"W2_sb{i}", [128, DM], BF16, "w")
        C.W2_sb.append(t); C.W2_B.append(b)
    C.WoB_sb, C.WoB_B = [], []
    for i in range(2):
        t, b = place(f"WoB{i}", [128, DM], BF16, "w")
        C.WoB_sb.append(t); C.WoB_B.append(b)
    W_END = cur["w"]
    assert W_END <= 144704 + 64, W_END

    cur["ow"] = 49152
    C.OWo_sb, C.OWo_B = place("OWo_sb", [128, NFC * DM], BF16, "ow")
    C.oxt, C.oxt_B = place("oxt", [128, NFC * 512], BF16, "ow")
    C.oyb, C.oyb_B = place("oyb", [128, NFC * 512], BF16, "ow")
    _ovb_off = cur["ow"]
    C.ovb, C.ovb_B = place("ovb", [128, NFC * 512], BF16, "ow")
    cur["ow_att"] = _ovb_off
    C.oatt, C.oatt_B = place("oatt", [128, NFC * 512], BF16, "ow_att")
    C.oQ, C.oQ_B = place("oQ", [128, NFC * 512], BF16, "ow")
    C.oK, C.oK_B = place("oK", [128, NFC * 512], BF16, "ow")
    C.oV, C.oV_B = place("oV", [128, 4 * NH * 65], BF16, "ow")
    C.oTt, C.oTt_B = [], []
    for i in range(2):
        t, b = place(f"oTt{i}", [128, 2 * 512], F32, "ow")
        C.oTt.append(t); C.oTt_B.append(b)
    C.odrow, C.odrow_B = place("odrow", [1, 128], F32, "ow")
    C.orc, C.orc_B = place("orc", [1, 128], BF16, "ow")
    C.orb, C.orb_B = place("orb", [64, 128], F32, "ow")
    C.oKmS, C.oKmS_B = place("oKmS", [128, NFC * 128], BF16, "ow")
    C.qc0, C.qc0_B = place("qc0", [128, SCH], BF16, "ow")
    C.kc0, C.kc0_B = place("kc0", [64, SCH], BF16, "ow")
    C.oSt, C.oSt_B = place("oSt", [64, NH * 65], F32, "ow")
    C.oStb, C.oStb_B = place("oStb", [128, NH * 65], BF16, "ow")
    C.tbig, C.tbig_B = [], []
    for i in range(2):
        t, b = place(f"tbig{i}", [128, 512], BF16, "ow")
        C.tbig.append(t); C.tbig_B.append(b)
    C.oSm, C.oSm_B = place("oSm", [128, 128], BF16, "ow")
    C.ofr, C.ofr_B = [], []
    for i in range(2):
        t, b = place(f"ofr{i}", [128, 512], F32, "ow")
        C.ofr.append(t); C.ofr_B.append(b)
    assert cur["ow"] <= 144704, cur["ow"]

    ACT = 144704
    cur["act"] = ACT
    C.sqb, C.sqb_B = place("sqb", [128, NFC * 512], BF16, "act")
    OV = cur["act"]
    cur["attn"] = OV
    C.xbf, C.xbf_B = place("xbf", [128, NFC * (CH + 1)], BF16, "attn")
    cur["attn_att"] = OV
    C.att, C.att_B = place("att", [128, NFC * CH], BF16, "attn_att")
    C.Qb, C.Qb_B = place("Qb", [128, NFC * CH], BF16, "attn")
    C.Kc, C.Kc_B = place("Kc", [128, NFC * CH], BF16, "attn")
    C.Kp, C.Kp_B = [], []
    for i in range(2):
        t, b = place(f"Kp{i}", [128, NFC * CH], BF16, "attn")
        C.Kp.append(t); C.Kp_B.append(b)
    C.Vkm, C.Vkm_B = [], []
    for i in range(2):
        t, b = place(f"Vkm{i}", [128, 2 * NH * DH], BF16, "attn")
        C.Vkm.append(t); C.Vkm_B.append(b)
    _sm_off = cur["attn"]
    C.Sm, C.Sm_B = place("Sm", [128, 384], F32, "attn")
    C.Pexp, C.Pexp_B = place("Pexp", [128, 384], BF16, "attn")
    C.Pt, C.Pt_B = place("Pt", [128, 384], BF16, "attn")
    C.TIt, C.TIt_B = place("TIt", [128, 4 * CH], BF16, "attn")
    _tq_off = cur["attn"]
    C.tq, C.tq_B = [], []
    for i in range(1):
        t, b = place(f"tq{i}", [128, CH], F32, "attn")
        C.tq.append(t); C.tq_B.append(b)
    C.tb, C.tb_B = [], []
    for i in range(3):
        t, b = place(f"tb{i}", [128, CH], BF16, "attn")
        C.tb.append(t); C.tb_B.append(b)
    C.sums, C.sums_B = place("sums", [128, 2 * NH], F32, "attn")
    C.recipc, C.recipc_B = place("recipc", [128, 2 * NH], F32, "attn")
    C.nmax, C.nmax_B = place("nmax", [128, 2], F32, "attn")
    C.xmo, C.xmo_B = [], []
    for i in range(2):
        t, b = place(f"xmo{i}", [128, 2 * CH], F32, "attn")
        C.xmo.append(t); C.xmo_B.append(b)
    cur["attn_afr0"] = _tq_off
    cur["attn_afr1"] = _sm_off
    C.afr, C.afr_B = [], []
    t, b = place("afr0", [128, 512], F32, "attn_afr0")
    C.afr.append(t); C.afr_B.append(b)
    t, b = place("afr1", [128, 512], F32, "attn_afr1")
    C.afr.append(t); C.afr_B.append(b)
    C.MSK_sb, C.MSK_B = place("MSK_sb", [128, 3 * 384], BF16, "attn")
    assert cur["attn"] <= ACT + 56064, cur["attn"]
    cur["ffn"] = OV
    C.xsb, C.xsb_B = place("xsb", [128, NFC * (CH + 1)], BF16, "ffn")
    C.xn, C.xn_B = place("xn", [128, NFC * CH], BF16, "ffn")
    C.h2i, C.h2i_B = [], []
    for i in range(2):
        t, b = place(f"h2i{i}", [128, CH], BF16, "ffn")
        C.h2i.append(t); C.h2i_B.append(b)
    C.gact, C.gact_B = [], []
    for i in range(2):
        t, b = place(f"gact{i}", [128, CH], BF16, "ffn")
        C.gact.append(t); C.gact_B.append(b)
    C.fres, C.fres_B = [], []
    for i in range(2):
        t, b = place(f"fres{i}", [128, 512], F32, "ffn")
        C.fres.append(t); C.fres_B.append(b)
    C.xnext, C.xnext_B = place("xnext", [128, NFC * CH], F32, "ffn")
    C.h2, C.h2_B = place("h2", [128, NOG1 * CH], BF16, "ffn")
    assert cur["ffn"] <= ACT + 56064, cur["ffn"]
    cur["oep"] = OV
    C.ozb, C.ozb_B = place("ozb", [128, NFC * 512], BF16, "oep")
    C.oo1, C.oo1_B = place("oo1", [128, NFC * 512], F32, "oep")
    C.oo1b, C.oo1b_B = place("oo1b", [128, NFC * 512], BF16, "oep")
    assert cur["oep"] <= ACT + 56064, cur["oep"]

    C.cursors = dict(cur)
    cur["fix"] = ((max(cur["attn"], cur["ffn"], cur["oep"]) + 31) // 32) * 32
    C.UT_sb, _b = place("UT_sb", [128, 128], BF16, "fix")
    C.R2_sb, _b = place("R2_sb", [128, 128], BF16, "fix")
    C.RO_sb, _b = place("RO_sb", [128, 128], BF16, "fix")
    C.ID_sb, _b = place("ID_sb", [128, 128], BF16, "fix")
    C.E2_sb, _b = place("E2_sb", [128, 2], BF16, "fix")
    C.ONES_sb, _b = place("ONES_sb", [128, 2], BF16, "fix")
    C.ONErow_sb, _b = place("ONErow_sb", [1, 128], BF16, "fix")
    C.CONST_sb, C.const_B = place("CONST_sb", [128, 16], F32, "fix")
    C.BASE2_sb, C.BASE2_B = place("BASE2_sb", [128, 1], F32, "fix")
    C.LQS_sb, C.LQS_B = place("LQS_sb", [2, 128], BF16, "fix")
    C.LKS_sb, C.LKS_B = place("LKS_sb", [2, 128], BF16, "fix")
    C.Lgb_sb, C.Lgb_B = place("Lgb_sb", [2, NFC * 128], BF16, "fix")
    C.gcol_sb, C.gcol_B = place("gcol_sb", [128, NFC], F32, "fix")
    C.b1a_sb, C.b1a_B = place("b1a_sb", [128, NOG1], F32, "fix")
    C.b1g_sb, C.b1g_B = place("b1g_sb", [128, NOG1], F32, "fix")
    C.b2c_sb, C.b2c_B = place("b2c_sb", [128, NFC], F32, "fix")
    C.rmu, C.rmu_B = place("rmu", [1, 512], F32, "fix")
    C.rvar, C.rvar_B = place("rvar", [1, 512], F32, "fix")
    C.rmsq, C.rmsq_B = place("rmsq", [1, 512], F32, "fix")
    C.rtmp, C.rtmp_B = C.rmsq, C.rmsq_B
    C.rowp, C.rowp_B = place("rowp", [2, 512], BF16, "fix")
    C.rstdb, C.rstdb_B = place("rstdb", [1, 512], BF16, "fix")
    C.rr, C.rr_B = place("rr", [2, CH], BF16, "fix")
    C.zcol, C.zcol_B = place("zcol", [128, 8], F32, "fix")
    C.lnt, C.lnt_B = place("lnt", [2, CH], F32, "fix")
    total = cur["fix"]
    avail = nc.sbuf_bytes_remaining if C.sb_base == nc.sbuf_base else 212863
    assert total <= 212863, f"SBUF layout over budget: {total}"

    cb = Buf("cbufs")
    C.cb = cb

    C.PS = [nc.alloc_psum_tensor(f"ps{i}", [128, 512], F32) for i in range(8)]
    C.PB = [Buf(f"ps{i}") for i in range(8)]

    def tap(name):
        if not debug_taps:
            return
        d = nc.dram_tensor(name, [DM, XCOLS], F32, kind="ExternalOutput")
        P.dma("gpsimd", (lambda d=d: lambda e: e.dma_start(out=d.ap(), in_=C.xA.ap()))(),
              reads=list(C.xA_b), writes=[], tag="tap")

    build_setup(C)
    build_prefix(C)
    build_outer(C)
    tap("dbg_x0")
    for ell in range(DEPTH):
        build_attn_layer(C, ell)
        if debug_taps and ell == 0:
            dxb = nc.dram_tensor("dbg_xb0", [DM, XCOLS], F32, kind="ExternalOutput")
            P.dma("gpsimd", lambda e: e.dma_start(out=dxb.ap(), in_=C.xB.ap()),
                  reads=list(C.xB_b), writes=[], tag="tapb")
        build_ffn_layer(C, ell)
        tap(f"dbg_x{ell + 1}")
    build_final(C)

    if debug_taps:
        C.dbg_out1 = nc.dram_tensor("dbg_out1", [DM, SL], F32, kind="ExternalOutput")
        P.dma(
            "gpsimd",
            lambda e: e.dma_start(out=C.dbg_out1.ap(), in_=C.out1.ap()),
            reads=list(C.out1_b),
            writes=[],
        )

    P.emit(nc)
    return nc, C


C_EPS24, C_EPS5, C_EPS6, C_I1024, C_MHALF, C_M1 = 0, 1, 2, 3, 4, 5


def ccol(C, i):
    return C.CONST_sb[:, i:i + 1]


def build_setup(C):
    P, nc = C.P, C.nc
    loads = [
        (C.UT_sb, C.UT), (C.R2_sb, C.R2M), (C.RO_sb, C.ROM),
        (C.ID_sb, C.IDEN), (C.E2_sb, C.E2),
    ]
    P.dma("sync", lambda e: e.dma_start(out=C.BASE2_sb[:, :], in_=C.BASE2D.ap()),
          writes=[C.BASE2_B], tag="constload")
    for sb, dr in loads:
        P.dma("sync", (lambda sb=sb, dr=dr: lambda e: e.dma_start(out=sb[:, :], in_=dr.ap()))(),
              writes=[C.cb], tag="constload")
    P.dma("gpsimd",
          lambda e: e.dma_start(out=C.CONST_sb[:, :],
                                in_=C.CONSTS.ap().unsqueeze(0).broadcast_to([128, 16])),
          writes=[C.const_B], tag="constload")
    P.op("vector", lambda e: e.memset(C.ONES_sb[:, :], 1.0), writes=[C.cb])
    P.op("vector", lambda e: e.memset(C.ONErow_sb[:, :], 1.0), writes=[C.cb])
    P.op("vector", lambda e: e.memset(C.rowp[0:2, :], -1.0), writes=[C.rowp_B])
    P.dma("gpsimd",
          lambda e: e.dma_start(
              out=C.oV[:, :].rearrange("p (en d) -> p en d", d=65)[:, :, 64:65],
              in_=C.ONED.ap().unsqueeze(0).broadcast_to([128, 64, 1]),
              allow_slow_non_contiguous=True),
          writes=[C.oV_B], tag="onedl")
    P.op("vector", lambda e: e.memset(C.oSt[:, :], 0.0), writes=[C.oSt_B])
    P.op("vector", lambda e: e.memset(C.zcol[:, :], 0.0), writes=[C.zcol_B])
    for ell in range(1, DEPTH):
        col = (14 - W_LEAD[ell]) * CH - 1
        P.dma("sync",
              (lambda col=col: lambda e: e.dma_start(
                  out=C.xA.ap().rearrange("(c p) w -> p c w", p=128)[:, :, col:col + 1],
                  in_=C.zcol[:, :].rearrange("p (c w) -> p c w", w=1),
                  allow_slow_non_contiguous=True))(),
              reads=[C.zcol_B],
              writes=region(C.xA_b, col // CH, col // CH + 1), tag="zc")
    P.op("vector", lambda e: e.memset(C.oStb[:, :], 0.0), writes=[C.oStb_B])
    P.op("vector", lambda e: e.memset(C.qc0[:, :], 0.0), writes=[C.qc0_B])

    pre = [
        (C.wqkv_bf, C.Wqkv), (C.wo6_bf, C.Wo6), (C.w1_bf, C.W1), (C.w2_bf, C.W2),
    ]
    for dst, src in pre:
        P.dma("gpsimd", (lambda dst=dst, src=src: lambda e: e.dma_start(out=dst.ap(), in_=src.ap()))(),
              writes=[C.wbf_b], tag="precast")
    for i, src in enumerate([C.wq, C.wk, C.wv, C.wo]):
        P.dma("gpsimd",
              (lambda i=i, src=src: lambda e: e.dma_start(
                  out=C.wqo_bf.ap()[i * DM:(i + 1) * DM, :], in_=src.ap()))(),
              writes=[C.wbf_b], tag="precast")

    for i in range(3):
        P.dma("sync",
              (lambda i=i: lambda e: e.dma_start(
                  out=C.Wqkv_sb[:, :].rearrange("p (w k c) -> p w k c", w=3, k=NFC)[:, i],
                  in_=C.wqo_bf.ap()[i * DM:(i + 1) * DM, :].rearrange("(k p) c -> p k c", p=128)))(),
              reads=[C.wbf_b], writes=[C.Wqkv_B], tag="owload")
    P.dma("sync",
          lambda e: e.dma_start(
              out=C.OWo_sb[:, :].rearrange("p (k c) -> p k c", k=NFC),
              in_=C.wqo_bf.ap()[3 * DM:4 * DM, :].rearrange("(k p) c -> p k c", p=128)),
          reads=[C.wbf_b], writes=[C.OWo_B], tag="owload")


def ln_rows(C, xsrc_bf, xsrc_buf, ncols, pb_sums, pb_sq, sq_tile, sq_buf, tag):
    P = C.P
    P.op("scalar",
         lambda e: e.activation(sq_tile[:, :NFC * ncols], xsrc_bf[:, :NFC * ncols], AF.Square),
         reads=[xsrc_buf], writes=[sq_buf], tag=tag + "_sq")
    for kc in range(NFC):
        P.op("tensor",
             (lambda kc=kc: lambda e: e.matmul(
                 C.PS[pb_sums][0:1, :ncols], C.ONES_sb[:, 0:1],
                 xsrc_bf[:, kc * ncols:(kc + 1) * ncols],
                 start=(kc == 0), stop=(kc == NFC - 1)))(),
             reads=[xsrc_buf, C.cb], writes=[C.PB[pb_sums]], tag=tag + "_s")
        P.op("tensor",
             (lambda kc=kc: lambda e: e.matmul(
                 C.PS[pb_sq][0:1, :ncols], C.ONES_sb[:, 0:1],
                 sq_tile[:, kc * ncols:(kc + 1) * ncols],
                 start=(kc == 0), stop=(kc == NFC - 1)))(),
             reads=[sq_buf, C.cb], writes=[C.PB[pb_sq]], tag=tag + "_sq2")


def ln_rows_finish(C, ncols, pb_sums, pb_sq, rows, rows_B, rstdb, rstdb_B, murow_dst, murow_buf, tag):
    P = C.P
    P.op("scalar",
         lambda e: e.activation(C.rmu[0:1, :ncols], C.PS[pb_sums][0:1, :ncols], AF.Copy,
                                scale=ccol(C, C_I1024)[0:1, :]),
         reads=[C.PB[pb_sums], C.const_B], writes=[C.rmu_B], tag=tag + "_mu")
    P.op("vector",
         lambda e: e.tensor_mul(C.rmsq[0:1, :ncols], C.rmu[0:1, :ncols], C.rmu[0:1, :ncols]),
         reads=[C.rmu_B], writes=[C.rmsq_B], tag=tag + "_musq")
    P.op("vector",
         lambda e: e.scalar_tensor_tensor(C.rvar[0:1, :ncols], C.PS[pb_sq][0:1, :ncols],
                                          ccol(C, C_I1024)[0:1, :], C.rmsq[0:1, :ncols],
                                          ALU.mult, ALU.subtract),
         reads=[C.PB[pb_sq], C.rmsq_B, C.const_B], writes=[C.rvar_B], tag=tag + "_var")
    P.op("scalar",
         lambda e: e.activation(C.rtmp[0:1, :ncols], C.rvar[0:1, :ncols], AF.Ln,
                                bias=ccol(C, C_EPS5)[0:1, :]),
         reads=[C.rvar_B, C.const_B], writes=[C.rtmp_B], tag=tag + "_ln")
    P.op("scalar",
         lambda e: e.activation(rstdb[0:1, :ncols], C.rtmp[0:1, :ncols], AF.Exp,
                                scale=ccol(C, C_MHALF)[0:1, :]),
         reads=[C.rtmp_B, C.const_B], writes=[rstdb_B], tag=tag + "_rstd")
    P.op("vector",
         lambda e: e.tensor_mul(murow_dst[0:1, :ncols], C.rmu[0:1, :ncols], rstdb[0:1, :ncols]),
         reads=[C.rmu_B, rstdb_B], writes=[murow_buf], tag=tag + "_mur")


def plain_ln(C, xb, xb_B, dst, dst_B, TT, tag):
    P = C.P
    ln_rows(C, xb, xb_B, TT, 0, 1, C.sqb, C.sqb_B, tag)
    ln_rows_finish(C, TT, 0, 1, None, None, C.rstdb, C.rstdb_B,
                   C.rowp, C.rowp_B, tag)
    P.op("tensor",
         lambda e: e.matmul(C.PS[2][:, :TT], C.ONErow_sb[0:1, :], C.rstdb[0:1, :TT],
                            start=True, stop=True),
         reads=[C.rstdb_B, C.cb], writes=[C.PB[2]], tag=tag + "_A")
    P.op("tensor",
         lambda e: e.matmul(C.PS[3][:, :TT], C.ONErow_sb[0:1, :], C.rowp[0:1, :TT],
                            start=True, stop=True),
         reads=[C.rowp_B, C.cb], writes=[C.PB[3]], tag=tag + "_C")
    for fc in range(NFC):
        P.op("vector",
             (lambda fc=fc: lambda e: e.tensor_mul(
                 dst[:, fc * TT:(fc + 1) * TT], xb[:, fc * TT:(fc + 1) * TT],
                 C.PS[2][:, :TT]))(),
             reads=[xb_B, C.PB[2]], writes=[dst_B], tag=tag + "_p1")
        P.op("vector",
             (lambda fc=fc: lambda e: e.tensor_sub(
                 dst[:, fc * TT:(fc + 1) * TT], dst[:, fc * TT:(fc + 1) * TT],
                 C.PS[3][:, :TT]))(),
             reads=[dst_B, C.PB[3]], writes=[dst_B], tag=tag + "_p2")


def build_prefix(C):
    P = C.P
    TT = 512
    ntiles = PREF // TT
    for t in range(ntiles):
        xb, xb_B = C.oxt, C.oxt_B
        P.dma("gpsimd",
              (lambda t=t, xb=xb: lambda e: e.dma_start(
                  out=xb[:, :].rearrange("p (c w) -> p c w", c=NFC),
                  in_=C.xpref.ap().rearrange("(c p) w -> p c w", p=128)[:, :, t * TT:(t + 1) * TT]))(),
              writes=[xb_B], tag="pf_x")
        tt, tt_B = C.oTt[t % 2], C.oTt_B[t % 2]
        P.dma("sync",
              (lambda t=t, tt=tt: lambda e: e.dma_start(
                  out=tt[:, :], in_=C.TP.ap()[:, t * 2 * TT:(t + 1) * 2 * TT]))(),
              writes=[tt_B], tag="pf_tt")
        plain_ln(C, xb, xb_B, C.oyb, C.oyb_B, TT, "pf_ln")
        kv_proj_scan(C, t, TT, C.oyb, C.oyb_B, tt, tt_B, with_q=False)


def kv_proj_scan(C, t, TT, ysrc, ysrc_B, tt, tt_B, with_q):
    P = C.P
    nsc = TT // SCH

    def wslot(i):
        return lambda kc, og: C.Wqkv_sb[:, (i * NFC + kc) * DM + og * 128:
                                        (i * NFC + kc) * DM + og * 128 + 128]

    proj_rope_outer(C, wslot(1), ysrc, ysrc_B, TT, tt, tt_B, sin_neg=False,
                    dst=C.oK, dst_B=C.oK_B, tag=f"kp{t}")
    for og in range(NFC):
        pb = 4 + og % 2
        for kc in range(NFC):
            P.op("tensor",
                 (lambda og=og, kc=kc, pb=pb: lambda e: e.matmul(
                     C.PS[pb][:, :TT], wslot(2)(kc, og),
                     ysrc[:, kc * TT:(kc + 1) * TT],
                     start=(kc == 0), stop=(kc == NFC - 1)))(),
                 reads=[ysrc_B, C.Wqkv_B], writes=[C.PB[pb]], tag=f"vproj{t}")
        P.op("scalar",
             (lambda og=og, pb=pb: lambda e: e.activation(
                 C.ovb[:, og * TT:(og + 1) * TT], C.PS[pb][:, :TT], AF.Copy))(),
             reads=[C.PB[pb]], writes=[C.ovb_B], tag=f"vevac{t}")
    for sc in range(nsc):
        for fc in range(NFC):
            pb = 6 + fc % 2
            P.op("tensor",
                 (lambda sc=sc, fc=fc, pb=pb: lambda e: e.transpose(
                     C.PS[pb].ap().bitcast(BF16)[:, 0:128],
                     C.ovb[:, fc * TT + sc * SCH: fc * TT + (sc + 1) * SCH],
                     C.ID_sb[:, :]))(),
                 reads=[C.ovb_B, C.cb], writes=[C.PB[pb]], tag=f"vT{t}")
            P.op("vector",
                 (lambda sc=sc, fc=fc, pb=pb: lambda e: e.tensor_copy(
                     C.oV[:, :].rearrange("p (en d) -> p en d", d=65)
                     [:, sc * NH + 2 * fc: sc * NH + 2 * fc + 2, 0:64],
                     C.PS[pb].ap().bitcast(BF16)[:, 0:128].rearrange(
                         "p (en d) -> p en d", d=DH)))(),
                 reads=[C.PB[pb]], writes=[C.oV_B], tag=f"vTe{t}")
    if with_q:
        proj_rope_outer(C, wslot(0), ysrc, ysrc_B, TT, tt, tt_B, sin_neg=False,
                        dst=C.oQ, dst_B=C.oQ_B, tag=f"qp{t}")
    for sc in range(nsc):
        for fc in range(NFC):
            pb = 6 + fc % 2
            P.op("tensor",
                 (lambda sc=sc, fc=fc, pb=pb: lambda e: e.transpose(
                     C.PS[pb].ap().bitcast(BF16)[:, 0:128],
                     C.oK[:, fc * TT + sc * SCH: fc * TT + (sc + 1) * SCH],
                     C.ID_sb[:, :]))(),
                 reads=[C.oK_B, C.cb], writes=[C.PB[pb]], tag=f"kT{t}")
            P.op("vector",
                 (lambda fc=fc, pb=pb: lambda e: e.tensor_copy(
                     C.oKmS[:, fc * 128:(fc + 1) * 128],
                     C.PS[pb].ap().bitcast(BF16)[:, 0:128]))(),
                 reads=[C.PB[pb]], writes=[C.oKmS_B], tag=f"kTe{t}")
        for h in range(NH):
            scan_chunk(C, t, sc, h, TT, with_q)


def proj_rope_outer(C, wsl, ysrc, ysrc_B, TT, tt, tt_B, sin_neg, dst, dst_B, tag):
    P = C.P
    for og in range(NFC):
        pb = 4 + og % 2
        for kc in range(NFC):
            P.op("tensor",
                 (lambda og=og, kc=kc, pb=pb: lambda e: e.matmul(
                     C.PS[pb][:, :TT], wsl(kc, og), ysrc[:, kc * TT:(kc + 1) * TT],
                     start=(kc == 0), stop=(kc == NFC - 1)))(),
                 reads=[ysrc_B, C.Wqkv_B], writes=[C.PB[pb]], tag=tag + "_mm")
        P.op("scalar",
             (lambda og=og, pb=pb: lambda e: e.activation(
                 dst[:, og * TT:(og + 1) * TT], C.PS[pb][:, :TT], AF.Copy))(),
             reads=[C.PB[pb]], writes=[dst_B], tag=tag + "_ev")
        P.op("tensor",
             (lambda og=og: lambda e: e.matmul(
                 C.PS[6][:, :TT], C.RO_sb[:, :], dst[:, og * TT:(og + 1) * TT],
                 start=True, stop=True))(),
             reads=[dst_B, C.cb], writes=[C.PB[6]], tag=tag + "_ro")
        P.op("vector",
             (lambda og=og: lambda e: e.tensor_mul(
                 C.tbig[0][:, :TT], C.PS[6][:, :TT], tt[:, TT:2 * TT]))(),
             reads=[C.PB[6], tt_B], writes=[C.tbig_B[0]], tag=tag + "_t2")
        P.op("vector",
             (lambda og=og: lambda e: e.tensor_mul(
                 C.tbig[1][:, :TT], dst[:, og * TT:(og + 1) * TT], tt[:, 0:TT]))(),
             reads=[dst_B, tt_B], writes=[C.tbig_B[1]], tag=tag + "_t1")
        P.op("vector",
             (lambda og=og: lambda e: e.tensor_add(
                 C.tbig[1][:, :TT], C.tbig[1][:, :TT], C.tbig[0][:, :TT]))(),
             reads=[C.tbig_B[1], C.tbig_B[0]], writes=[C.tbig_B[1]], tag=tag + "_ad")
        P.op("scalar",
             (lambda og=og: lambda e: e.activation(
                 dst[:, og * TT:(og + 1) * TT], C.tbig[1][:, :TT], AF.Relu))(),
             reads=[C.tbig_B[1]], writes=[dst_B], tag=tag + "_rl")


def scan_chunk(C, t, sc, h, TT, with_q):
    P = C.P
    fc, hp = h // 2, h % 2
    r0 = hp * 64
    kcols = lambda buf: buf[r0:r0 + 64, fc * TT + sc * SCH: fc * TT + (sc + 1) * SCH]
    ent = sc * NH + h
    vkm = C.oV[:, :].rearrange("p (en d) -> p en d", d=65)[:, ent]
    stf = C.oSt[:, h * 65:(h + 1) * 65]
    stb = C.oStb[:, h * 65:(h + 1) * 65]
    stb_lo = C.oStb[0:64, h * 65:(h + 1) * 65]

    P.op("scalar",
         lambda e: e.copy(C.kc0[:, :], kcols(C.oK)),
         reads=[C.oK_B], writes=[C.kc0_B], tag=f"kcp{t}")
    if with_q:
        P.op("scalar",
             lambda e: e.copy(C.qc0[0:64, :], kcols(C.oQ)),
             reads=[C.oQ_B], writes=[C.qc0_B], tag=f"qcp{t}")
        P.op("tensor",
             lambda e: e.matmul(C.PS[5][:, 0:128], C.kc0[:, :], C.qc0[0:64, :],
                                start=True, stop=True),
             reads=[C.kc0_B, C.qc0_B], writes=[C.PB[5]], tag=f"sT{t}")
        P.op("vector",
             lambda e: e.tensor_mul(C.oSm[:, :], C.PS[5][:, 0:128], C.UT_sb[:, :]),
             reads=[C.PB[5], C.cb], writes=[C.oSm_B], tag=f"sTm{t}")
        P.op("tensor",
             lambda e: e.matmul(C.PS[6][0:65, 0:128], stb, C.qc0[:, :],
                                start=True, stop=False),
             reads=[C.oStb_B, C.qc0_B], writes=[C.PB[6]], tag=f"nd1{t}")
        P.op("tensor",
             lambda e: e.matmul(C.PS[6][0:65, 0:128], vkm, C.oSm[:, :],
                                start=False, stop=True),
             reads=[C.oV_B, C.oSm_B], writes=[C.PB[6]], tag=f"nd2{t}")
        P.op("scalar",
             lambda e: e.copy(C.odrow[0:1, 0:128], C.PS[6][64:65, 0:128]),
             reads=[C.PB[6]], writes=[C.odrow_B], tag=f"dcp{t}")
        P.op("scalar",
             lambda e: e.activation(C.odrow[0:1, 0:128],
                                    C.odrow[0:1, 0:128], AF.Ln,
                                    bias=ccol(C, C_EPS6)[0:1, :]),
             reads=[C.odrow_B, C.const_B], writes=[C.odrow_B], tag=f"dln{t}")
        P.op("scalar",
             lambda e: e.activation(C.orc[0:1, 0:128],
                                    C.odrow[0:1, 0:128], AF.Exp,
                                    scale=ccol(C, C_M1)[0:1, :]),
             reads=[C.odrow_B, C.const_B], writes=[C.orc_B], tag=f"dex{t}")
        P.op("tensor",
             lambda e: e.matmul(C.PS[7][0:64, 0:128], C.ONErow_sb[0:1, 0:64],
                                C.orc[0:1, 0:128], start=True, stop=True),
             reads=[C.orc_B, C.cb], writes=[C.PB[7]], tag=f"dbc{t}")
        P.op("scalar",
             lambda e: e.copy(C.orb[:, :], C.PS[7][0:64, 0:128]),
             reads=[C.PB[7]], writes=[C.orb_B], tag=f"dbe{t}")
        P.op("vector",
             lambda e: e.tensor_mul(
                 C.oatt[r0:r0 + 64, fc * TT + sc * SCH: fc * TT + (sc + 1) * SCH],
                 C.PS[6][0:64, 0:128], C.orb[:, :]),
             reads=[C.PB[6], C.orb_B], writes=[C.oatt_B], tag=f"dmu{t}")

    P.op("tensor",
         lambda e: e.matmul(C.PS[4][0:64, 0:65],
                            C.oKmS[:, fc * 128 + r0: fc * 128 + r0 + 64], vkm,
                            start=True, stop=True),
         reads=[C.oKmS_B, C.oV_B], writes=[C.PB[4]], tag=f"cM{t}")
    P.op("vector",
         lambda e: e.tensor_add(stf, stf, C.PS[4][0:64, 0:65]),
         reads=[C.oSt_B, C.PB[4]], writes=[C.oSt_B], tag=f"stu{t}")
    P.op("scalar",
         lambda e: e.copy(stb_lo, stf),
         reads=[C.oSt_B], writes=[C.oStb_B], tag=f"stc{t}")


def build_outer(C):
    P = C.P
    TT = 512
    ntiles = XCOLS // TT
    for t in range(ntiles):
        xb, xb_B = C.oxt, C.oxt_B
        P.dma("gpsimd",
              (lambda t=t, xb=xb: lambda e: e.dma_start(
                  out=xb[:, :].rearrange("p (c w) -> p c w", c=NFC),
                  in_=C.xout.ap().rearrange("(c p) w -> p c w", p=128)[:, :, t * TT:(t + 1) * TT]))(),
              writes=[xb_B], tag="ot_x")
        tt, tt_B = C.oTt[t % 2], C.oTt_B[t % 2]
        P.dma("sync",
              (lambda t=t, tt=tt: lambda e: e.dma_start(
                  out=tt[:, :], in_=C.TO.ap()[:, t * 2 * TT:(t + 1) * 2 * TT]))(),
              writes=[tt_B], tag="ot_tt")
        plain_ln(C, xb, xb_B, C.oyb, C.oyb_B, TT, "ot_ln")
        kv_proj_scan(C, 100 + t, TT, C.oyb, C.oyb_B, tt, tt_B, with_q=True)
        for og in range(NFC):
            pb = 4 + og % 2
            for kc in range(NFC):
                P.op("tensor",
                     (lambda og=og, kc=kc, pb=pb: lambda e: e.matmul(
                         C.PS[pb][:, :TT],
                         C.OWo_sb[:, kc * DM + og * 128: kc * DM + og * 128 + 128],
                         C.oatt[:, kc * TT:(kc + 1) * TT],
                         start=(kc == 0), stop=(kc == NFC - 1)))(),
                     reads=[C.oatt_B, C.OWo_B], writes=[C.PB[pb]], tag="ot_wo")
            P.op("scalar",
                 (lambda og=og, pb=pb: lambda e: e.activation(
                     C.ozb[:, og * TT:(og + 1) * TT], C.PS[pb][:, :TT], AF.Copy))(),
                 reads=[C.PB[pb]], writes=[C.ozb_B], tag="ot_ze")
        plain_ln(C, C.ozb, C.ozb_B, C.oo1, C.oo1_B, TT, "ot_l2")
        for fc in range(NFC):
            fr, fr_B = C.ofr[fc % 2], C.ofr_B[fc % 2]
            P.dma("sync",
                  (lambda t=t, fc=fc, fr=fr: lambda e: e.dma_start(
                      out=fr[:, :],
                      in_=C.xout.ap()[fc * 128:(fc + 1) * 128, t * TT:(t + 1) * TT]))(),
                  writes=[fr_B], tag="ot_fr")
            P.op("vector",
                 (lambda fc=fc, fr=fr: lambda e: e.tensor_add(
                     C.oo1[:, fc * TT:(fc + 1) * TT], C.oo1[:, fc * TT:(fc + 1) * TT],
                     fr[:, :]))(),
                 reads=[C.oo1_B, fr_B], writes=[C.oo1_B], tag="ot_o1c")
        if (t + 1) * TT > OLEAD:
            lo = max(0, OLEAD - t * TT)
            P.dma("sync",
                  (lambda t=t, lo=lo: lambda e: e.dma_start(
                      out=C.out1.ap().rearrange("(c p) w -> p c w", p=128)
                      [:, :, t * TT - OLEAD + lo: (t + 1) * TT - OLEAD],
                      in_=C.oo1[:, :].rearrange("p (c w) -> p c w", c=NFC)[:, :, lo:TT]))(),
                  reads=[C.oo1_B],
                  writes=region(C.out1_b, (t * TT - OLEAD + lo) // CH,
                                ((t + 1) * TT - OLEAD) // CH),
                  tag="ot_o1s")
        P.op("scalar", lambda e: e.activation(C.oo1b[:, :], C.oo1[:, :], AF.Copy),
             reads=[C.oo1_B], writes=[C.oo1b_B], tag="ot_x0c")
        plain_ln(C, C.oo1b, C.oo1b_B, C.oo1, C.oo1_B, TT, "ot_l3")
        P.dma("sync",
              (lambda t=t: lambda e: e.dma_start(
                  out=C.xA.ap().rearrange("(c p) w -> p c w", p=128)[:, :, t * TT:(t + 1) * TT],
                  in_=C.oo1[:, :].rearrange("p (c w) -> p c w", c=NFC)))(),
              reads=[C.oo1_B],
              writes=region(C.xA_b, t * TT // CH, (t + 1) * TT // CH),
              tag="ot_x0s")


def _wqkv(C, third, sub, kc):
    o = kc * 3072 + third * 1024 + sub * 128
    return C.Wqkv_sb[:, o:o + 128]


def _xs_slice(C, tile, fc, ncol=CH):
    off = fc * (CH + 1) + (1 if fc < 4 else 0)
    return tile[:, off:off + ncol]


def _ld_layer_params(C, ell):
    P = C.P
    P.op("vector", lambda e: e.memset(C.LQS_sb[:, :], 0.0), writes=[C.LQS_B], tag=f"L{ell}_pz")
    P.op("vector", lambda e: e.memset(C.LKS_sb[:, :], 0.0), writes=[C.LKS_B], tag=f"L{ell}_pz")
    P.dma("gpsimd", lambda e: e.dma_start(out=C.LQS_sb[0:1, 0:64],
                                          in_=C.qs6.ap()[ell:ell + 1, :]),
          writes=[C.LQS_B], tag=f"L{ell}_pq")
    P.dma("gpsimd", lambda e: e.dma_start(out=C.LQS_sb[1:2, 64:128],
                                          in_=C.qs6.ap()[ell:ell + 1, :]),
          writes=[C.LQS_B], tag=f"L{ell}_pq")
    P.dma("gpsimd", lambda e: e.dma_start(out=C.LKS_sb[0:1, 0:64],
                                          in_=C.ks6.ap()[ell:ell + 1, :]),
          writes=[C.LKS_B], tag=f"L{ell}_pk")
    P.dma("gpsimd", lambda e: e.dma_start(out=C.LKS_sb[1:2, 64:128],
                                          in_=C.ks6.ap()[ell:ell + 1, :]),
          writes=[C.LKS_B], tag=f"L{ell}_pk")
    P.dma("gpsimd", lambda e: e.dma_start(out=C.Lgb_sb[0:1, :],
                                          in_=C.lng.ap()[ell:ell + 1, :]),
          writes=[C.Lgb_B], tag=f"L{ell}_pg")
    P.dma("gpsimd", lambda e: e.dma_start(out=C.Lgb_sb[1:2, :],
                                          in_=C.lnb.ap()[ell:ell + 1, :]),
          writes=[C.Lgb_B], tag=f"L{ell}_pg")
    P.dma("sync", lambda e: e.dma_start(
        out=C.gcol_sb[:, :],
        in_=C.lng.ap()[ell].rearrange("(f p) -> p f", p=128),
        allow_slow_non_contiguous=True),
        writes=[C.gcol_B], tag=f"L{ell}_pg2")
    P.dma("sync", lambda e: e.dma_start(
        out=C.b1a_sb[:, 0:21],
        in_=C.b1.ap()[ell, 0:2688].rearrange("(og p) -> p og", p=128),
        allow_slow_non_contiguous=True),
        writes=[C.b1a_B], tag=f"L{ell}_pb")
    P.dma("sync", lambda e: e.dma_start(
        out=C.b1a_sb[0:42, 21:22],
        in_=C.b1.ap()[ell, 2688:2730].rearrange("(p o) -> p o", o=1)),
        writes=[C.b1a_B], tag=f"L{ell}_pb")
    P.dma("sync", lambda e: e.dma_start(
        out=C.b1g_sb[:, 0:21],
        in_=C.b1.ap()[ell, 2730:5418].rearrange("(og p) -> p og", p=128),
        allow_slow_non_contiguous=True),
        writes=[C.b1g_B], tag=f"L{ell}_pb")
    P.dma("sync", lambda e: e.dma_start(
        out=C.b1g_sb[0:42, 21:22],
        in_=C.b1.ap()[ell, 5418:5460].rearrange("(p o) -> p o", o=1)),
        writes=[C.b1g_B], tag=f"L{ell}_pb")
    P.dma("sync", lambda e: e.dma_start(
        out=C.b2c_sb[:, :],
        in_=C.b2.ap()[ell].rearrange("(f p) -> p f", p=128),
        allow_slow_non_contiguous=True),
        writes=[C.b2c_B], tag=f"L{ell}_pb2")


def _rope_inner(C, ell, bc, which, fc, pb_src, tag):
    P = C.P
    par = bc % 2
    LSB, LSB_B = (C.LQS_sb, C.LQS_B) if which == "q" else (C.LKS_sb, C.LKS_B)
    dst, dst_B = (C.Qb, C.Qb_B) if which == "q" else (C.Kc, C.Kc_B)
    toff = 0 if which == "q" else 2 * CH
    P.op("scalar",
         lambda e: e.activation(C.tq[0][:, :], C.PS[pb_src][:, 0:CH], AF.Copy),
         reads=[C.PB[pb_src]], writes=[C.tq_B[0]], tag=tag + "ev")
    P.op("scalar",
         lambda e: e.activation(C.Pt[:, 0:CH], C.tq[0][:, :], AF.Square),
         reads=[C.tq_B[0]], writes=[C.Pt_B], tag=tag + "sq")
    P.op("tensor",
         lambda e: e.matmul(C.PS[2][0:2, 0:CH], C.E2_sb[:, :], C.Pt[:, 0:CH],
                            start=True, stop=True),
         reads=[C.Pt_B, C.cb], writes=[C.PB[2]], tag=tag + "ss")
    P.op("scalar",
         lambda e: e.activation(C.lnt[0:2, 0:CH], C.PS[2][0:2, 0:CH], AF.Ln,
                                bias=ccol(C, C_EPS24)[0:2, :]),
         reads=[C.PB[2], C.const_B], writes=[C.lnt_B], tag=tag + "ln")
    P.op("scalar",
         lambda e: e.activation(C.rr[0:2, 0:CH], C.lnt[0:2, 0:CH], AF.Exp,
                                scale=ccol(C, C_MHALF)[0:2, :]),
         reads=[C.lnt_B, C.const_B], writes=[C.rr_B], tag=tag + "rs")
    P.op("tensor",
         lambda e: e.matmul(C.PS[3][:, 0:CH], LSB[:, :], C.rr[0:2, 0:CH],
                            start=True, stop=True),
         reads=[C.rr_B, LSB_B], writes=[C.PB[3]], tag=tag + "bc")
    P.op("vector",
         lambda e: e.tensor_mul(C.tb[0][:, :], C.tq[0][:, :], C.PS[3][:, 0:CH]),
         reads=[C.tq_B[0], C.PB[3]], writes=[C.tb_B[0]], tag=tag + "qn")
    P.op("tensor",
         lambda e: e.matmul(C.PS[2][:, 0:CH], C.R2_sb[:, :], C.tb[0][:, :],
                            start=True, stop=True),
         reads=[C.tb_B[0], C.cb], writes=[C.PB[2]], tag=tag + "rq")
    P.op("vector",
         lambda e: e.tensor_mul(C.tb[1][:, :], C.tb[0][:, :],
                                C.TIt[:, toff:toff + CH]),
         reads=[C.tb_B[0], C.TIt_B], writes=[C.tb_B[1]], tag=tag + "t1")
    P.op("vector",
         lambda e: e.tensor_mul(C.tb[2][:, :], C.PS[2][:, 0:CH],
                                C.TIt[:, toff + CH:toff + 2 * CH]),
         reads=[C.PB[2], C.TIt_B], writes=[C.tb_B[2]], tag=tag + "t2")
    P.op("vector",
         (lambda fc=fc: lambda e: e.tensor_add(
             dst[:, fc * CH:(fc + 1) * CH], C.tb[1][:, :], C.tb[2][:, :]))(),
         reads=[C.tb_B[1], C.tb_B[2]], writes=[dst_B], tag=tag + "o")
    if which == "k":
        P.op("vector",
             (lambda fc=fc, par=par: lambda e: e.tensor_scalar_mul(
                 C.Kp[par][:, fc * CH:(fc + 1) * CH],
                 C.Kc[:, fc * CH:(fc + 1) * CH], C.BASE2_sb[:, 0:1]))(),
             reads=[dst_B, C.BASE2_B], writes=[C.Kp_B[par]], tag=tag + "kp")


def build_attn_layer(C, ell):
    P = C.P
    xsrc, xsrc_b = C.xA, C.xA_b
    xdst, xdst_b = C.xB, C.xB_b
    AQ0 = 15 - W_LEAD[ell]
    _ld_layer_params(C, ell)
    P.dma("sync",
          lambda e: e.dma_start(
              out=C.Wqkv_sb[:, :].rearrange("p (k c) -> p k c", k=NFC),
              in_=C.wqkv_bf.ap()[ell * DM:(ell + 1) * DM, :].rearrange(
                  "(k p) c -> p k c", p=128)),
          reads=[C.wbf_b], writes=[C.Wqkv_B], tag=f"aL{ell}_w")
    P.dma("sync", lambda e: e.dma_start(out=C.MSK_sb[:, :], in_=C.MSK.ap()),
          writes=[C.MSK_B], tag=f"aL{ell}_m")
    for bc in range(AQ0 - 1, NXC):
        attn_chunk(C, ell, bc, window=(bc >= AQ0))


def attn_chunk(C, ell, bc, window):
    P = C.P
    par = bc % 2
    xsrc, xsrc_b = C.xA, C.xA_b
    xdst, xdst_b = C.xB, C.xB_b
    tg = f"aL{ell}c{bc}_"
    P.dma("gpsimd",
          (lambda bc=bc: lambda e: e.dma_start(
              out=C.xbf[:, :].rearrange("p (c w) -> p c w", c=NFC),
              in_=xsrc.ap().rearrange("(c p) w -> p c w", p=128)
              [:, :, bc * CH - 1: (bc + 1) * CH]))(),
          reads=region(xsrc_b, bc - 1, bc + 1), writes=[C.xbf_B], tag=tg + "x")
    P.dma("sync",
          (lambda bc=bc: lambda e: e.dma_start(
              out=C.TIt[:, :], in_=C.TI.ap()[:, bc * 4 * CH:(bc + 1) * 4 * CH]))(),
          writes=[C.TIt_B], tag=tg + "tt")
    for og in range(24):
        third, sub = og // 8, og % 8
        pb = og % 2
        for kc in range(NFC):
            P.op("tensor",
                 (lambda third=third, sub=sub, kc=kc, pb=pb: lambda e: e.matmul(
                     C.PS[pb][:, 0:CH], _wqkv(C, third, sub, kc),
                     _xs_slice(C, C.xbf, kc),
                     start=(kc == 0), stop=(kc == NFC - 1)))(),
                 reads=[C.xbf_B, C.Wqkv_B], writes=[C.PB[pb]], tag=tg + "mm")
        if third == 0:
            _rope_inner(C, ell, bc, "q", sub, pb, tg + f"rq{sub}")
        elif third == 1:
            _rope_inner(C, ell, bc, "k", sub, pb, tg + f"rk{sub}")
        else:
            fc = sub
            P.op("scalar",
                 (lambda pb=pb: lambda e: e.activation(
                     C.Pexp[:, 0:CH], C.PS[pb][:, 0:CH], AF.Copy))(),
                 reads=[C.PB[pb]], writes=[C.Pexp_B], tag=tg + "vev")
            for kb in range(2):
                pbt = 6 + kb
                P.op("tensor",
                     (lambda kb=kb, pbt=pbt: lambda e: e.transpose(
                         C.PS[pbt].ap().bitcast(BF16)[:, 0:128],
                         C.Pexp[:, kb * 128:(kb + 1) * 128], C.ID_sb[:, :]))(),
                     reads=[C.Pexp_B, C.cb], writes=[C.PB[pbt]], tag=tg + "vT")
                P.op("vector",
                     (lambda fc=fc, kb=kb, par=par, pbt=pbt: lambda e: e.tensor_copy(
                         C.Vkm[par][:, :].rearrange("p (en d) -> p en d", d=DH)
                         [:, kb * NH + 2 * fc: kb * NH + 2 * fc + 2, :],
                         C.PS[pbt].ap().bitcast(BF16)[:, 0:128].rearrange(
                             "p (en d) -> p en d", d=DH)))(),
                     reads=[C.PB[pbt]], writes=[C.Vkm_B[par]], tag=tg + "vTe")
    if window:
        first = (bc == LEADC)
        for h in range(NH):
            attn_window_head(C, ell, bc, h, first, tg)
    if window:
        attn_wo(C, ell, bc, tg)


def attn_window_head(C, ell, bc, h, first, tg):
    P = C.P
    par = bc % 2
    fc, hp = h // 2, h % 2
    r0 = hp * 64
    for qh in range(2):
        j = h * 2 + qh
        pb_s = 4 + (j % 2)
        qsl = C.Qb[r0:r0 + 64, fc * CH + qh * 128: fc * CH + qh * 128 + 128]
        kp = C.Kp[1 - par]
        if qh == 0:
            P.op("tensor",
                 (lambda pb_s=pb_s, qsl=qsl, kp=kp, r0=r0, fc=fc: lambda e: e.matmul(
                     C.PS[pb_s][:, 0:256], qsl,
                     kp[r0:r0 + 64, fc * CH: fc * CH + 256], start=True, stop=True))(),
                 reads=[C.Qb_B, C.Kp_B[1 - par]], writes=[C.PB[pb_s]], tag=tg + "s1")
            P.op("tensor",
                 (lambda pb_s=pb_s, qsl=qsl, r0=r0, fc=fc: lambda e: e.matmul(
                     C.PS[pb_s][:, 256:384], qsl,
                     C.Kc[r0:r0 + 64, fc * CH: fc * CH + 128], start=True, stop=True))(),
                 reads=[C.Qb_B, C.Kc_B], writes=[C.PB[pb_s]], tag=tg + "s2")
        else:
            P.op("tensor",
                 (lambda pb_s=pb_s, qsl=qsl, kp=kp, r0=r0, fc=fc: lambda e: e.matmul(
                     C.PS[pb_s][:, 0:128], qsl,
                     kp[r0:r0 + 64, fc * CH + 128: fc * CH + 256], start=True, stop=True))(),
                 reads=[C.Qb_B, C.Kp_B[1 - par]], writes=[C.PB[pb_s]], tag=tg + "s1")
            P.op("tensor",
                 (lambda pb_s=pb_s, qsl=qsl, r0=r0, fc=fc: lambda e: e.matmul(
                     C.PS[pb_s][:, 128:384], qsl,
                     C.Kc[r0:r0 + 64, fc * CH: fc * CH + 256], start=True, stop=True))(),
                 reads=[C.Qb_B, C.Kc_B], writes=[C.PB[pb_s]], tag=tg + "s2")
        mblk = (1 + qh) if first else 0
        P.op("vector",
             (lambda pb_s=pb_s, mblk=mblk: lambda e: e.tensor_add(
                 C.Sm[:, :], C.PS[pb_s][:, 0:384], C.MSK_sb[:, mblk * 384:(mblk + 1) * 384]))(),
             reads=[C.PB[pb_s], C.MSK_B], writes=[C.Sm_B], tag=tg + "msk")
        nmc = j % 2
        P.op("vector",
             (lambda nmc=nmc: lambda e: e.tensor_reduce(
                 C.nmax[:, nmc:nmc + 1], C.Sm[:, :], AX.X, ALU.max, negate=True))(),
             reads=[C.Sm_B], writes=[C.nmax_B], tag=tg + "mx")
        P.op("scalar",
             (lambda nmc=nmc, j=j: lambda e: e.activation(
                 C.Pexp[:, :], C.Sm[:, :], AF.Exp, bias=C.nmax[:, nmc:nmc + 1],
                 accum_out=C.sums[:, j:j + 1]))(),
             reads=[C.Sm_B, C.nmax_B], writes=[C.Pexp_B, C.sums_B], tag=tg + "ex")
        P.op("vector",
             (lambda j=j: lambda e: e.reciprocal(C.recipc[:, j:j + 1], C.sums[:, j:j + 1]))(),
             reads=[C.sums_B], writes=[C.recipc_B], tag=tg + "rc")
        P.op("scalar",
             (lambda j=j: lambda e: e.activation(
                 C.Pexp[:, :], C.Pexp[:, :], AF.Copy,
                 scale=C.recipc[:, j:j + 1]))(),
             reads=[C.Pexp_B, C.recipc_B], writes=[C.Pexp_B], tag=tg + "dv")
        for kb in range(3):
            P.op("tensor",
                 (lambda kb=kb: lambda e: e.transpose(
                     C.PS[6].ap().bitcast(BF16)[:, kb * 128:(kb + 1) * 128],
                     C.Pexp[:, kb * 128:(kb + 1) * 128], C.ID_sb[:, :]))(),
                 reads=[C.Pexp_B, C.cb], writes=[C.PB[6]], tag=tg + "pT")
        P.op("vector",
             lambda e: e.tensor_copy(C.Pt[:, :], C.PS[6].ap().bitcast(BF16)[:, 0:384]),
             reads=[C.PB[6]], writes=[C.Pt_B], tag=tg + "pTe")
        blocks = [(1 - par, 0), (1 - par, 1), (par, 0)] if qh == 0 else \
                 [(1 - par, 1), (par, 0), (par, 1)]
        for i3, (vp, kb) in enumerate(blocks):
            P.op("tensor",
                 (lambda i3=i3, vp=vp, kb=kb, r0=r0, h=h: lambda e: e.matmul(
                     C.PS[7][r0:r0 + 64, 0:128],
                     C.Vkm[vp][:, (kb * NH + h) * DH:(kb * NH + h) * DH + DH],
                     C.Pt[:, i3 * 128:(i3 + 1) * 128],
                     start=(i3 == 0), stop=(i3 == 2), tile_position=(0, r0)))(),
                 reads=[C.Vkm_B[vp], C.Pt_B], writes=[C.PB[7]], tag=tg + "pv")
        P.op("scalar",
             (lambda r0=r0, fc=fc, qh=qh: lambda e: e.activation(
                 C.att[r0:r0 + 64, fc * CH + qh * 128: fc * CH + qh * 128 + 128],
                 C.PS[7][r0:r0 + 64, 0:128], AF.Copy))(),
             reads=[C.PB[7]], writes=[C.att_B], tag=tg + "ave")


def attn_wo(C, ell, bc, tg):
    P = C.P
    par = bc % 2
    xsrc, xsrc_b = C.xA, C.xA_b
    xdst, xdst_b = C.xB, C.xB_b
    for i in range(4):
        fr, fr_B = C.afr[i % 2], C.afr_B[i % 2]
        P.dma("sync",
              (lambda i=i, bc=bc, fr=fr: lambda e: e.dma_start(
                  out=fr[:, :].rearrange("p (c w) -> p c w", c=2),
                  in_=xsrc.ap().rearrange("(c p) w -> p c w", p=128)
                  [:, 2 * i:2 * i + 2, bc * CH:(bc + 1) * CH]))(),
              reads=region(xsrc_b, bc, bc + 1), writes=[fr_B], tag=tg + "fr")
        for k in range(2):
            og = 2 * i + k
            pb = og % 2
            wb, wb_B = C.WoB_sb[og % 2], C.WoB_B[og % 2]
            P.dma("sync",
                  (lambda og=og, wb=wb: lambda e: e.dma_start(
                      out=wb[:, :].rearrange("p (k c) -> p k c", k=NFC),
                      in_=C.wo6_bf.ap()[ell * DM:(ell + 1) * DM,
                                        og * 128:(og + 1) * 128].rearrange(
                          "(k p) c -> p k c", p=128)))(),
                  reads=[C.wbf_b], writes=[wb_B], tag=tg + "wol")
            for kc in range(NFC):
                P.op("tensor",
                     (lambda og=og, kc=kc, pb=pb, wb=wb: lambda e: e.matmul(
                         C.PS[pb][:, 0:CH], wb[:, kc * 128:(kc + 1) * 128],
                         C.att[:, kc * CH:(kc + 1) * CH],
                         start=(kc == 0), stop=(kc == NFC - 1)))(),
                     reads=[C.att_B, wb_B], writes=[C.PB[pb]], tag=tg + "wo")
            P.op("vector",
                 (lambda og=og, pb=pb, fr=fr, k=k, i=i: lambda e: e.tensor_add(
                     C.xmo[i % 2][:, k * CH:(k + 1) * CH], C.PS[pb][:, 0:CH],
                     fr[:, k * CH:(k + 1) * CH]))(),
                 reads=[C.PB[pb], fr_B], writes=[C.xmo_B[i % 2]], tag=tg + "xm")
        P.dma("sync",
              (lambda bc=bc, i=i: lambda e: e.dma_start(
                  out=xdst.ap().rearrange("(c p) w -> p c w", p=128)
                  [:, 2 * i:2 * i + 2, bc * CH:(bc + 1) * CH],
                  in_=C.xmo[i % 2][:, :].rearrange("p (c w) -> p c w", c=2)))(),
              reads=[C.xmo_B[i % 2]], writes=region(xdst_b, bc, bc + 1), tag=tg + "xs")


def build_ffn_layer(C, ell):
    P = C.P
    AQ0 = 15 - W_LEAD[ell]
    FQ0 = AQ0 + 1
    P.dma("sync",
          lambda e: e.dma_start(
              out=C.W1_sb[:, :].rearrange("p (k c) -> p k c", k=NFC),
              in_=C.w1_bf.ap()[ell * DM:(ell + 1) * DM, :].rearrange(
                  "(k p) c -> p k c", p=128)),
          reads=[C.wbf_b], writes=[C.W1_B], tag=f"fL{ell}_w")
    for bc in range(FQ0, NXC):
        ffn_chunk(C, ell, bc)


def ffn_chunk(C, ell, bc):
    P = C.P
    xsrc, xsrc_b = C.xB, C.xB_b
    xdst, xdst_b = C.xA, C.xA_b
    tg = f"fL{ell}c{bc}_"
    P.dma("gpsimd",
          (lambda bc=bc: lambda e: e.dma_start(
              out=C.xsb[:, :].rearrange("p (c w) -> p c w", c=NFC),
              in_=xsrc.ap().rearrange("(c p) w -> p c w", p=128)
              [:, :, bc * CH - 1:(bc + 1) * CH]))(),
          reads=region(xsrc_b, bc - 1, bc + 1), writes=[C.xsb_B], tag=tg + "x")
    for fc in range(NFC):
        P.op("scalar",
             (lambda fc=fc: lambda e: e.activation(
                 C.sqb[:, fc * CH:(fc + 1) * CH], _xs_slice(C, C.xsb, fc), AF.Square))(),
             reads=[C.xsb_B], writes=[C.sqb_B], tag=tg + "sq")
        P.op("tensor",
             (lambda fc=fc: lambda e: e.matmul(
                 C.PS[0][0:1, 0:CH], C.ONES_sb[:, 0:1], _xs_slice(C, C.xsb, fc),
                 start=(fc == 0), stop=(fc == NFC - 1)))(),
             reads=[C.xsb_B, C.cb], writes=[C.PB[0]], tag=tg + "st1")
        P.op("tensor",
             (lambda fc=fc: lambda e: e.matmul(
                 C.PS[1][0:1, 0:CH], C.ONES_sb[:, 0:1], C.sqb[:, fc * CH:(fc + 1) * CH],
                 start=(fc == 0), stop=(fc == NFC - 1)))(),
             reads=[C.sqb_B, C.cb], writes=[C.PB[1]], tag=tg + "st2")
    ln_rows_finish(C, CH, 0, 1, None, None, C.rstdb, C.rstdb_B, C.rowp, C.rowp_B, tg + "lr")
    P.op("tensor",
         lambda e: e.matmul(C.PS[2][:, 0:CH], C.ONErow_sb[0:1, :], C.rstdb[0:1, 0:CH],
                            start=True, stop=True),
         reads=[C.rstdb_B, C.cb], writes=[C.PB[2]], tag=tg + "A")
    for fc in range(NFC):
        pbc = 3 + fc % 2
        P.op("tensor",
             (lambda fc=fc, pbc=pbc: lambda e: e.matmul(
                 C.PS[pbc][:, 0:CH], C.Lgb_sb[:, fc * 128:(fc + 1) * 128],
                 C.rowp[0:2, 0:CH], start=True, stop=True))(),
             reads=[C.rowp_B, C.Lgb_B], writes=[C.PB[pbc]], tag=tg + "C")
        P.op("vector",
             (lambda fc=fc: lambda e: e.scalar_tensor_tensor(
                 C.h2i[0][:, :], _xs_slice(C, C.xsb, fc), C.gcol_sb[:, fc:fc + 1],
                 C.PS[2][:, 0:CH], ALU.mult, ALU.mult))(),
             reads=[C.xsb_B, C.gcol_B, C.PB[2]], writes=[C.h2i_B[0]], tag=tg + "n1")
        P.op("vector",
             (lambda fc=fc, pbc=pbc: lambda e: e.tensor_sub(
                 C.xn[:, fc * CH:(fc + 1) * CH], C.h2i[0][:, :], C.PS[pbc][:, 0:CH]))(),
             reads=[C.h2i_B[0], C.PB[pbc]], writes=[C.xn_B], tag=tg + "n2")
    for og in range(NOG1):
        m = 128 if og < NOG1 - 1 else 42
        pa, pg = 5, 6
        for kc in range(NFC):
            P.op("tensor",
                 (lambda og=og, kc=kc, m=m: lambda e: e.matmul(
                     C.PS[5][0:m, 0:CH],
                     C.W1_sb[:, kc * DFF + og * 128: kc * DFF + og * 128 + m],
                     C.xn[:, kc * CH:(kc + 1) * CH],
                     start=(kc == 0), stop=(kc == NFC - 1)))(),
                 reads=[C.xn_B, C.W1_B], writes=[C.PB[5]], tag=tg + "w1a")
            P.op("tensor",
                 (lambda og=og, kc=kc, m=m: lambda e: e.matmul(
                     C.PS[6][0:m, 0:CH],
                     C.W1_sb[:, kc * DFF + DFF2 + og * 128: kc * DFF + DFF2 + og * 128 + m],
                     C.xn[:, kc * CH:(kc + 1) * CH],
                     start=(kc == 0), stop=(kc == NFC - 1)))(),
                 reads=[C.xn_B, C.W1_B], writes=[C.PB[6]], tag=tg + "w1g")
        gpar = og % 2
        P.op("scalar",
             (lambda og=og, m=m, gpar=gpar: lambda e: e.activation(
                 C.gact[gpar][0:m, :], C.PS[6][0:m, 0:CH], AF.Gelu,
                 bias=C.b1g_sb[0:m, og:og + 1]))(),
             reads=[C.PB[6], C.b1g_B], writes=[C.gact_B[gpar]], tag=tg + "gl")
        P.op("vector",
             (lambda og=og, m=m, gpar=gpar: lambda e: e.scalar_tensor_tensor(
                 C.h2[0:m, og * CH:(og + 1) * CH], C.PS[5][0:m, 0:CH],
                 C.b1a_sb[0:m, og:og + 1],
                 C.gact[gpar][0:m, :], ALU.add, ALU.mult))(),
             reads=[C.PB[5], C.b1a_B, C.gact_B[gpar]], writes=[C.h2_B], tag=tg + "h2")
    for og in range(NOG1):
        m = 128 if og < NOG1 - 1 else 42
        wb, wb_B = C.W2_sb[og % 2], C.W2_B[og % 2]
        P.dma("sync",
              (lambda og=og, m=m, wb=wb: lambda e: e.dma_start(
                  out=wb[0:m, :],
                  in_=C.w2_bf.ap()[ell * DFF2 + og * 128: ell * DFF2 + og * 128 + m, :]))(),
              reads=[C.wbf_b], writes=[wb_B], tag=tg + "w2l")
        for og2 in range(NFC):
            P.op("tensor",
                 (lambda og=og, og2=og2, m=m, wb=wb: lambda e: e.matmul(
                      C.PS[og2][:, 0:CH],
                      wb[0:m, og2 * 128:(og2 + 1) * 128], C.h2[0:m, og * CH:(og + 1) * CH],
                      start=(og == 0), stop=(og == NOG1 - 1)))(),
                 reads=[C.h2_B, wb_B], writes=[C.PB[og2]], tag=tg + "w2")
    for i in range(4):
        fr, fr_B = C.fres[i % 2], C.fres_B[i % 2]
        P.dma("sync",
              (lambda i=i, bc=bc, fr=fr: lambda e: e.dma_start(
                  out=fr[:, :].rearrange("p (c w) -> p c w", c=2),
                  in_=xsrc.ap().rearrange("(c p) w -> p c w", p=128)
                  [:, 2 * i:2 * i + 2, bc * CH:(bc + 1) * CH]))(),
              reads=region(xsrc_b, bc, bc + 1), writes=[fr_B], tag=tg + "fr")
        for k in range(2):
            og2 = 2 * i + k
            P.op("vector",
                 (lambda og2=og2, fr=fr, k=k: lambda e:
                  e.scalar_tensor_tensor(
                      C.xnext[:, og2 * CH:(og2 + 1) * CH],
                      C.PS[og2][:, 0:CH],
                      C.b2c_sb[:, og2:og2 + 1], fr[:, k * CH:(k + 1) * CH],
                      ALU.add, ALU.add))(),
                 reads=[C.PB[og2], C.b2c_B, fr_B], writes=[C.xnext_B], tag=tg + "xo")
    P.dma("sync",
          (lambda bc=bc: lambda e: e.dma_start(
              out=xdst.ap().rearrange("(c p) w -> p c w", p=128)[:, :, bc * CH:(bc + 1) * CH],
              in_=C.xnext[:, :].rearrange("p (c w) -> p c w", c=NFC)))(),
          reads=[C.xnext_B], writes=region(xdst_b, bc, bc + 1), tag=tg + "xs")


def build_final(C):
    P = C.P
    TT = 512
    for t in range(SL // TT):
        col0 = OLEAD + t * TT
        xb, xb_B = C.oxt, C.oxt_B
        P.dma("gpsimd",
              (lambda col0=col0, xb=xb: lambda e: e.dma_start(
                  out=xb[:, :].rearrange("p (c w) -> p c w", c=NFC),
                  in_=C.xA.ap().rearrange("(c p) w -> p c w", p=128)[:, :, col0:col0 + TT]))(),
              reads=region(C.xA_b, col0 // CH, (col0 + TT) // CH),
              writes=[xb_B], tag="fi_x")
        plain_ln(C, xb, xb_B, C.oo1, C.oo1_B, TT, "fi_ln")
        for fc in range(NFC):
            fr, fr_B = C.ofr[fc % 2], C.ofr_B[fc % 2]
            P.dma("sync",
                  (lambda t=t, fc=fc, fr=fr: lambda e: e.dma_start(
                      out=fr[:, :],
                      in_=C.out1.ap()[fc * 128:(fc + 1) * 128, t * TT:(t + 1) * TT]))(),
                  reads=region(C.out1_b, t * TT // CH, (t + 1) * TT // CH),
                  writes=[fr_B], tag="fi_fr")
            P.op("vector",
                 (lambda fc=fc, fr=fr: lambda e: e.tensor_add(
                     C.oo1[:, fc * TT:(fc + 1) * TT], C.oo1[:, fc * TT:(fc + 1) * TT],
                     fr[:, :]))(),
                 reads=[C.oo1_B, fr_B], writes=[C.oo1_B], tag="fi_c")
        P.dma("sync",
              (lambda t=t: lambda e: e.dma_start(
                  out=C.y.ap().rearrange("(c p) w -> p c w", p=128)[:, :, t * TT:(t + 1) * TT],
                  in_=C.oo1[:, :].rearrange("p (c w) -> p c w", c=NFC)))(),
              reads=[C.oo1_B], writes=[C.y_b], tag="fi_ys")


def _host_shared():
    sh = {}
    i = np.arange(128)[:, None]
    j = np.arange(384)[None, :]
    band = (i <= j) & (j <= i + 256)
    m_norm = np.where(band, 0.0, -1e30).astype(np.float32)
    m_f0 = np.where(band & (j >= 256), 0.0, -1e30).astype(np.float32)
    m_f1 = np.where(band & (j >= 128), 0.0, -1e30).astype(np.float32)
    sh["m_norm"], sh["m_f0"], sh["m_f1"] = m_norm, m_f0, m_f1
    jj = np.arange(128)[:, None]
    ii = np.arange(128)[None, :]
    sh["UT"] = (jj <= ii).astype(np.float32).astype(BF)
    R = np.zeros((128, 128), np.float32)
    for h0 in (0, 64):
        for r in range(32):
            R[h0 + r + 32, h0 + r] = -1.0
            R[h0 + r, h0 + r + 32] = 1.0
    sh["R2M"] = R.astype(BF)
    RO = np.zeros((128, 128), np.float32)
    for h0 in (0, 64):
        for p in range(32):
            RO[h0 + 2 * p + 1, h0 + 2 * p] = -1.0
            RO[h0 + 2 * p, h0 + 2 * p + 1] = 1.0
    sh["ROM"] = RO.astype(BF)
    sh["IDEN"] = np.eye(128, dtype=BF)
    E2 = np.zeros((128, 2), np.float32)
    E2[0:64, 0] = 1.0
    E2[64:128, 1] = 1.0
    sh["E2"] = E2.astype(BF)
    consts = np.zeros(16, np.float32)
    consts[C_EPS24] = 1e-24
    consts[C_EPS5] = 1e-5
    consts[C_EPS6] = 1e-6
    consts[C_I1024] = 1.0 / 1024.0
    consts[C_MHALF] = -0.5
    consts[C_M1] = -1.0
    sh["CONSTS"] = consts
    return sh


def _rope_outer_tables(pos):
    j = np.arange(32)
    invf = (10000.0 ** (-2 * j / 64.0)).astype(np.float64)
    ang = pos[None, :].astype(np.float64) * invf[:, None]
    cos = np.cos(ang)
    sin = np.sin(ang)
    cos64 = np.repeat(cos, 2, axis=0)
    sin64 = np.repeat(sin, 2, axis=0)
    C128 = np.concatenate([cos64, cos64], axis=0).astype(np.float32)
    S128 = np.concatenate([sin64, sin64], axis=0).astype(np.float32)
    return C128, S128


def _rope_inner_tables(pos):
    p = np.maximum(pos, 0).astype(np.float64)
    i = np.arange(32)
    invf = 10000.0 ** (-2 * i / 64.0)
    base = (2 * i + 0.4 * 64) / (1.4 * 64)
    ang = p[None, :] * invf[:, None]
    cos = np.cos(ang)
    sin = np.sin(ang)
    tm = np.mod(p, 256.0)
    fq = base[:, None] ** ((tm[None, :] - 128.0) / 128.0)
    fk = base[:, None] ** ((128.0 - tm[None, :]) / 128.0)
    def dup(a):
        a64 = np.concatenate([a, a], axis=0)
        return np.concatenate([a64, a64], axis=0)
    Tqc = (dup(cos * fq) * QK_SCALE).astype(np.float32)
    Tqs = (dup(sin * fq) * QK_SCALE).astype(np.float32)
    Tkc = dup(cos * fk).astype(np.float32)
    Tks = dup(sin * fk).astype(np.float32)
    return Tqc, Tqs, Tkc, Tks


def _fm(x2d):
    return np.ascontiguousarray(x2d.T, dtype=np.float32)


def _core_inputs(core, inputs, sh):
    b, k = core // 4, core % 4
    s = k * SL
    x = np.asarray(inputs["L0_attn_x"])[b]
    lo = s - OLEAD - PREF
    hi = s - OLEAD
    xp = np.zeros((PREF, DM), np.float32)
    vlo = max(lo, 0)
    if hi > 0:
        xp[vlo - lo: hi - lo] = x[vlo:hi]
    xo = np.zeros((XCOLS, DM), np.float32)
    olo = s - OLEAD
    vlo = max(olo, 0)
    xo[vlo - olo:] = x[vlo: s + SL]
    d = {
        "xpref": _fm(xp),
        "xout": _fm(xo),
        "wq": np.asarray(inputs["L0_attn_wq"], np.float32),
        "wk": np.asarray(inputs["L0_attn_wk"], np.float32),
        "wv": np.asarray(inputs["L0_attn_wv"], np.float32),
        "wo": np.asarray(inputs["L0_attn_wo"], np.float32),
        "Wqkv": np.asarray(inputs["L0_ffn_Wqkv"], np.float32).reshape(DEPTH * DM, 3 * DM),
        "Wo6": np.asarray(inputs["L0_ffn_Wo"], np.float32).reshape(DEPTH * DM, DM),
        "W1": np.asarray(inputs["L0_ffn_W1"], np.float32).reshape(DEPTH * DM, DFF),
        "W2": np.asarray(inputs["L0_ffn_W2"], np.float32).reshape(DEPTH * DFF2, DM),
        "qs6": np.asarray(inputs["L0_ffn_q_scale"], np.float32),
        "ks6": np.asarray(inputs["L0_ffn_k_scale"], np.float32),
        "lng": np.asarray(inputs["L0_ffn_ln_g"], np.float32),
        "lnb": np.asarray(inputs["L0_ffn_ln_b"], np.float32),
        "b1": np.asarray(inputs["L0_ffn_b1"], np.float32),
        "b2": np.asarray(inputs["L0_ffn_b2"], np.float32),
    }
    ppos = np.arange(lo, hi)
    opos = np.arange(olo, s + SL)
    TPc, TPs = _rope_outer_tables(np.maximum(ppos, 0))
    TP = np.empty((128, 2 * PREF), np.float32)
    for t in range(PREF // 512):
        TP[:, t * 1024: t * 1024 + 512] = TPc[:, t * 512:(t + 1) * 512]
        TP[:, t * 1024 + 512: (t + 1) * 1024] = TPs[:, t * 512:(t + 1) * 512]
    TOc, TOs = _rope_outer_tables(np.maximum(opos, 0))
    TO = np.empty((128, 2 * XCOLS), np.float32)
    for t in range(XCOLS // 512):
        TO[:, t * 1024: t * 1024 + 512] = TOc[:, t * 512:(t + 1) * 512]
        TO[:, t * 1024 + 512:(t + 1) * 1024] = TOs[:, t * 512:(t + 1) * 512]
    d["TP"], d["TO"] = TP, TO
    Tqc, Tqs, Tkc, Tks = _rope_inner_tables(opos)
    TI = np.empty((128, 4 * XCOLS), BF)
    for c in range(NXC):
        base = c * 4 * CH
        TI[:, base:base + CH] = Tqc[:, c * CH:(c + 1) * CH].astype(BF)
        TI[:, base + CH:base + 2 * CH] = Tqs[:, c * CH:(c + 1) * CH].astype(BF)
        TI[:, base + 2 * CH:base + 3 * CH] = Tkc[:, c * CH:(c + 1) * CH].astype(BF)
        TI[:, base + 3 * CH:base + 4 * CH] = Tks[:, c * CH:(c + 1) * CH].astype(BF)
    d["TI"] = TI
    mf0 = sh["m_f0"] if k == 0 else sh["m_norm"]
    mf1 = sh["m_f1"] if k == 0 else sh["m_norm"]
    d["MSK"] = np.concatenate(
        [sh["m_norm"].astype(BF), mf0.astype(BF), mf1.astype(BF)], axis=1)
    d["UT"] = sh["UT"]; d["R2M"] = sh["R2M"]; d["ROM"] = sh["ROM"]
    d["IDEN"] = sh["IDEN"]; d["E2"] = sh["E2"]; d["CONSTS"] = sh["CONSTS"]
    d["ONED"] = np.ones((64, 1), np.float32)
    i6 = np.arange(128) % 64
    base = (2 * (i6 % 32) + 0.4 * 64) / (1.4 * 64)
    d["BASE2D"] = (base ** 2).astype(np.float32).reshape(128, 1)
    return d


_NC_CACHE = {}


def kernel(**inputs):
    key = "main"
    if key not in _NC_CACHE:
        _NC_CACHE[key] = build_nc()
    nc, C = _NC_CACHE[key]
    sh = _host_shared()
    in_maps = [_core_inputs(c, inputs, sh) for c in range(NCORES)]
    res = run_bass_kernel_spmd(nc, in_maps, core_ids=list(range(NCORES)))
    out = np.empty((B, L, DM), np.float32)
    for c in range(NCORES):
        b, k = c // 4, c % 4
        out[b, k * SL:(k + 1) * SL] = np.asarray(res.results[c]["y"]).T
    return out
